# version 02
# baseline (speedup 1.0000x reference)
import math
from contextlib import ExitStack

import numpy as np

import concourse.bass as bass
import concourse.mybir as mybir
from concourse.bass_utils import run_bass_kernel_spmd

F32 = mybir.dt.float32
BF16 = mybir.dt.bfloat16
AF = mybir.ActivationFunctionType
ALU = mybir.AluOpType
AX = mybir.AxisListType

ENGINES = ("sync", "tensor", "vector", "scalar", "gpsimd")

NCORES = 8
S = 2048
BS = 2
D = 1024
DC = D // 128



class Span:
    __slots__ = ("name", "p0", "p1", "b0", "b1")

    def __init__(self, name, p0, p1, b0, b1):
        self.name, self.p0, self.p1, self.b0, self.b1 = name, p0, p1, b0, b1

    def overlaps(self, o):
        return (self.name == o.name and self.p0 < o.p1 and o.p0 < self.p1
                and self.b0 < o.b1 and o.b0 < self.b1)

    def contains(self, o):
        return (self.name == o.name and self.p0 <= o.p0 and o.p1 <= self.p1
                and self.b0 <= o.b0 and o.b1 <= self.b1)

    def __repr__(self):
        return f"Span({self.name},p{self.p0}:{self.p1},b{self.b0}:{self.b1})"


class Tile:

    def __init__(self, handle, name, shape, dtype):
        self.h = handle
        self.name = name
        self.shape = tuple(shape)
        self.dtype = dtype
        self.esz = 2 if dtype == BF16 else 4

    def __getitem__(self, key):
        ps, fs = key
        if isinstance(ps, int):
            ps = slice(ps, ps + 1)
        if isinstance(fs, int):
            fs = slice(fs, fs + 1)
        p0 = ps.start or 0
        p1 = ps.stop if ps.stop is not None else self.shape[0]
        f0 = fs.start or 0
        f1 = fs.stop if fs.stop is not None else self.shape[1]
        ap = self.h[p0:p1, f0:f1]
        return ap, Span(self.name, p0, p1, f0 * self.esz, f1 * self.esz)

    def full(self):
        return self[0:self.shape[0], 0:self.shape[1]]

    def strided(self, p0, p1, f0, nseg, segstride, seglen):
        ap = bass.AP(self.h, p0 * self.shape[1] + f0,
                     [[self.shape[1], p1 - p0], [segstride, nseg], [1, seglen]])
        span = Span(self.name, p0, p1, f0 * self.esz,
                    (f0 + (nseg - 1) * segstride + seglen) * self.esz)
        return ap, span


class DTen:

    def __init__(self, handle, name):
        self.h = handle
        self.name = "D:" + name

    def __getitem__(self, key):
        return self.h[key], Span(self.name, 0, 1, 0, 1 << 62)


class Ctx:
    def __init__(self, nc):
        self.nc = nc
        self.rec = []
        self.writers = {}
        self.readers = {}
        self.n_compute = {e: 0 for e in ENGINES}
        self.n_dma = {e: 0 for e in ENGINES}
        self.DMA_POOL = 8
        self.dma_uses = {e: [0] * self.DMA_POOL for e in ENGINES}
        self.sems = {}
        self.dma_sems = {}
        self.es = None
        self.out_dma_events = []
        self.sbuf_bytes = 0

    def sbuf(self, name, shape, dtype):
        h = self.es.enter_context(self.nc.sbuf_tensor(name, list(shape), dtype))
        self.sbuf_bytes += shape[1] * (2 if dtype == BF16 else 4)
        return Tile(h, name, shape, dtype)

    def psum(self, name, shape, dtype):
        h = self.es.enter_context(self.nc.psum_tensor(name, list(shape), dtype))
        return Tile(h, name, shape, dtype)

    def dram(self, name, shape, dtype, kind="Internal"):
        h = self.nc.dram_tensor(name, list(shape), dtype, kind=kind)
        return DTen(h, name)

    def op(self, engine, emit, reads=(), writes=(), kind="compute",
           is_output=False):
        waits = self._waits(engine, reads, writes)
        if kind == "dma":
            slot = self.n_dma[engine] % self.DMA_POOL
            self.dma_uses[engine][slot] += 1
            uses = self.dma_uses[engine][slot]
            if uses > 1:
                key = ("dma", engine, slot)
                if waits.get(key, 0) < 16 * (uses - 1):
                    waits[key] = 16 * (uses - 1)
            self.n_dma[engine] += 1
            ev = (engine, "dma", slot, uses)
        else:
            self.n_compute[engine] += 1
            ev = (engine, "cmp", self.n_compute[engine])
        self.rec.append((engine, emit, waits, ev, kind))
        if is_output:
            self.out_dma_events.append(ev)
        for s in writes:
            lst = self.writers.setdefault(s.name, [])
            lst[:] = [(os, oev) for (os, oev) in lst if not s.contains(os)]
            lst.append((s, ev))
            rl = self.readers.get(s.name)
            if rl:
                rl[:] = [(os, oev) for (os, oev) in rl if not s.contains(os)]
        for s in reads:
            self.readers.setdefault(s.name, []).append((s, ev))
        return ev

    def _waits(self, engine, reads, writes):
        need = {}

        def add(ev):
            if ev[1] == "dma":
                e, _k, slot, uses = ev
                key = ("dma", e, slot)
                val = uses * 16
            else:
                e, k, n = ev
                if e == engine and engine == "tensor":
                    return
                key = ("cmp", e)
                val = n
            if need.get(key, 0) < val:
                need[key] = val

        for s in reads:
            for (ws, ev) in self.writers.get(s.name, ()):
                if ws.overlaps(s):
                    add(ev)
        for s in writes:
            for (ws, ev) in self.writers.get(s.name, ()):
                if ws.overlaps(s):
                    add(ev)
            for (rs, ev) in self.readers.get(s.name, ()):
                if rs.overlaps(s):
                    add(ev)
        return need

    def emit(self):
        for e in ENGINES:
            self.sems[e] = self.es.enter_context(self.nc.semaphore(f"s_{e}"))
        for e in ENGINES:
            if self.n_dma[e] > 0:
                self.dma_sems[e] = [
                    self.es.enter_context(self.nc.semaphore(f"d_{e}{i}"))
                    for i in range(self.DMA_POOL)]
        per = {e: [] for e in ENGINES}
        for r in self.rec:
            per[r[0]].append(r)
        block = self.es.enter_context(self.nc.Block())
        sems, dsems = self.sems, self.dma_sems
        final_waits = {}
        for e in ENGINES:
            for slot in range(self.DMA_POOL):
                if self.dma_uses[e][slot] > 0:
                    final_waits[("dma", e, slot)] = 16 * self.dma_uses[e][slot]

        def make_body(ename):
            ops = per[ename]

            def body(eng):
                wm = {}
                for (_e, emit, waits, ev, kind) in ops:
                    for key in sorted(waits, key=repr):
                        val = waits[key]
                        if wm.get(key, 0) >= val:
                            continue
                        if key[0] == "cmp":
                            eng.wait_ge(sems[key[1]], val)
                        else:
                            eng.wait_ge(dsems[key[1]][key[2]], val)
                        wm[key] = val
                    inst = emit(eng)
                    if ev[1] == "dma":
                        inst.then_inc(dsems[ev[0]][ev[2]], 16)
                    else:
                        inst.then_inc(sems[ev[0]], 1)
                if ename == "gpsimd":
                    for key, val in sorted(final_waits.items(), key=repr):
                        eng.wait_ge(dsems[key[1]][key[2]], val)
            return body

        for e in ENGINES:
            if per[e] or e == "gpsimd":
                getattr(block, e)(make_body(e))

    def dma(self, q, out, in_, is_output=False, accum=False):
        oap, osp = out
        iap, isp = in_

        def em(eng, oap=oap, iap=iap, accum=accum):
            if accum:
                return eng.dma_start(out=oap, in_=iap, accum_op=ALU.add)
            return eng.dma_start(out=oap, in_=iap)
        return self.op(q, em, reads=[isp], writes=[osp], kind="dma",
                       is_output=is_output)

    def mmgroup(self, psum_out, pairs, accum_into=False, stop_group=True):
        oap, osp = psum_out
        reads = [sp for (l, r) in pairs for sp in (l[1], r[1])]
        aps = [(l[0], r[0]) for (l, r) in pairs]

        def em(eng, aps=aps, oap=oap, accum_into=accum_into,
               stop_group=stop_group):
            n = len(aps)
            inst = None
            for i, (la, ra) in enumerate(aps):
                inst = eng.matmul(oap, la, ra,
                                  start=(i == 0 and not accum_into),
                                  stop=(i == n - 1 and stop_group))
            return inst
        return self.op("tensor", em, reads=reads, writes=[osp])

    def transpose(self, psum_out, in_, ident):
        oap, osp = psum_out
        iap, isp = in_
        idap, idsp = ident

        def em(eng, oap=oap, iap=iap, idap=idap):
            return eng.transpose(oap, iap, idap)
        return self.op("tensor", em, reads=[isp, idsp], writes=[osp])

    def act(self, eng, out, in_, func, bias=0.0, scale=1.0, accum_out=None):
        oap, osp = out
        iap, isp = in_
        reads = [isp]
        writes = [osp]
        if isinstance(bias, tuple):
            reads.append(bias[1])
            bias = bias[0]
        if isinstance(scale, tuple):
            reads.append(scale[1])
            scale = scale[0]
        acc = None
        if accum_out is not None:
            acc = accum_out[0]
            writes.append(accum_out[1])

        def em(e, oap=oap, iap=iap, func=func, bias=bias, scale=scale, acc=acc):
            if acc is not None:
                return e.activation(oap, iap, func, bias=bias, scale=scale,
                                    accum_out=acc)
            return e.activation(oap, iap, func, bias=bias, scale=scale)
        return self.op(eng, em, reads=reads, writes=writes)

    def tt(self, eng, op, out, in0, in1):
        oap, osp = out
        a0, s0 = in0
        a1, s1 = in1

        def em(e, oap=oap, a0=a0, a1=a1, op=op):
            return e.tensor_tensor(oap, a0, a1, op)
        return self.op(eng, em, reads=[s0, s1], writes=[osp])

    def ts(self, eng, out, in0, s1, s2, op0, op1=ALU.bypass, accum_out=None):
        oap, osp = out
        a0, sp0 = in0
        reads = [sp0]
        if isinstance(s1, tuple):
            reads.append(s1[1])
            s1 = s1[0]
        if isinstance(s2, tuple):
            reads.append(s2[1])
            s2 = s2[0]
        writes = [osp]
        acc = None
        if accum_out is not None:
            acc = accum_out[0]
            writes.append(accum_out[1])

        def em(e, oap=oap, a0=a0, s1=s1, s2=s2, op0=op0, op1=op1, acc=acc):
            return e.tensor_scalar(oap, a0, s1, s2, op0, op1, accum_out=acc)
        return self.op(eng, em, reads=reads, writes=writes)

    def stt(self, eng, out, in0, scalar, in1, op0, op1):
        oap, osp = out
        a0, sp0 = in0
        a1, sp1 = in1
        reads = [sp0, sp1]
        if isinstance(scalar, tuple):
            reads.append(scalar[1])
            scalar = scalar[0]

        def em(e, oap=oap, a0=a0, a1=a1, scalar=scalar, op0=op0, op1=op1):
            return e.scalar_tensor_tensor(oap, a0, scalar, a1, op0, op1)
        return self.op(eng, em, reads=reads, writes=[osp])

    def copy(self, eng, out, in_):
        oap, osp = out
        iap, isp = in_

        def em(e, oap=oap, iap=iap, eng=eng):
            return e.copy(oap, iap) if eng == "scalar" else e.tensor_copy(oap, iap)
        return self.op(eng, em, reads=[isp], writes=[osp])

    def reduce(self, eng, op, out, in_, negate=False):
        oap, osp = out
        iap, isp = in_

        def em(e, oap=oap, iap=iap, op=op, negate=negate):
            return e.tensor_reduce(oap, iap, AX.X, op, negate=negate)
        return self.op(eng, em, reads=[isp], writes=[osp])

    def scan(self, out, d0, d1, initial, op0=ALU.mult, op1=ALU.add):
        oap, osp = out
        a0, s0 = d0
        a1, s1 = d1
        reads = [s0, s1]
        if isinstance(initial, tuple):
            reads.append(initial[1])
            initial = initial[0]

        def em(e, oap=oap, a0=a0, a1=a1, initial=initial, op0=op0, op1=op1):
            return e.tensor_tensor_scan(oap, a0, a1, initial, op0, op1)
        return self.op("vector", em, reads=reads, writes=[osp])

    def recip(self, out, in_):
        oap, osp = out
        iap, isp = in_

        def em(e, oap=oap, iap=iap):
            return e.reciprocal(oap, iap)
        return self.op("vector", em, reads=[isp], writes=[osp])

    def memset(self, eng, out, val):
        oap, osp = out

        def em(e, oap=oap, val=val):
            return e.memset(oap, val)
        return self.op(eng, em, reads=[], writes=[osp])



def to_layoutB(x):
    b, t, d = x.shape
    return np.ascontiguousarray(x.transpose(0, 2, 1).reshape(b, d // 128, 128, t))


def from_layoutB(xb):
    b, dc, p, t = xb.shape
    return np.ascontiguousarray(xb.reshape(b, dc * p, t).transpose(0, 2, 1))


def bias_chunks(v):
    n = v.shape[0] // 128
    return np.ascontiguousarray(v.reshape(n, 128).T)


def host_constants():
    c = {}
    c["c_identf"] = np.eye(128, dtype=np.float32)
    masks = np.zeros((128, 4 * 512), np.float32)
    for o in range(4):
        k_idx = np.arange(128)[:, None] + o * 128
        q_idx = np.arange(512)[None, :]
        masks[:, o * 512:(o + 1) * 512] = np.where(k_idx > q_idx, -1.0e6, 0.0)
    c["c_causal"] = masks
    return c



def d4T(dten, s, t0, t1):
    ap = dten.h[s, :, :, t0:t1].transpose([1, 0, 2])
    return ap, Span(dten.name, 0, 1, 0, 1 << 62)


def emit_layernorm_to_dram(ctx, T, x_in, xn_d):
    XF, SQ, STT = T["SCRF"], T["SCRB"], T["STT"]
    onesb = T["ONESBC"]
    onesrf = T["ONESRF"]
    P = T["PS"]
    for s in range(BS):
        for nb in range(S // 512):
            t0 = nb * 512
            for c in range(DC):
                ctx.dma("sync", XF[0:128, c * 512:(c + 1) * 512],
                        x_in[s, c, :, t0:t0 + 512])
            for c in range(DC):
                ctx.act("scalar", SQ[0:128, c * 512:(c + 1) * 512],
                        XF[0:128, c * 512:(c + 1) * 512], AF.Copy)
            ctx.mmgroup(P[0][0:1, 0:512],
                        [(onesb, SQ[0:128, c * 512:(c + 1) * 512])
                         for c in range(DC)])
            for c in range(DC):
                ctx.tt("vector", ALU.mult, SQ[0:128, c * 512:(c + 1) * 512],
                       SQ[0:128, c * 512:(c + 1) * 512],
                       SQ[0:128, c * 512:(c + 1) * 512])
            ctx.mmgroup(P[1][0:1, 0:512],
                        [(onesb, SQ[0:128, c * 512:(c + 1) * 512])
                         for c in range(DC)])
            f = lambda i: STT[0:1, i * 512:(i + 1) * 512]
            ctx.act("scalar", f(0), P[0][0:1, 0:512], AF.Copy, scale=1.0 / D)
            ctx.act("scalar", f(1), P[1][0:1, 0:512], AF.Copy, scale=1.0 / D)
            ctx.act("scalar", f(2), f(0), AF.Square)
            ctx.tt("vector", ALU.subtract, f(1), f(1), f(2))
            ctx.ts("vector", f(1), f(1), 1e-5, None, ALU.add)
            ctx.recip(f(2), f(1))
            ctx.act("scalar", f(1), f(2), AF.Sqrt)
            ctx.tt("vector", ALU.mult, f(3), f(0), f(1))
            ctx.mmgroup(P[2][0:128, 0:512], [(onesrf, f(1))])
            ctx.mmgroup(P[3][0:128, 0:512], [(onesrf, f(3))])
            for c in range(DC):
                ctx.tt("vector", ALU.mult, XF[0:128, c * 512:(c + 1) * 512],
                       XF[0:128, c * 512:(c + 1) * 512], P[2][0:128, 0:512])
            for c in range(DC):
                ctx.tt("vector", ALU.subtract, SQ[0:128, c * 512:(c + 1) * 512],
                       XF[0:128, c * 512:(c + 1) * 512], P[3][0:128, 0:512])
            for c in range(DC):
                ctx.dma("sync", xn_d[s, c, :, t0:t0 + 512],
                        SQ[0:128, c * 512:(c + 1) * 512])


def load_xn_tile(ctx, T, xn_d, s, t0, tlen, q="sync"):
    XNT = T["XNT"]
    ctx.dma(q, XNT.strided(0, 128, 0, DC, tlen, tlen), d4T(xn_d, s, t0, t0 + tlen))
    return XNT


def alloc_common(ctx):
    T = {}
    T["A1"] = ctx.sbuf("A1", [128, 32768], BF16)
    T["A2"] = ctx.sbuf("A2", [128, 32768], BF16)
    T["HA"] = ctx.sbuf("HA", [128, 16384], BF16)
    T["XNT"] = ctx.sbuf("XNT", [128, 4096], BF16)
    T["SCRF"] = ctx.sbuf("SCRF", [128, 4096], F32)
    T["SCRB"] = ctx.sbuf("SCRB", [128, 4096], BF16)
    T["MASK"] = ctx.sbuf("MASKT", [128, 2048], BF16)
    T["IDF"] = ctx.sbuf("IDF", [128, 128], F32)
    T["IDB"] = ctx.sbuf("IDB", [128, 128], BF16)
    T["SELB"] = ctx.sbuf("SELB", [128, 2], BF16)
    T["STT"] = ctx.sbuf("STT", [1, 2048], F32)
    T["GAT"] = ctx.sbuf("GAT", [128, 256], F32)
    T["B1S"] = ctx.sbuf("B1S", [128, 32], F32)
    T["RBS"] = ctx.sbuf("RBS", [128, 8], F32)
    T["BOS"] = ctx.sbuf("BOS", [128, 8], F32)
    onesb_t = ctx.sbuf("ONESBC", [128, 1], BF16)
    onesrf_t = ctx.sbuf("ONESRF", [1, 128], F32)
    T["PS"] = [ctx.psum(f"PSB{i}", [128, 512], F32) for i in range(8)]
    ctx.memset("vector", onesb_t.full(), 1.0)
    ctx.memset("vector", onesrf_t.full(), 1.0)
    T["ONESBC"] = onesb_t.full()
    T["ONESRF"] = onesrf_t.full()
    T["_onesrf_t"] = onesrf_t
    T["_onesb_t"] = onesb_t
    return T



def prep_l0(inputs):
    d = {}
    d["l0_wq"] = inputs["L0_attn_Wq"]
    d["l0_wk"] = inputs["L0_attn_Wk"]
    d["l0_wv"] = inputs["L0_attn_Wv"]
    wo = inputs["L0_attn_Wo"]
    d["l0_wo_rep"] = np.ascontiguousarray(
        wo.reshape(16, 64, 1024).transpose(1, 0, 2)[
            np.tile(np.arange(64), 2)])
    d["l0_bo"] = bias_chunks(inputs["L0_attn_bo"])
    d["l0_rw"] = inputs["L0_ffn_router_W"]
    d["l0_rb"] = np.ascontiguousarray(inputs["L0_ffn_router_b"].reshape(1, 8))
    d["l0_w1"] = inputs["L0_ffn_W1"]
    d["l0_b1"] = np.ascontiguousarray(
        np.stack([bias_chunks(b) for b in inputs["L0_ffn_b1"]]))
    d["l0_w2"] = inputs["L0_ffn_W2"]
    d["l0_b2"] = np.ascontiguousarray(inputs["L0_ffn_b2"]).reshape(8, 1, 1024)
    return d


def emit_l0(ctx, T, H, x_in, x_out, dbg=None):
    xn_d = ctx.dram("l0xn_d", [BS, DC, 128, S], BF16)
    a_d = ctx.dram("l0a_d", [BS, DC, 128, S], F32)
    y_d = ctx.dram("l0y_d", [BS, S // 128, 128, D], F32)
    ctx_d = ctx.dram("l0ctx_d", [BS, DC, 128, S], BF16)

    A1, A2, HA, XNT = T["A1"], T["A2"], T["HA"], T["XNT"]
    SCRF, SCRB = T["SCRF"], T["SCRB"]
    P = T["PS"]
    onesrf_t = T["_onesrf_t"]
    WOFF = 16640

    ctx.dma("sync", T["IDF"].full(), H["c_identf"][:, :])
    ctx.dma("gpsimd", T["MASK"].full(), H["c_causal"][:, :])
    ctx.dma("sync", T["BOS"].full(), H["l0_bo"][:, :])

    emit_layernorm_to_dram(ctx, T, x_in, xn_d)
    if dbg is not None and "xn" in dbg:
        for s in range(BS):
            for c in range(DC):
                ctx.dma("gpsimd", dbg["xn"][s, c, :, :], xn_d[s, c, :, :])

    NH = 16
    for s in range(BS):
        for (wkey, kind) in (("l0_wk", "K"), ("l0_wq", "Q"), ("l0_wv", "V")):
            for c in range(DC):
                ctx.dma("gpsimd", A2[0:128, WOFF + c * 1024:WOFF + (c + 1) * 1024],
                        H[wkey][c * 128:(c + 1) * 128, :])
            for nb in range(S // 512):
                t0 = nb * 512
                load_xn_tile(ctx, T, xn_d, s, t0, 512)
                if kind in ("K", "Q"):
                    base = 0 if kind == "K" else 16384
                    for m in range(DC):
                        pb = P[m % 2]
                        ctx.mmgroup(pb[0:128, 0:512],
                                    [(A2[0:128, WOFF + c * 1024 + m * 128:
                                         WOFF + c * 1024 + (m + 1) * 128],
                                      XNT[0:128, c * 512:(c + 1) * 512])
                                     for c in range(DC)])
                        ctx.act("scalar", A1[0:128, base + m * S + t0:
                                             base + m * S + t0 + 512],
                                pb[0:128, 0:512], AF.Copy)
                else:
                    for st in range(4):
                        tt = nb * 4 + st
                        for half in range(2):
                            pb = P[2 + (st * 2 + half) % 2]
                            ctx.mmgroup(pb[0:128, 0:512],
                                        [(XNT[0:128, c * 512 + st * 128:
                                              c * 512 + (st + 1) * 128],
                                          A2[0:128, WOFF + c * 1024 + half * 512:
                                             WOFF + c * 1024 + (half + 1) * 512])
                                         for c in range(DC)])
                            ctx.copy("vector",
                                     A2.strided(0, 128, tt * 1040 + half * 8 * 65,
                                                8, 65, 64),
                                     pb[0:128, 0:512])
        ctx.memset("vector", A2.strided(0, 128, 64, 16 * 16, 65, 1), 1.0)

        for qb in range(4):
            for h in range(NH):
                po = 64 * (h % 2)
                ch = h // 2
                nkt = (qb + 1) * 4
                qsl = A1[po:po + 64, 16384 + ch * S + qb * 512:
                         16384 + ch * S + (qb + 1) * 512]
                for kt in range(nkt):
                    pb = P[4 + kt % 2]
                    ctx.mmgroup(pb[0:128, 0:512],
                                [(A1[po:po + 64, ch * S + kt * 128:
                                     ch * S + (kt + 1) * 128], qsl)])
                    if kt >= qb * 4:
                        o = kt - qb * 4
                        ctx.tt("vector", ALU.add, pb[0:128, 0:512],
                               pb[0:128, 0:512],
                               T["MASK"][0:128, o * 512:(o + 1) * 512])
                    ctx.act("scalar", HA[0:128, kt * 512:(kt + 1) * 512],
                            pb[0:128, 0:512], AF.Exp, scale=0.125)
                avp = P[6]
                ctx.mmgroup(avp[0:65, 0:512],
                            [(A2.strided(0, 128, kt * 1040 + h * 65, 1, 65, 65),
                              HA[0:128, kt * 512:(kt + 1) * 512])
                             for kt in range(nkt)])
                rslot = T["STT"][0:1, 1024 + (h % 2) * 512:
                                 1024 + (h % 2) * 512 + 512]
                ctx.recip(rslot, avp[64:65, 0:512])
                ctx.copy("scalar", SCRB[po:po + 64, 0:512], avp[0:64, 0:512])
                if h % 2 == 1:
                    rb = P[7]
                    ctx.mmgroup(rb[0:64, 0:512],
                                [(onesrf_t[0:1, 0:64],
                                  T["STT"][0:1, 1024:1536])])
                    ctx.mmgroup(rb[64:128, 0:512],
                                [(onesrf_t[0:1, 0:64],
                                  T["STT"][0:1, 1536:2048])])
                    ctx.tt("vector", ALU.mult, SCRB[0:128, 512:1024],
                           SCRB[0:128, 0:512], rb[0:128, 0:512])
                    ctx.dma("sync", ctx_d[s, ch, :, qb * 512:(qb + 1) * 512],
                            SCRB[0:128, 512:1024])

        ctx.dma("gpsimd", A2[0:128, 0:16384], H["l0_wo_rep"][:, :, :])
        for ch in range(DC):
            ctx.dma("sync", XNT[0:128, 0:S], ctx_d[s, ch, :, :])
            for hh in range(2):
                h = 2 * ch + hh
                po = 64 * hh
                for m2 in range(DC):
                    pb = P[m2 % 2]
                    pairs = []
                    for j in range(16):
                        lhsT = A2[po:po + 64,
                                  j * 1024 + m2 * 128:j * 1024 + (m2 + 1) * 128]
                        rhs = XNT.strided(po, po + 64, j, 128, 16, 1)
                        pairs.append((lhsT, rhs))
                    ctx.mmgroup(pb[0:128, 0:128], pairs)
                    ctx.copy("scalar",
                             SCRF[0:128, (m2 % 8) * 128:(m2 % 8 + 1) * 128],
                             pb[0:128, 0:128])
                    ctx.dma("gpsimd", a_d[s, m2, :, 128 * h:128 * h + 128],
                            SCRF[0:128, (m2 % 8) * 128:(m2 % 8 + 1) * 128])
    if dbg is not None and "ctx" in dbg:
        for s in range(BS):
            for c in range(DC):
                ctx.dma("gpsimd", dbg["ctx"][s, c, :, :], ctx_d[s, c, :, :])
    if dbg is not None and "a" in dbg:
        for s in range(BS):
            for c in range(DC):
                ctx.dma("gpsimd", dbg["a"][s, c, :, :], a_d[s, c, :, :])

    for c in range(DC):
        ctx.dma("gpsimd", A2[0:128, c * 8:(c + 1) * 8],
                H["l0_rw"][c * 128:(c + 1) * 128, :])
    ctx.dma("sync", T["STT"][0:1, 0:8], H["l0_rb"][:, :])
    ctx.mmgroup(P[5][0:128, 0:8],
                [(onesrf_t[0:1, 0:128], T["STT"][0:1, 0:8])])
    ctx.copy("vector", T["RBS"].full(), P[5][0:128, 0:8])
    for s in range(BS):
        for tt in range(16):
            t0 = tt * 128
            ctx.dma("sync", XNT.strided(0, 128, 0, DC, 128, 128),
                    d4T(xn_d, s, t0, t0 + 128))
            pb = P[tt % 2]
            ctx.mmgroup(pb[0:128, 0:8],
                        [(XNT[0:128, c * 128:(c + 1) * 128],
                          A2[0:128, c * 8:(c + 1) * 8]) for c in range(DC)])
            lg = SCRF[0:128, 0:8]
            pr = SCRF[0:128, 8:16]
            mx = SCRF[0:128, 16:17]
            sm = SCRF[0:128, 17:18]
            ctx.tt("vector", ALU.add, lg, pb[0:128, 0:8], T["RBS"].full())
            ctx.reduce("vector", ALU.max, mx, lg, negate=True)
            ctx.act("scalar", pr, lg, AF.Exp, bias=mx, accum_out=sm)
            ctx.recip(sm, sm)
            ctx.ts("vector", pr, pr, sm, None, ALU.mult)
            ctx.reduce("vector", ALU.max, mx, pr)
            ctx.ts("vector", lg, pr, mx, None, ALU.is_ge)
            ctx.tt("vector", ALU.mult,
                   T["GAT"][0:128, s * 128 + tt * 8:s * 128 + (tt + 1) * 8],
                   pr, lg)

    if dbg is not None and "gat" in dbg:
        ctx.dma("gpsimd", dbg["gat"][:, :], T["GAT"].full())
    for e in range(8):
        for c in range(DC):
            ctx.dma("gpsimd", A1[0:128, c * 4096:(c + 1) * 4096],
                    H["l0_w1"][e, c * 128:(c + 1) * 128, :])
        for c2 in range(32):
            ctx.dma("gpsimd", A2[0:128, c2 * 1024:(c2 + 1) * 1024],
                    H["l0_w2"][e, c2 * 128:(c2 + 1) * 128, :])
        ctx.dma("sync", T["B1S"].full(), H["l0_b1"][e, :, :])
        ctx.dma("sync", T["STT"][0:1, 0:1024], H["l0_b2"][e, :, :])
        for half in range(2):
            ctx.mmgroup(P[6][0:128, 0:512],
                        [(onesrf_t[0:1, 0:128],
                          T["STT"][0:1, half * 512:(half + 1) * 512])])
            ctx.copy("vector",
                     SCRF[0:128, 3072 + half * 512:3072 + (half + 1) * 512],
                     P[6][0:128, 0:512])
        for s in range(BS):
            for nb in range(S // 512):
                t0 = nb * 512
                load_xn_tile(ctx, T, xn_d, s, t0, 512)
                for m in range(32):
                    pb = P[m % 2]
                    ctx.mmgroup(pb[0:128, 0:512],
                                [(A1[0:128, c * 4096 + m * 128:
                                     c * 4096 + (m + 1) * 128],
                                  XNT[0:128, c * 512:(c + 1) * 512])
                                 for c in range(DC)])
                    ctx.act("scalar", HA[0:128, m * 512:(m + 1) * 512],
                            pb[0:128, 0:512], AF.Relu,
                            bias=T["B1S"][0:128, m:m + 1])
                for st in range(4):
                    tt = nb * 4 + st
                    gsl = T["GAT"][0:128, s * 128 + tt * 8 + e:
                                   s * 128 + tt * 8 + e + 1]
                    for half in range(2):
                        pb = P[2 + (st * 2 + half) % 2]
                        ctx.mmgroup(pb[0:128, 0:512],
                                    [(HA[0:128, k2 * 512 + st * 128:
                                         k2 * 512 + (st + 1) * 128],
                                      A2[0:128, k2 * 1024 + half * 512:
                                         k2 * 1024 + (half + 1) * 512])
                                     for k2 in range(32)])
                        tb = SCRF[0:128, (st % 2) * 1024 + half * 512:
                                  (st % 2) * 1024 + half * 512 + 512]
                        ctx.tt("vector", ALU.add, tb, pb[0:128, 0:512],
                               SCRF[0:128, 3072 + half * 512:
                                    3072 + (half + 1) * 512])
                        ctx.act("scalar", tb, tb, AF.Copy, scale=gsl)
                        ctx.dma("gpsimd",
                                y_d[s, tt, :, half * 512:(half + 1) * 512],
                                tb, accum=(e > 0))
    if dbg is not None and "y" in dbg:
        for s in range(BS):
            for tt in range(16):
                ctx.dma("gpsimd", dbg["y"][s, tt, :, :], y_d[s, tt, :, :])

    for s in range(BS):
        for tt in range(16):
            t0 = tt * 128
            ctx.dma("sync", SCRF[0:128, 0:1024], y_d[s, tt, :, :])
            ctx.dma("sync", SCRF.strided(0, 128, 1024, DC, 128, 128),
                    d4T(x_in, s, t0, t0 + 128))
            ctx.dma("scalar", SCRF.strided(0, 128, 2048, DC, 128, 128),
                    d4T(a_d, s, t0, t0 + 128))
            for c in range(DC):
                pb = P[4 + c % 2]
                ctx.transpose(pb[0:128, 0:128],
                              SCRF[0:128, c * 128:(c + 1) * 128],
                              T["IDF"].full())
                ob = SCRF[0:128, 3072 + (c % 4) * 128:3072 + (c % 4 + 1) * 128]
                ctx.stt("vector", ob, pb[0:128, 0:128],
                        T["BOS"][0:128, c:c + 1],
                        SCRF[0:128, 1024 + c * 128:1024 + (c + 1) * 128],
                        ALU.add, ALU.add)
                ctx.tt("vector", ALU.add, ob, ob,
                       SCRF[0:128, 2048 + c * 128:2048 + (c + 1) * 128])
                ctx.dma("gpsimd", x_out[s, c, :, t0:t0 + 128], ob,
                        is_output=True)


def declare_inputs(ctx, arrays):
    H = {}
    for name, arr in arrays.items():
        H[name] = ctx.dram(name, list(arr.shape), F32, kind="ExternalInput")
    return H


def build_layer_program(layer, weights, consts, dbg_keys=()):
    nc = bass.Bass("TRN2", target_bir_lowering=False)
    ctx = Ctx(nc)
    es = ExitStack()
    ctx.es = es
    with es:
        H = declare_inputs(ctx, {**weights, **consts})
        x_in = ctx.dram("x_in", [BS, DC, 128, S], F32, kind="ExternalInput")
        x_out = ctx.dram("x_out", [BS, DC, 128, S], F32, kind="ExternalOutput")
        dbg = {}
        for k in dbg_keys:
            if k == "xn":
                dbg["xn"] = ctx.dram("dbg_xn", [BS, DC, 128, S], BF16,
                                     kind="ExternalOutput")
            elif k == "a":
                dbg["a"] = ctx.dram("dbg_a", [BS, DC, 128, S], F32,
                                    kind="ExternalOutput")
            elif k == "y":
                if layer == 1:
                    dbg["y"] = ctx.dram("dbg_y", [BS, DC, 128, S], BF16,
                                        kind="ExternalOutput")
                else:
                    dbg["y"] = ctx.dram("dbg_y", [BS, S // 128, 128, D], F32,
                                        kind="ExternalOutput")
            elif k == "ctx":
                dbg["ctx"] = ctx.dram("dbg_ctx", [BS, DC, 128, S], BF16,
                                      kind="ExternalOutput")
            elif k == "gat":
                dbg["gat"] = ctx.dram("dbg_gat", [128, 256], F32,
                                      kind="ExternalOutput")
            elif k == "l3i":
                dbg["l3i"] = ctx.dram("dbg_l3i", [3, 128, 2048], BF16,
                                      kind="ExternalOutput")
            elif k == "gat":
                dbg["gat"] = ctx.dram("dbg_gat", [128, 256], F32,
                                      kind="ExternalOutput")
            elif k == "l3i":
                dbg["l3i"] = ctx.dram("dbg_l3i", [3, 128, 2048], BF16,
                                      kind="ExternalOutput")
        T = alloc_common(ctx)
        if layer == 0:
            emit_l0(ctx, T, H, x_in, x_out, dbg=dbg if dbg else None)
        elif layer == 1:
            emit_l1(ctx, T, H, x_in, x_out, dbg=dbg if dbg else None)
        elif layer == 2:
            emit_l2(ctx, T, H, x_in, x_out, dbg=dbg if dbg else None)
        elif layer == 3:
            emit_l3(ctx, T, H, x_in, x_out, dbg=dbg if dbg else None)
        else:
            raise NotImplementedError(layer)
        ctx.emit()
    return nc






def f32v(tile, f0, f1):
    ap = tile.h[0:tile.shape[0], 2 * f0:2 * f1].bitcast(F32)
    return ap, Span(tile.name, 0, tile.shape[0], f0 * 4, f1 * 4)


def f32vp(tile, p0, p1, f0, f1):
    ap = tile.h[p0:p1, 2 * f0:2 * f1].bitcast(F32)
    return ap, Span(tile.name, p0, p1, f0 * 4, f1 * 4)


def l2_dft_const():
    M = np.zeros((512, 256), np.float64)
    M[np.arange(1, 257), np.arange(256)] = 1.0
    t = np.arange(257, 512)
    M[t, 512 - t] = -1.0
    A = np.imag(np.fft.fft(M, axis=0))
    return np.ascontiguousarray(A.T).astype(np.float32)


def prep_l2(inputs):
    d = {}
    adft = l2_dft_const()
    d["l2_adft"] = np.ascontiguousarray(
        adft.reshape(2, 128, 512).transpose(1, 0, 2).reshape(128, 1024))
    d["l2_ap"] = np.ascontiguousarray(
        inputs["L2_attn_a_params"].reshape(2, 128).T)
    d["l2_bwT"] = np.ascontiguousarray(inputs["L2_attn_B_w"].T)
    d["l2_cwT"] = np.ascontiguousarray(inputs["L2_attn_C_w"].T)
    d["l2_D"] = bias_chunks(inputs["L2_attn_D"])
    d["l2_gwT"] = np.ascontiguousarray(inputs["L2_ffn_gate_w"].T)
    d["l2_gb"] = np.ascontiguousarray(inputs["L2_ffn_gate_b"].reshape(1, 8))
    w1 = np.concatenate([inputs["L2_ffn_w1"], inputs["L2_ffn_sw1"][None]], 0)
    w3 = np.concatenate([inputs["L2_ffn_w3"], inputs["L2_ffn_sw3"][None]], 0)
    w2 = np.concatenate([inputs["L2_ffn_w2"], inputs["L2_ffn_sw2"][None]], 0)
    b1 = np.concatenate([inputs["L2_ffn_b1"], inputs["L2_ffn_sb1"][None]], 0)
    b3 = np.concatenate([inputs["L2_ffn_b3"], inputs["L2_ffn_sb3"][None]], 0)
    b2 = np.concatenate([inputs["L2_ffn_b2"], inputs["L2_ffn_sb2"][None]], 0)
    HP = 1408 // 128
    d["l2_w1"] = np.ascontiguousarray(w1)
    d["l2_w3"] = np.ascontiguousarray(w3)
    d["l2_w2"] = np.ascontiguousarray(w2)
    d["l2_b1"] = np.ascontiguousarray(
        np.stack([b.reshape(HP, 128).T for b in b1]))
    d["l2_b3"] = np.ascontiguousarray(
        np.stack([b.reshape(HP, 128).T for b in b3]))
    d["l2_b2"] = np.ascontiguousarray(b2).reshape(9, 1, 1024)
    return d


def emit_l2(ctx, T, H, x_in, x_out, dbg=None):
    xn_d = ctx.dram("l2xn_d", [BS, DC, 128, S], BF16)
    a_d = ctx.dram("l2a_d", [BS, DC, 128, S], F32)
    y_d = ctx.dram("l2y_d", [BS, S // 128, 128, D], F32)

    A1, A2, HA, XNT = T["A1"], T["A2"], T["HA"], T["XNT"]
    SCRF, SCRB, STT = T["SCRF"], T["SCRB"], T["STT"]
    P = T["PS"]
    onesrf_t = T["_onesrf_t"]

    ctx.dma("sync", T["IDF"].full(), H["c_identf"][:, :])
    emit_layernorm_to_dram(ctx, T, x_in, xn_d)
    if dbg is not None and "xn" in dbg:
        for s in range(BS):
            for c in range(DC):
                ctx.dma("gpsimd", dbg["xn"][s, c, :, :], xn_d[s, c, :, :])

    ctx.dma("sync", SCRF[0:128, 0:1024], H["l2_adft"][:, :])
    ctx.dma("sync", SCRF[0:128, 1024:1026], H["l2_ap"][:, :])
    for wc in range(4):
        ctx.mmgroup(P[0][0:128, 0:1], [
            (SCRF[0:128, k * 512 + wc * 128:k * 512 + (wc + 1) * 128],
             SCRF[0:128, 1024 + k:1024 + k + 1]) for k in range(2)])
        ctx.copy("vector", SCRF[0:128, 1032 + wc:1033 + wc], P[0][0:128, 0:1])
    OM = lambda: SCRF[0:128, 1032:1036]
    ctx.tt("vector", ALU.mult, SCRF[0:128, 1040:1044], OM(), OM())
    ctx.ts("vector", SCRF[0:128, 1044:1048], SCRF[0:128, 1040:1044], 1.0, None, ALU.add)
    ctx.recip(SCRF[0:128, 1044:1048], SCRF[0:128, 1044:1048])
    ctx.ts("vector", SCRF[0:128, 1048:1052], SCRF[0:128, 1040:1044], -1.0, 1.0,
           ALU.mult, ALU.add)
    ctx.tt("vector", ALU.mult, SCRF[0:128, 1048:1052], SCRF[0:128, 1048:1052],
           SCRF[0:128, 1044:1048])
    ctx.ts("vector", SCRF[0:128, 1052:1056], OM(), -2.0, None, ALU.mult)
    ctx.tt("vector", ALU.mult, SCRF[0:128, 1052:1056], SCRF[0:128, 1052:1056],
           SCRF[0:128, 1044:1048])
    ctx.copy("vector", SCRF[0:128, 1056:1060], SCRF[0:128, 1048:1052])
    ctx.copy("vector", SCRF[0:128, 1060:1064], SCRF[0:128, 1052:1056])
    for c in range(4):
        ctx.memset("vector", f32v(A1, c * 2048, c * 2048 + 1), 1.0)
        ctx.memset("vector", f32v(A1, 8192 + c * 2048, 8192 + c * 2048 + 1), 0.0)
    L = 1
    while L < S:
        for c in range(4):
            Cb = lambda a, b: f32v(A1, c * 2048 + a, c * 2048 + b)
            Sb = lambda a, b: f32v(A1, 8192 + c * 2048 + a, 8192 + c * 2048 + b)
            mr = SCRF[0:128, 1056 + c:1057 + c]
            mi = SCRF[0:128, 1060 + c:1061 + c]
            ctx.ts("vector", SCRF[0:128, 1536:1536 + L], Sb(0, L), mi, None, ALU.mult)
            ctx.ts("vector", SCRF[0:128, 2560:2560 + L], Cb(0, L), mi, None, ALU.mult)
            ctx.stt("vector", Cb(L, 2 * L), Cb(0, L), mr,
                    SCRF[0:128, 1536:1536 + L], ALU.mult, ALU.subtract)
            ctx.stt("vector", Sb(L, 2 * L), Sb(0, L), mr,
                    SCRF[0:128, 2560:2560 + L], ALU.mult, ALU.add)
        if 2 * L < S:
            MR = SCRF[0:128, 1056:1060]
            MI = SCRF[0:128, 1060:1064]
            ctx.tt("vector", ALU.mult, SCRF[0:128, 1064:1068], MR, MI)
            ctx.tt("vector", ALU.mult, SCRF[0:128, 1068:1072], MR, MR)
            ctx.tt("vector", ALU.mult, SCRF[0:128, 1072:1076], MI, MI)
            ctx.tt("vector", ALU.subtract, MR, SCRF[0:128, 1068:1072],
                   SCRF[0:128, 1072:1076])
            ctx.ts("vector", MI, SCRF[0:128, 1064:1068], 2.0, None, ALU.mult)
        L *= 2
    ctx.memset("vector", f32v(SCRB, 0, 2048), 1.0)

    for k in range(4):
        ctx.dma("gpsimd", A2[0:128, 24576 + k * 1024:24576 + (k + 1) * 1024],
                H["l2_cwT"][k * 128:(k + 1) * 128, :])
    for k in range(DC):
        ctx.dma("gpsimd", A2[0:128, 28672 + k * 512:28672 + (k + 1) * 512],
                H["l2_bwT"][k * 128:(k + 1) * 128, :])
    ctx.dma("sync", T["BOS"].full(), H["l2_D"][:, :])

    for s in range(BS):
        for nb in range(S // 512):
            t0 = nb * 512
            load_xn_tile(ctx, T, xn_d, s, t0, 512)
            for m in range(4):
                pb = P[m % 2]
                ctx.mmgroup(pb[0:128, 0:512],
                            [(A2[0:128, 28672 + k * 512 + m * 128:
                                 28672 + k * 512 + (m + 1) * 128],
                              XNT[0:128, k * 512:(k + 1) * 512])
                             for k in range(DC)])
                ctx.copy("scalar", f32v(A2, m * 2048 + t0, m * 2048 + t0 + 512),
                         pb[0:128, 0:512])
        for c in range(4):
            BUc = lambda: f32v(A2, c * 2048, (c + 1) * 2048)
            Cc = lambda: f32v(A1, c * 2048, (c + 1) * 2048)
            Sc = lambda: f32v(A1, 8192 + c * 2048, 8192 + (c + 1) * 2048)
            T1 = lambda: f32v(HA, 0, 2048)
            Q1 = lambda: f32v(HA, 2048, 4096)
            Q2 = lambda: f32v(HA, 4096, 6144)
            ONE = lambda: f32v(SCRB, 0, 2048)
            ctx.tt("vector", ALU.mult, T1(), Cc(), BUc())
            ctx.scan(Q1(), ONE(), T1(), 0.0)
            ctx.tt("vector", ALU.mult, T1(), Sc(), BUc())
            ctx.scan(Q2(), ONE(), T1(), 0.0)
            ctx.tt("vector", ALU.mult, Q1(), Cc(), Q1())
            ctx.tt("vector", ALU.mult, Q2(), Sc(), Q2())
            ctx.tt("vector", ALU.add, A2[0:128, 16384 + c * 2048:
                                         16384 + (c + 1) * 2048], Q1(), Q2())
        for nb in range(S // 512):
            t0 = nb * 512
            load_xn_tile(ctx, T, xn_d, s, t0, 512)
            for m in range(DC):
                pb = P[2 + m % 2]
                ctx.mmgroup(pb[0:128, 0:512],
                            [(A2[0:128, 24576 + k * 1024 + m * 128:
                                 24576 + k * 1024 + (m + 1) * 128],
                              A2[0:128, 16384 + k * 2048 + t0:
                                 16384 + k * 2048 + t0 + 512])
                             for k in range(4)])
                ctx.stt("vector", SCRF[0:128, (m % 4) * 512:(m % 4 + 1) * 512],
                        XNT[0:128, m * 512:(m + 1) * 512],
                        T["BOS"][0:128, m:m + 1], pb[0:128, 0:512],
                        ALU.mult, ALU.add)
                ctx.dma("gpsimd", a_d[s, m, :, t0:t0 + 512],
                        SCRF[0:128, (m % 4) * 512:(m % 4 + 1) * 512])
    if dbg is not None and "a" in dbg:
        for s in range(BS):
            for c in range(DC):
                ctx.dma("gpsimd", dbg["a"][s, c, :, :], a_d[s, c, :, :])

    for c in range(DC):
        ctx.dma("gpsimd", A2[0:128, c * 8:(c + 1) * 8],
                H["l2_gwT"][c * 128:(c + 1) * 128, :])
    ctx.dma("sync", STT[0:1, 0:8], H["l2_gb"][:, :])
    ctx.mmgroup(P[5][0:128, 0:8],
                [(onesrf_t[0:1, 0:128], STT[0:1, 0:8])])
    ctx.copy("vector", T["RBS"].full(), P[5][0:128, 0:8])
    for s in range(BS):
        for tt in range(16):
            t0 = tt * 128
            ctx.dma("sync", XNT.strided(0, 128, 0, DC, 128, 128),
                    d4T(xn_d, s, t0, t0 + 128))
            pb = P[tt % 2]
            ctx.mmgroup(pb[0:128, 0:8],
                        [(XNT[0:128, c * 128:(c + 1) * 128],
                          A2[0:128, c * 8:(c + 1) * 8]) for c in range(DC)])
            lg = SCRF[0:128, 0:8]
            bi = SCRF[0:128, 8:16]
            mx = SCRF[0:128, 16:17]
            sm = SCRF[0:128, 17:18]
            msk = SCRF[0:128, 24:32]
            ctx.copy("vector", lg, pb[0:128, 0:8])
            ctx.reduce("vector", ALU.max, mx, lg, negate=True)
            ctx.act("scalar", lg, lg, AF.Exp, bias=mx, accum_out=sm)
            ctx.recip(sm, sm)
            ctx.ts("vector", lg, lg, sm, None, ALU.mult)
            ctx.tt("vector", ALU.add, bi, lg, T["RBS"].full())
            ctx.reduce("vector", ALU.max, mx, bi)
            ctx.ts("vector", msk, bi, mx, None, ALU.is_ge)
            ctx.stt("vector", bi, msk, -1.0e30, bi, ALU.mult, ALU.add)
            ctx.reduce("vector", ALU.max, mx, bi)
            ctx.tt("vector", ALU.add, bi, lg, T["RBS"].full())
            ctx.ts("vector", msk, bi, mx, None, ALU.is_ge)
            ctx.tt("vector", ALU.mult,
                   T["GAT"][0:128, s * 128 + tt * 8:s * 128 + (tt + 1) * 8],
                   lg, msk)

    HP = 11
    for e in range(9):
        for c in range(DC):
            ctx.dma("gpsimd", A1[0:128, c * 1408:(c + 1) * 1408],
                    H["l2_w1"][e, c * 128:(c + 1) * 128, :])
            ctx.dma("gpsimd", A1[0:128, 16384 + c * 1408:16384 + (c + 1) * 1408],
                    H["l2_w3"][e, c * 128:(c + 1) * 128, :])
        for c2 in range(HP):
            ctx.dma("gpsimd", A2[0:128, c2 * 1024:(c2 + 1) * 1024],
                    H["l2_w2"][e, c2 * 128:(c2 + 1) * 128, :])
        ctx.dma("sync", T["B1S"][0:128, 0:11], H["l2_b1"][e, :, :])
        ctx.dma("sync", T["B1S"][0:128, 16:27], H["l2_b3"][e, :, :])
        ctx.dma("sync", STT[0:1, 0:1024], H["l2_b2"][e, :, :])
        for half in range(2):
            ctx.mmgroup(P[6][0:128, 0:512],
                        [(onesrf_t[0:1, 0:128],
                          STT[0:1, half * 512:(half + 1) * 512])])
            ctx.copy("vector",
                     SCRF[0:128, 3072 + half * 512:3072 + (half + 1) * 512],
                     P[6][0:128, 0:512])
        for s in range(BS):
            for nb in range(S // 512):
                t0 = nb * 512
                load_xn_tile(ctx, T, xn_d, s, t0, 512)
                for m in range(HP):
                    p1 = P[0]
                    p3 = P[1]
                    ctx.mmgroup(p1[0:128, 0:512],
                                [(A1[0:128, c * 1408 + m * 128:
                                     c * 1408 + (m + 1) * 128],
                                  XNT[0:128, c * 512:(c + 1) * 512])
                                 for c in range(DC)])
                    ctx.mmgroup(p3[0:128, 0:512],
                                [(A1[0:128, 16384 + c * 1408 + m * 128:
                                     16384 + c * 1408 + (m + 1) * 128],
                                  XNT[0:128, c * 512:(c + 1) * 512])
                                 for c in range(DC)])
                    tb = SCRF[0:128, 0:512]
                    ctx.ts("vector", tb, p3[0:128, 0:512],
                           T["B1S"][0:128, 16 + m:17 + m], None, ALU.add)
                    ctx.stt("vector", SCRF[0:128, 512:1024], p1[0:128, 0:512],
                            T["B1S"][0:128, m:m + 1], tb, ALU.add, ALU.mult)
                    ctx.act("scalar", HA[0:128, m * 512:(m + 1) * 512],
                            SCRF[0:128, 512:1024], AF.Silu)
                for st in range(4):
                    tt = nb * 4 + st
                    gsl = T["GAT"][0:128, s * 128 + tt * 8 + min(e, 7):
                                   s * 128 + tt * 8 + min(e, 7) + 1]
                    for half in range(2):
                        pb = P[2 + (st * 2 + half) % 2]
                        ctx.mmgroup(pb[0:128, 0:512],
                                    [(HA[0:128, k2 * 512 + st * 128:
                                         k2 * 512 + (st + 1) * 128],
                                      A2[0:128, k2 * 1024 + half * 512:
                                         k2 * 1024 + (half + 1) * 512])
                                     for k2 in range(HP)])
                        tb = SCRF[0:128, 1024 + (st % 2) * 1024 + half * 512:
                                  1024 + (st % 2) * 1024 + half * 512 + 512]
                        ctx.tt("vector", ALU.add, tb, pb[0:128, 0:512],
                               SCRF[0:128, 3072 + half * 512:
                                    3072 + (half + 1) * 512])
                        if e < 8:
                            ctx.act("scalar", tb, tb, AF.Copy, scale=gsl)
                        ctx.dma("gpsimd",
                                y_d[s, tt, :, half * 512:(half + 1) * 512],
                                tb, accum=(e > 0))
    if dbg is not None and "y" in dbg:
        for s in range(BS):
            for tt in range(16):
                ctx.dma("gpsimd", dbg["y"][s, tt, :, :], y_d[s, tt, :, :])

    emit_combine(ctx, T, x_in, x_out, a_d, y_d, bo=None)


def emit_combine(ctx, T, x_in, x_out, a_d, y_d, bo=None):
    SCRF, P = T["SCRF"], T["PS"]
    for s in range(BS):
        for tt in range(16):
            t0 = tt * 128
            ctx.dma("sync", SCRF[0:128, 0:1024], y_d[s, tt, :, :])
            ctx.dma("sync", SCRF.strided(0, 128, 1024, DC, 128, 128),
                    d4T(x_in, s, t0, t0 + 128))
            ctx.dma("scalar", SCRF.strided(0, 128, 2048, DC, 128, 128),
                    d4T(a_d, s, t0, t0 + 128))
            for c in range(DC):
                pb = P[4 + c % 2]
                ctx.transpose(pb[0:128, 0:128],
                              SCRF[0:128, c * 128:(c + 1) * 128],
                              T["IDF"].full())
                ob = SCRF[0:128, 3072 + (c % 4) * 128:3072 + (c % 4 + 1) * 128]
                if bo is not None:
                    ctx.stt("vector", ob, pb[0:128, 0:128],
                            bo[0:128, c:c + 1],
                            SCRF[0:128, 1024 + c * 128:1024 + (c + 1) * 128],
                            ALU.add, ALU.add)
                else:
                    ctx.tt("vector", ALU.add, ob, pb[0:128, 0:128],
                           SCRF[0:128, 1024 + c * 128:1024 + (c + 1) * 128])
                ctx.tt("vector", ALU.add, ob, ob,
                       SCRF[0:128, 2048 + c * 128:2048 + (c + 1) * 128])
                ctx.dma("gpsimd", x_out[s, c, :, t0:t0 + 128], ob,
                        is_output=True)



def l1_rotary_consts():
    DH, N, W = 64, 2048, 512
    half = np.arange(0, DH, 2, dtype=np.float64)
    inv_freq = 1.0 / 10000.0 ** (half / DH)
    t = np.arange(N, dtype=np.float64)
    fr = t[:, None] * inv_freq[None, :]
    fr = np.concatenate([fr, fr], -1)
    base = (half + 0.4 * DH) / (1.4 * DH)
    power = (t - N // 2) / float(W // 2)
    sc = base[None, :] ** power[:, None]
    sc = np.concatenate([sc, sc], -1)
    cos, sin = np.cos(fr), np.sin(fr)
    sgn = np.where(np.arange(DH) < 32, -1.0, 1.0)[None, :]
    CSQ = (cos * sc)
    SSQ = (sgn * sin * sc)
    CSK = (cos / sc)
    SSK = (sgn * sin / sc)
    out = {}
    for nm, tab in (("c_l1csq", CSQ), ("c_l1ssq", SSQ),
                    ("c_l1csk", CSK), ("c_l1ssk", SSK)):
        tt = np.tile(tab.T, (2, 1))
        out[nm] = np.ascontiguousarray(tt).astype(np.float32)
    m = np.zeros((128, 4096), np.float32)
    for qt in range(4):
        i_loc = qt * 128 + np.arange(128)[:, None]
        jj = np.arange(512)[None, :]
        m[:, qt * 512:(qt + 1) * 512] = np.where(jj < i_loc, -1e30, 0.0)
        m[:, 2048 + qt * 512:2048 + (qt + 1) * 512] = \
            np.where(jj > i_loc, -1e30, 0.0)
    out["c_l1mask"] = m
    return out


def prep_l1(inputs):
    d = {}
    d["l1_winT"] = np.ascontiguousarray(inputs["L1_attn_W_in"].T)
    d["l1_waT"] = np.ascontiguousarray(inputs["L1_attn_W_alpha"].T)
    d["l1_wbT"] = np.ascontiguousarray(inputs["L1_attn_W_beta"].T)
    d["l1_wvT"] = np.ascontiguousarray(inputs["L1_attn_W_v"].T)
    d["l1_woutT"] = np.ascontiguousarray(inputs["L1_attn_W_out"].T)
    d["l1_ba"] = bias_chunks(inputs["L1_attn_b_alpha"])
    d["l1_bb"] = bias_chunks(inputs["L1_attn_b_beta"])
    d["l1_bv"] = bias_chunks(inputs["L1_attn_b_v"])
    d["l1_wqkv"] = inputs["L1_ffn_wqkv"]
    d["l1_qs"] = np.ascontiguousarray(
        np.tile(inputs["L1_ffn_q_scale"], (1, 2))[:, :, None])
    d["l1_ks"] = np.ascontiguousarray(
        np.tile(inputs["L1_ffn_k_scale"], (1, 2))[:, :, None])
    d["l1_wo"] = inputs["L1_ffn_wo"]
    d["l1_lg"] = np.ascontiguousarray(
        np.stack([bias_chunks(g) for g in inputs["L1_ffn_ln_g"]]))
    d["l1_lb"] = np.ascontiguousarray(
        np.stack([bias_chunks(b) for b in inputs["L1_ffn_ln_b"]]))
    w1 = inputs["L1_ffn_w1"]
    HID = 2730
    HPAD = 2816
    w1a = np.zeros((2, 1024, HPAD), np.float32)
    w1g = np.zeros((2, 1024, HPAD), np.float32)
    w1a[:, :, :HID] = w1[:, :, :HID]
    w1g[:, :, :HID] = w1[:, :, HID:]
    b1 = inputs["L1_ffn_b1"]
    b1a = np.zeros((2, HPAD), np.float32)
    b1g = np.zeros((2, HPAD), np.float32)
    b1a[:, :HID] = b1[:, :HID]
    b1g[:, :HID] = b1[:, HID:]
    w2 = inputs["L1_ffn_w2"]
    w2p = np.zeros((2, HPAD, 1024), np.float32)
    w2p[:, :HID] = w2
    d["l1_w1a"] = np.ascontiguousarray(w1a.reshape(2, 8, 128, HPAD))
    d["l1_w1g"] = np.ascontiguousarray(w1g.reshape(2, 8, 128, HPAD))
    d["l1_w2p"] = np.ascontiguousarray(w2p)
    d["l1_b1a"] = np.ascontiguousarray(
        np.stack([b.reshape(22, 128).T for b in b1a]))
    d["l1_b1g"] = np.ascontiguousarray(
        np.stack([b.reshape(22, 128).T for b in b1g]))
    d["l1_b2"] = np.ascontiguousarray(inputs["L1_ffn_b2"]).reshape(2, 1, 1024)
    return d


def emit_l1(ctx, T, H, x_in, x_out, dbg=None):
    xn_d = ctx.dram("l1xn_d", [BS, DC, 128, S], BF16)
    a_d = ctx.dram("l1a_d", [BS, DC, 128, S], F32)
    f_d = ctx.dram("l1f_d", [BS, DC, 128, S], BF16)

    A1, A2, HA, XNT = T["A1"], T["A2"], T["HA"], T["XNT"]
    SCRF, SCRB, STT = T["SCRF"], T["SCRB"], T["STT"]
    P = T["PS"]
    onesrf_t = T["_onesrf_t"]
    SEL = T["B1S"]

    ctx.dma("gpsimd", T["IDB"].full(), H["c_identf"][:, :])
    emit_layernorm_to_dram(ctx, T, x_in, xn_d)
    if dbg is not None and "xn" in dbg:
        for s in range(BS):
            for c in range(DC):
                ctx.dma("gpsimd", dbg["xn"][s, c, :, :], xn_d[s, c, :, :])

    for s in range(BS):
        for c in range(DC):
            ctx.dma("gpsimd", A2[0:128, c * 1024:(c + 1) * 1024],
                    H["l1_winT"][c * 128:(c + 1) * 128, :])
        for nb in range(S // 512):
            t0 = nb * 512
            load_xn_tile(ctx, T, xn_d, s, t0, 512)
            for m in range(DC):
                pb = P[m % 2]
                ctx.mmgroup(pb[0:128, 0:512],
                            [(A2[0:128, c * 1024 + m * 128:c * 1024 + (m + 1) * 128],
                              XNT[0:128, c * 512:(c + 1) * 512])
                             for c in range(DC)])
                ctx.act("scalar", A1[0:128, m * S + t0:m * S + t0 + 512],
                        pb[0:128, 0:512], AF.Silu)
        for c in range(DC):
            ctx.dma("gpsimd", A2[0:128, c * 1024:(c + 1) * 1024],
                    H["l1_waT"][c * 128:(c + 1) * 128, :])
            ctx.dma("gpsimd", A2[0:128, 8192 + c * 1024:8192 + (c + 1) * 1024],
                    H["l1_wbT"][c * 128:(c + 1) * 128, :])
            ctx.dma("gpsimd", A2[0:128, 16384 + c * 1024:16384 + (c + 1) * 1024],
                    H["l1_wvT"][c * 128:(c + 1) * 128, :])
        ctx.dma("sync", SEL[0:128, 0:8], H["l1_ba"][:, :])
        ctx.dma("sync", SEL[0:128, 8:16], H["l1_bb"][:, :])
        ctx.dma("sync", SEL[0:128, 16:24], H["l1_bv"][:, :])
        for m in range(DC):
            for nb in range(S // 512):
                t0 = nb * 512
                pa, pbt, pv = P[0], P[1], P[2]
                for (pp, woff) in ((pa, 0), (pbt, 8192), (pv, 16384)):
                    ctx.mmgroup(pp[0:128, 0:512],
                                [(A2[0:128, woff + c * 1024 + m * 128:
                                     woff + c * 1024 + (m + 1) * 128],
                                  A1[0:128, c * S + t0:c * S + t0 + 512])
                                 for c in range(DC)])
                ctx.act("scalar", SCRF[0:128, t0:t0 + 512], pa[0:128, 0:512],
                        AF.Sigmoid, bias=SEL[0:128, m:m + 1])
                ctx.act("scalar", f32v(HA, 0, 512), pbt[0:128, 0:512],
                        AF.Sigmoid, bias=SEL[0:128, 8 + m:9 + m])
                ctx.act("scalar", f32v(HA, 512, 1024), pv[0:128, 0:512],
                        AF.Tanh, bias=SEL[0:128, 16 + m:17 + m])
                ctx.tt("vector", ALU.mult, SCRF[0:128, 2048 + t0:2048 + t0 + 512],
                       f32v(HA, 0, 512), f32v(HA, 512, 1024))
            ctx.scan(f32v(HA, 1024, 3072), SCRF[0:128, 0:2048],
                     SCRF[0:128, 2048:4096], 0.0)
            ctx.act("scalar", f32v(HA, 3072, 5120), f32v(HA, 1024, 3072), AF.Silu)
            ctx.tt("vector", ALU.mult, A1[0:128, 16384 + m * S:16384 + (m + 1) * S],
                   f32v(HA, 1024, 3072), f32v(HA, 3072, 5120))
        for c in range(DC):
            ctx.dma("gpsimd", A2[0:128, c * 1024:(c + 1) * 1024],
                    H["l1_woutT"][c * 128:(c + 1) * 128, :])
        for nb in range(S // 512):
            t0 = nb * 512
            for m in range(DC):
                pb = P[m % 2]
                ctx.mmgroup(pb[0:128, 0:512],
                            [(A2[0:128, c * 1024 + m * 128:c * 1024 + (m + 1) * 128],
                              A1[0:128, 16384 + c * S + t0:16384 + c * S + t0 + 512])
                             for c in range(DC)])
                ctx.copy("scalar", SCRF[0:128, (m % 4) * 512:(m % 4 + 1) * 512],
                         pb[0:128, 0:512])
                ctx.dma("gpsimd", a_d[s, m, :, t0:t0 + 512],
                        SCRF[0:128, (m % 4) * 512:(m % 4 + 1) * 512])
    if dbg is not None and "a" in dbg:
        for s in range(BS):
            for c in range(DC):
                ctx.dma("gpsimd", dbg["a"][s, c, :, :], a_d[s, c, :, :])

    SELB = T["SELB"]
    ctx.memset("vector", SELB[0:128, 0:2], 0.0)
    ctx.memset("vector", SELB[0:64, 0:1], 1.0)
    ctx.memset("vector", SELB[64:128, 1:2], 1.0)
    NW = S // 512

    for s in range(BS):
        for c in range(DC):
            ctx.dma("sync", A1[0:128, c * S:(c + 1) * S], xn_d[s, c, :, :])
        for l in range(2):
            for c in range(DC):
                ctx.dma("gpsimd", A2[0:128, c * 1536:(c + 1) * 1536],
                        H["l1_wqkv"][l, c * 128:(c + 1) * 128, :])
            for c in range(4):
                ctx.dma("gpsimd", A2[0:128, 12288 + c * 1024:12288 + (c + 1) * 1024],
                        H["l1_wo"][l, c * 128:(c + 1) * 128, :])
            ctx.dma("sync", SEL[0:128, 24:25], H["l1_qs"][l, :, :])
            ctx.dma("sync", SEL[0:128, 25:26], H["l1_ks"][l, :, :])
            for (nm, off, scol) in (("c_l1csq", 16384, 24), ("c_l1ssq", 18432, 24),
                                    ("c_l1csk", 20480, 25), ("c_l1ssk", 22528, 25)):
                ctx.dma("gpsimd", A2[0:128, off:off + 2048], H[nm][:, :])
                ctx.ts("vector", A2[0:128, off:off + 2048],
                       A2[0:128, off:off + 2048], SEL[0:128, scol:scol + 1],
                       None, ALU.mult)
            ctx.dma("gpsimd", A2[0:128, 24576:28672], H["c_l1mask"][:, :])
            for w in range(NW):
                t0 = w * 512
                qb = 6144 * (w % 2)
                if w == 0:
                    for ci in range(4):
                        c = 4 + ci
                        ctx.copy("vector",
                                 HA[0:128, 12288 + ci * 512 + 1:
                                    12288 + (ci + 1) * 512],
                                 A1[0:128, c * S:c * S + 511])
                        ctx.memset("vector",
                                   HA[0:128, 12288 + ci * 512:12288 + ci * 512 + 1],
                                   0.0)
                for m in range(12):
                    pb = P[m % 2]
                    pairs = []
                    for c in range(DC):
                        if c < 4:
                            rhs = A1[0:128, c * S + t0:c * S + t0 + 512]
                        elif w == 0:
                            rhs = HA[0:128, 12288 + (c - 4) * 512:
                                     12288 + (c - 3) * 512]
                        else:
                            rhs = A1[0:128, c * S + t0 - 1:c * S + t0 + 511]
                        pairs.append(
                            (A2[0:128, c * 1536 + m * 128:c * 1536 + (m + 1) * 128],
                             rhs))
                    ctx.mmgroup(pb[0:128, 0:512], pairs)
                    ctx.copy("scalar", HA[0:128, qb + m * 512:qb + (m + 1) * 512],
                             pb[0:128, 0:512])
                for ch in range(8):
                    src = HA[0:128, qb + ch * 512:qb + (ch + 1) * 512]
                    U = f32v(SCRB, 0, 512)
                    SW = f32v(SCRB, 512, 1024)
                    T1 = f32v(SCRB, 1024, 1536)
                    SQB = SCRB[0:128, 3072:3584]
                    ctx.tt("vector", ALU.mult, SQB, src, src)
                    ctx.mmgroup(P[4][0:1, 0:512], [(SELB[0:128, 0:1], SQB)])
                    ctx.mmgroup(P[5][0:1, 0:512], [(SELB[0:128, 1:2], SQB)])
                    for (pp, fo) in ((P[4], 0), (P[5], 512)):
                        ctx.act("scalar", STT[0:1, fo:fo + 512], pp[0:1, 0:512],
                                AF.Sqrt)
                        ctx.ts("vector", STT[0:1, fo:fo + 512],
                               STT[0:1, fo:fo + 512], 1e-12, None, ALU.max)
                        ctx.recip(STT[0:1, fo:fo + 512], STT[0:1, fo:fo + 512])
                    rb = P[6]
                    ctx.mmgroup(rb[0:64, 0:512],
                                [(onesrf_t[0:1, 0:64], STT[0:1, 0:512])])
                    ctx.mmgroup(rb[64:128, 0:512],
                                [(onesrf_t[0:1, 0:64], STT[0:1, 512:1024])])
                    ctx.tt("vector", ALU.mult, U, src, rb[0:128, 0:512])
                    for bb in range(4):
                        sb = (bb ^ 1) * 32
                        ctx.copy("vector" if bb % 2 else "scalar",
                                 f32vp(SCRB, bb * 32, bb * 32 + 32, 512, 1024),
                                 f32vp(SCRB, sb, sb + 32, 0, 512))
                    toff = 16384 if ch < 4 else 20480
                    ctx.tt("vector", ALU.mult, T1, SW,
                           A2[0:128, toff + 2048 + t0:toff + 2048 + t0 + 512])
                    ctx.tt("vector", ALU.mult, U, U,
                           A2[0:128, toff + t0:toff + t0 + 512])
                    ctx.tt("vector", ALU.add, src, U, T1)
                for h in range(8):
                    po = 64 * (h % 2)
                    chq = h // 2
                    for kb in range(8):
                        if w == 0 and kb < 4:
                            continue
                        buf = qb if kb >= 4 else 6144 * ((w - 1) % 2)
                        kk = kb % 4
                        src_v = HA[po:po + 64,
                                   buf + (8 + chq) * 512 + kk * 128:
                                   buf + (8 + chq) * 512 + (kk + 1) * 128]
                        ptp = (P[7].h[0:128, 0:32].bitcast(BF16),
                               Span(P[7].name, 0, 128, 0, 128))
                        ctx.transpose(ptp, src_v, T["IDB"][po:po + 64, po:po + 64])
                        ctx.copy("vector", SCRB[0:128, kb * 64:(kb + 1) * 64], ptp)
                    for qt in range(4):
                        qsl = HA[po:po + 64, qb + chq * 512 + qt * 128:
                                 qb + chq * 512 + (qt + 1) * 128]
                        sp0 = P[0]
                        sp1 = P[1]
                        if w > 0:
                            ctx.mmgroup(sp0[0:128, 0:512],
                                        [(qsl, HA[po:po + 64,
                                                  6144 * ((w - 1) % 2) + (4 + chq) * 512:
                                                  6144 * ((w - 1) % 2) + (4 + chq) * 512 + 512])])
                        ctx.mmgroup(sp1[0:128, 0:512],
                                    [(qsl, HA[po:po + 64, qb + (4 + chq) * 512:
                                              qb + (4 + chq) * 512 + 512])])
                        SSa = f32v(HA, 7168, 7680)
                        SSb = f32v(HA, 7680, 8192)
                        SSfull = f32v(HA, 7168, 8192)
                        if w > 0:
                            ctx.stt("vector", SSa, sp0[0:128, 0:512], 8.0,
                                    A2[0:128, 24576 + qt * 512:24576 + (qt + 1) * 512],
                                    ALU.mult, ALU.add)
                        else:
                            ctx.memset("vector", SSa, -1e30)
                        ctx.stt("vector", SSb, sp1[0:128, 0:512], 8.0,
                                A2[0:128, 26624 + qt * 512:26624 + (qt + 1) * 512],
                                ALU.mult, ALU.add)
                        mx = SCRF[0:128, 3072:3073]
                        sm = SCRF[0:128, 3073:3074]
                        rc = SCRF[0:128, 3074:3075]
                        ctx.reduce("vector", ALU.max, mx, SSfull, negate=True)
                        ctx.act("scalar", f32v(HA, 6144, 7168), SSfull, AF.Exp,
                                bias=mx, accum_out=sm)
                        ctx.recip(rc, sm)
                        ctx.ts("vector", SCRB[0:128, 1024:2048],
                               f32v(HA, 6144, 7168), rc, None, ALU.mult)
                        av = P[2 if qt % 2 == 0 else 3]
                        first = True
                        for kb in range(8):
                            if w == 0 and kb < 4:
                                continue
                            ptp = (P[7].h[0:128, 0:64].bitcast(BF16),
                                   Span(P[7].name, 0, 128, 0, 256))
                            ctx.transpose(ptp,
                                          SCRB[0:128, 1024 + kb * 128:
                                               1024 + (kb + 1) * 128],
                                          T["IDB"].full())
                            ctx.copy("vector", SCRB[0:128, 512:640], ptp)
                            ctx.mmgroup(av[0:64, 0:128],
                                        [(SCRB[0:128, kb * 64:(kb + 1) * 64],
                                          SCRB[0:128, 512:640])],
                                        accum_into=(not first),
                                        stop_group=(kb == 7))
                            first = False
                        ctx.copy("scalar",
                                 A1[po:po + 64, 16384 + chq * S + t0 + qt * 128:
                                    16384 + chq * S + t0 + (qt + 1) * 128],
                                 av[0:64, 0:128])
            for nb in range(NW):
                t0 = nb * 512
                for m in range(DC):
                    pb = P[m % 2]
                    ctx.mmgroup(pb[0:128, 0:512],
                                [(A2[0:128, 12288 + c * 1024 + m * 128:
                                     12288 + c * 1024 + (m + 1) * 128],
                                  A1[0:128, 16384 + c * S + t0:
                                     16384 + c * S + t0 + 512])
                                 for c in range(4)])
                    ctx.tt("vector", ALU.add, A1[0:128, m * S + t0:m * S + t0 + 512],
                           A1[0:128, m * S + t0:m * S + t0 + 512],
                           pb[0:128, 0:512])
            ctx.dma("sync", SEL[0:128, 0:8], H["l1_lg"][l, :, :])
            ctx.dma("sync", SEL[0:128, 8:16], H["l1_lb"][l, :, :])
            ctx.dma("sync", STT[0:1, 1024:2048], H["l1_b2"][l, :, :])
            ctx.dma("sync", SEL[0:128, 0:22][0].__class__ and SEL.h[0:128, 32:32],
                    SEL.h[0:128, 32:32]) if False else None
            for c2 in range(22):
                ctx.dma("gpsimd", A2[0:128, c2 * 1024:(c2 + 1) * 1024],
                        H["l1_w2p"][l, c2 * 128:(c2 + 1) * 128, :])
            ctx.dma("sync", T["BOS"][0:128, 0:22] if False else T["GAT"][0:128, 0:22],
                    H["l1_b1a"][l, :, :])
            ctx.dma("sync", T["GAT"][0:128, 32:54], H["l1_b1g"][l, :, :])
            for nb in range(NW):
                t0 = nb * 512
                if nb == 0:
                    for ci in range(4):
                        c = 4 + ci
                        ctx.copy("vector",
                                 HA[0:128, 12288 + ci * 512 + 1:
                                    12288 + (ci + 1) * 512],
                                 A1[0:128, c * S:c * S + 511])
                        ctx.memset("vector",
                                   HA[0:128, 12288 + ci * 512:12288 + ci * 512 + 1],
                                   0.0)

                def shsl(c):
                    if c < 4:
                        return A1[0:128, c * S + t0:c * S + t0 + 512]
                    if nb == 0:
                        return HA[0:128, 12288 + (c - 4) * 512:12288 + (c - 3) * 512]
                    return A1[0:128, c * S + t0 - 1:c * S + t0 + 511]
                ctx.mmgroup(P[0][0:1, 0:512],
                            [(T["ONESBC"], shsl(c)) for c in range(DC)])
                for c in range(DC):
                    ctx.tt("vector", ALU.mult, SCRB[0:128, c * 512 if False else 0:512]
                           if False else SCRB[0:128, 0:512], shsl(c), shsl(c)) \
                        if False else None
                for c in range(DC):
                    ctx.tt("vector", ALU.mult, SCRB[0:128, 0:512], shsl(c), shsl(c))
                    ctx.mmgroup(P[1][0:1, 0:512],
                                [(T["ONESBC"], SCRB[0:128, 0:512])],
                                accum_into=(c > 0), stop_group=(c == DC - 1))
                f = lambda i: STT[0:1, i * 256:(i + 1) * 256] if False else STT[0:1, i * 512:(i + 1) * 512]
                ctx.act("scalar", f(0), P[0][0:1, 0:512], AF.Copy, scale=1.0 / D)
                ctx.act("scalar", f(1), P[1][0:1, 0:512], AF.Copy, scale=1.0 / D)
                ctx.act("scalar", SCRF[0:128, 3072:3584][0:1] if False else STT[0:1, 1536:2048] if False else f(1),
                        f(1), AF.Copy) if False else None
                ctx.act("scalar", STT[0:1, 1536:2048] if False else SCRF[0:1, 0:512],
                        f(0), AF.Square)
                ctx.tt("vector", ALU.subtract, f(1), f(1), SCRF[0:1, 0:512])
                ctx.ts("vector", f(1), f(1), 1e-5, None, ALU.add)
                ctx.recip(SCRF[0:1, 0:512], f(1))
                ctx.act("scalar", f(1), SCRF[0:1, 0:512], AF.Sqrt)
                ctx.tt("vector", ALU.mult, STT[0:1, 1024:1536] if False else SCRF[0:1, 512:1024],
                       f(0), f(1))
                rbr = P[2]
                rbm = P[3]
                ctx.mmgroup(rbr[0:128, 0:512], [(onesrf_t[0:1, 0:128], f(1))])
                ctx.mmgroup(rbm[0:128, 0:512],
                            [(onesrf_t[0:1, 0:128], SCRF[0:1, 512:1024])])
                for c in range(DC):
                    ctx.tt("vector", ALU.mult, f32v(HA, 2048, 2560) if False else SCRB[0:128, 0:512],
                           shsl(c), rbr[0:128, 0:512]) if False else None
                for c in range(DC):
                    ctx.tt("vector", ALU.mult, f32vp(SCRB, 0, 128, 0, 512)[0:2][0] if False else f32v(SCRB, 0, 512),
                           shsl(c), rbr[0:128, 0:512])
                    ctx.tt("vector", ALU.subtract, f32v(SCRB, 512, 1024),
                           f32v(SCRB, 0, 512), rbm[0:128, 0:512])
                    ctx.ts("vector", HA[0:128, c * 512:(c + 1) * 512],
                           f32v(SCRB, 512, 1024), SEL[0:128, c:c + 1],
                           SEL[0:128, 8 + c:9 + c], ALU.mult, ALU.add)
                for m in range(22):
                    ctx.dma("gpsimd", A2[0:128, 22528 + (m % 2) * 1024:
                                         22528 + (m % 2) * 1024 + 1024],
                            d4T(DTen(H["l1_w1a"].h, "l1_w1a_x") if False else H["l1_w1a"], l, 0, 0)
                            ) if False else None
                    pa = P[4]
                    pg = P[5]
                    wsl_a = A2[0:128, 22528 + (m % 4) * 1024:
                               22528 + (m % 4) * 1024 + 1024]
                    ctx.dma("gpsimd", wsl_a,
                            (H["l1_w1a"].h[l, :, :, m * 128:(m + 1) * 128]
                             .transpose([1, 0, 2]),
                             Span(H["l1_w1a"].name, 0, 1, 0, 1 << 62)))
                    wsl_g = A2[0:128, 26624 + (m % 4) * 1024:
                               26624 + (m % 4) * 1024 + 1024]
                    ctx.dma("gpsimd", wsl_g,
                            (H["l1_w1g"].h[l, :, :, m * 128:(m + 1) * 128]
                             .transpose([1, 0, 2]),
                             Span(H["l1_w1g"].name, 0, 1, 0, 1 << 62)))
                    ctx.mmgroup(pa[0:128, 0:512],
                                [(A2[0:128, 22528 + (m % 4) * 1024 + c * 128:
                                     22528 + (m % 4) * 1024 + (c + 1) * 128],
                                  HA[0:128, c * 512:(c + 1) * 512])
                                 for c in range(DC)])
                    ctx.mmgroup(pg[0:128, 0:512],
                                [(A2[0:128, 26624 + (m % 4) * 1024 + c * 128:
                                     26624 + (m % 4) * 1024 + (c + 1) * 128],
                                  HA[0:128, c * 512:(c + 1) * 512])
                                 for c in range(DC)])
                    ctx.act("scalar", f32v(SCRB, 0, 512), pg[0:128, 0:512],
                            AF.Gelu, bias=T["GAT"][0:128, 32 + m:33 + m])
                    ctx.stt("vector", HA[0:128, 4096 + m * 512:4096 + (m + 1) * 512],
                            pa[0:128, 0:512], T["GAT"][0:128, m:m + 1],
                            f32v(SCRB, 0, 512), ALU.add, ALU.mult)
                b2b = P[6]
                for half in range(2):
                    ctx.mmgroup(b2b[0:128, 0:512],
                                [(onesrf_t[0:1, 0:128],
                                  STT[0:1, 1024 + half * 512:1024 + (half + 1) * 512])])
                    ctx.copy("scalar", SCRF[0:128, 2048:2560], b2b[0:128, 0:512])
                    for mo in range(4):
                        m2c = half * 4 + mo
                        pb = P[7]
                        ctx.mmgroup(pb[0:128, 0:512],
                                    [(A2[0:128, k2 * 1024 + m2c * 128:
                                         k2 * 1024 + (m2c + 1) * 128],
                                      HA[0:128, 4096 + k2 * 512:4096 + (k2 + 1) * 512])
                                     for k2 in range(22)])
                        ctx.tt("vector", ALU.add, f32v(SCRB, 0, 512),
                               pb[0:128, 0:512], SCRF[0:128, 2048:2560])
                        ctx.tt("vector", ALU.add,
                               A1[0:128, m2c * S + t0:m2c * S + t0 + 512],
                               A1[0:128, m2c * S + t0:m2c * S + t0 + 512],
                               f32v(SCRB, 0, 512))
        for c in range(DC):
            ctx.dma("sync", f_d[s, c, :, :], A1[0:128, c * S:(c + 1) * S])
    if dbg is not None and "y" in dbg:
        for s in range(BS):
            for c in range(DC):
                ctx.dma("gpsimd", dbg["y"][s, c, :, :], f_d[s, c, :, :])

    emit_combine_B(ctx, T, x_in, x_out, a_d, f_d)


def emit_combine_B(ctx, T, x_in, x_out, a_d, f_d):
    SCRF = T["SCRF"]
    for s in range(BS):
        for c in range(DC):
            for nb in range(S // 512):
                t0 = nb * 512
                q = nb % 2
                ctx.dma("sync", SCRF[0:128, q * 2048:q * 2048 + 512],
                        x_in[s, c, :, t0:t0 + 512])
                ctx.dma("scalar", SCRF[0:128, q * 2048 + 512:q * 2048 + 1024],
                        a_d[s, c, :, t0:t0 + 512])
                ctx.dma("sync", T["SCRB"][0:128, q * 512:q * 512 + 512],
                        f_d[s, c, :, t0:t0 + 512])
                ctx.tt("vector", ALU.add, SCRF[0:128, q * 2048:q * 2048 + 512],
                       SCRF[0:128, q * 2048:q * 2048 + 512],
                       SCRF[0:128, q * 2048 + 512:q * 2048 + 1024])
                ctx.tt("vector", ALU.add, SCRF[0:128, q * 2048:q * 2048 + 512],
                       SCRF[0:128, q * 2048:q * 2048 + 512],
                       T["SCRB"][0:128, q * 512:q * 512 + 512])
                ctx.dma("gpsimd", x_out[s, c, :, t0:t0 + 512],
                        SCRF[0:128, q * 2048:q * 2048 + 512], is_output=True)



def prep_l3(inputs):
    d = {}
    d["l3_wqkvT"] = np.ascontiguousarray(inputs["L3_attn_Wqkv_w"].T)
    d["l3_qkvb"] = np.ascontiguousarray(
        inputs["L3_attn_Wqkv_b"].reshape(24, 128).T)
    d["l3_bv"] = np.ascontiguousarray(
        inputs["L3_attn_Wqkv_b"][2048:].reshape(1, 1024))
    pj = np.ascontiguousarray(inputs["L3_attn_proj"].T)
    d["l3_pj2"] = np.ascontiguousarray(np.vstack([pj, pj]))
    d["l3_outwT"] = np.ascontiguousarray(inputs["L3_attn_out_w"].T)
    d["l3_outb"] = bias_chunks(inputs["L3_attn_out_b"])
    d["l3_rwT"] = np.ascontiguousarray(inputs["L3_ffn_router_w"].T)
    gu = inputs["L3_ffn_gate_up_proj"]
    d["l3_gug"] = np.ascontiguousarray(gu[:, :, :1024])
    d["l3_guu"] = np.ascontiguousarray(gu[:, :, 1024:])
    d["l3_down"] = inputs["L3_ffn_down_proj"]
    d["l3_sgwT"] = np.ascontiguousarray(inputs["L3_ffn_shared_gate_w"].T)
    d["l3_suwT"] = np.ascontiguousarray(inputs["L3_ffn_shared_up_w"].T)
    d["l3_sdwT"] = np.ascontiguousarray(inputs["L3_ffn_shared_down_w"].T)
    return d


def l3_mask_consts():
    i = np.arange(128)[:, None]
    jj = np.arange(128)[None, :]
    mloc = np.zeros((128, 512), np.float32)
    mloc[:, 0:128] = np.where(jj < i, -1e30, 0.0)
    mloc[:, 128:256] = np.where(jj > i, -1e30, 0.0)
    mloc[:, 256:384] = -1e30
    mloc[:, 384:512] = np.where(jj > i, -1e30, 0.0)
    m01 = np.zeros((128, 512), np.float32)
    SCALE = 0.125
    m01[:, 0:128] = np.where(jj >= i, SCALE, 0.0)
    m01[:, 128:256] = np.where(jj <= i, SCALE, 0.0)
    m01[:, 256:384] = 0.0
    m01[:, 384:512] = np.where(jj <= i, SCALE, 0.0)
    tril = np.where(jj >= i, 1.0, 0.0).astype(np.float32)
    return {"c_l3mloc": mloc, "c_l3m01": m01, "c_l3tril": tril}


def emit_l3(ctx, T, H, x_in, x_out, dbg=None):
    xn_d = ctx.dram("l3xn_d", [BS, DC, 128, S], BF16)
    a_d = ctx.dram("l3a_d", [BS, DC, 128, S], F32)
    y_d = ctx.dram("l3y_d", [BS, S // 128, 128, D], F32)
    ctx_d = ctx.dram("l3ctx_d", [BS, DC, 128, S], BF16)
    gates_d = ctx.dram("l3gates_d", [BS, 16, S], F32)

    A1, A2, HA, XNT = T["A1"], T["A2"], T["HA"], T["XNT"]
    SCRF, SCRB, STT = T["SCRF"], T["SCRB"], T["STT"]
    P = T["PS"]
    onesrf_t = T["_onesrf_t"]
    EPS = 1e-4
    SC = 0.125
    RSC = float(math.sqrt(SC))

    ctx.dma("sync", T["IDF"].full(), H["c_identf"][:, :])
    ctx.dma("gpsimd", T["IDB"].full(), H["c_identf"][:, :])
    emit_layernorm_to_dram(ctx, T, x_in, xn_d)
    if dbg is not None and "xn" in dbg:
        for s in range(BS):
            for c in range(DC):
                ctx.dma("gpsimd", dbg["xn"][s, c, :, :], xn_d[s, c, :, :])

    PJ2 = ctx.sbuf("PJ2B", [128, 24], BF16)
    ctx.dma("gpsimd", PJ2[0:128, 0:24], H["l3_pj2"][:, :])
    ctx.dma("sync", T["B1S"][0:128, 0:24], H["l3_qkvb"][:, :])
    MLOC = T["MASK"]
    ctx.dma("gpsimd", MLOC[0:128, 0:512], H["c_l3mloc"][:, :])
    ctx.dma("gpsimd", MLOC[0:128, 512:1024], H["c_l3m01"][:, :])
    ctx.dma("gpsimd", MLOC[0:128, 1024:1152], H["c_l3tril"][:, :])

    for s in range(BS):
        for (part, kind) in ((0, "q"), (1, "k"), (2, "v")):
            for c in range(DC):
                ctx.dma("gpsimd", A2[0:128, 16640 + c * 1024:16640 + (c + 1) * 1024],
                        (H["l3_wqkvT"].h[c * 128:(c + 1) * 128,
                                         part * 1024:(part + 1) * 1024],
                         Span(H["l3_wqkvT"].name, 0, 1, 0, 1 << 62)))
            if kind == "v":
                ctx.dma("sync", STT[0:1, 0:1024], H["l3_bv"][:, :])
                for half in range(2):
                    ctx.mmgroup(P[6][0:128, 0:512],
                                [(onesrf_t[0:1, 0:128],
                                  STT[0:1, half * 512:(half + 1) * 512])])
                    ctx.copy("vector",
                             SCRF[0:128, 2048 + half * 512:2048 + (half + 1) * 512],
                             P[6][0:128, 0:512])
            for nb in range(S // 512):
                t0 = nb * 512
                load_xn_tile(ctx, T, xn_d, s, t0, 512)
                if kind in ("q", "k"):
                    base = 0 if kind == "q" else 16384
                    for m in range(DC):
                        pb = P[m % 2]
                        ctx.mmgroup(pb[0:128, 0:512],
                                    [(A2[0:128, 16640 + c * 1024 + m * 128:
                                         16640 + c * 1024 + (m + 1) * 128],
                                      XNT[0:128, c * 512:(c + 1) * 512])
                                     for c in range(DC)])
                        ctx.ts("vector", A1[0:128, base + m * S + t0:
                                            base + m * S + t0 + 512],
                               pb[0:128, 0:512],
                               T["B1S"][0:128, part * 8 + m:part * 8 + m + 1],
                               None, ALU.add)
                else:
                    for st in range(4):
                        tt = nb * 4 + st
                        for half in range(2):
                            pb = P[2 + (st * 2 + half) % 2]
                            ctx.mmgroup(pb[0:128, 0:512],
                                        [(XNT[0:128, c * 512 + st * 128:
                                              c * 512 + (st + 1) * 128],
                                          A2[0:128, 16640 + c * 1024 + half * 512:
                                             16640 + c * 1024 + (half + 1) * 512])
                                         for c in range(DC)])
                            ctx.tt("vector", ALU.add,
                                   A2.strided(0, 128, tt * 1040 + half * 8 * 65,
                                              8, 65, 64),
                                   pb[0:128, 0:512],
                                   SCRF[0:128, 2048 + half * 512:
                                        2048 + (half + 1) * 512])
        ctx.memset("vector", A2.strided(0, 128, 64, 16 * 16, 65, 1), 1.0)

        for h in range(16):
            po = 64 * (h % 2)
            ch = h // 2

            def lf_pass(base, h=h, po=po, ch=ch):
                for nb in range(4):
                    t0 = nb * 512
                    pxd = P[0]
                    pdg = P[1]
                    ctx.mmgroup(pxd[0:24, 0:512],
                                [(PJ2[po:po + 64, 0:24],
                                  A1[po:po + 64, base + ch * S + t0:
                                     base + ch * S + t0 + 512])])
                    ctx.tt("vector", ALU.mult,
                           (SCRB.h[po:po + 64, 0:512],
                            Span("SCRB", po, po + 64, 0, 1024)),
                           A1[po:po + 64, base + ch * S + t0:
                              base + ch * S + t0 + 512],
                           A1[po:po + 64, base + ch * S + t0:
                              base + ch * S + t0 + 512])
                    ctx.mmgroup(pdg[0:1, 0:512],
                                [(ctx_onesb_slice(T, po),
                                  (SCRB.h[po:po + 64, 0:512],
                                   Span("SCRB", po, po + 64, 0, 1024)))])
                    ctx.act("scalar", STT[0:1, 0:512], pdg[0:1, 0:512], AF.Copy,
                            scale=-0.5 * SC)
                    pbq = P[2]
                    ctx.mmgroup(pbq[0:24, 0:512],
                                [(onesrf_t[0:1, 0:24], STT[0:1, 0:512])])
                    ctx.copy("vector", (SCRB.h[0:24, 1536:2048],
                                        Span("SCRB", 0, 24, 3072, 4096)),
                             pbq[0:24, 0:512])
                    ctx.stt("vector", SCRF[0:24, t0:t0 + 512], pxd[0:24, 0:512],
                            RSC, (SCRB.h[0:24, 1536:2048],
                                  Span("SCRB", 0, 24, 3072, 4096)),
                            ALU.mult, ALU.add)

            lf_pass(0)
            for cc in range(16):
                pt = P[3]
                ctx.transpose(pt[0:128, 0:24], SCRF[0:24, cc * 128:(cc + 1) * 128],
                              T["IDF"][0:24, 0:24])
                ctx.reduce("vector", ALU.max, SCRF[0:128, 3072 + cc:3073 + cc],
                           pt[0:128, 0:24], negate=True)
                ctx.act("scalar", SCRB[0:128, 1024:1048], pt[0:128, 0:24], AF.Exp,
                        bias=SCRF[0:128, 3072 + cc:3073 + cc])
                ctx.ts("vector", SCRB[0:128, 1024:1048], SCRB[0:128, 1024:1048],
                       EPS, None, ALU.add)
                pt2 = P[4]
                ptb = (pt2.h[0:24, 0:64].bitcast(BF16),
                       Span(pt2.name, 0, 24, 0, 128))
                ctx.transpose(ptb, SCRB[0:128, 1024:1048], T["IDB"].full())
                ctx.copy("vector", HA[0:24, cc * 128:(cc + 1) * 128], ptb)
            lf_pass(16384)
            pt3 = P[3]
            ctx.reduce("vector", ALU.max,
                       (SCRF.h[0:24, 2048:2049], Span("SCRF", 0, 24, 8192, 8196)),
                       (SCRF.h[0:24, 0:2048], Span("SCRF", 0, 24, 0, 8192)))
            ctx.transpose(pt3[0:1, 0:24],
                          (SCRF.h[0:24, 2048:2049], Span("SCRF", 0, 24, 8192, 8196)),
                          T["IDF"][0:24, 0:24])
            ctx.reduce("vector", ALU.max, STT[0:1, 512:513], pt3[0:1, 0:24],
                       negate=True)
            pb24 = P[4]
            ctx.mmgroup(pb24[0:24, 0:1], [(onesrf_t[0:1, 0:24], STT[0:1, 512:513])])
            ctx.copy("vector", (SCRF.h[0:24, 2049:2050],
                                Span("SCRF", 0, 24, 8196, 8200)), pb24[0:24, 0:1])
            ctx.act("scalar", (SCRB.h[0:24, 2048:4096],
                               Span("SCRB", 0, 24, 4096, 8192)),
                    (SCRF.h[0:24, 0:2048], Span("SCRF", 0, 24, 0, 8192)),
                    AF.Exp, bias=(SCRF.h[0:24, 2049:2050],
                                  Span("SCRF", 0, 24, 8196, 8200)))
            ctx.ts("vector", HA[0:24, 2048:4096],
                   (SCRB.h[0:24, 2048:4096], Span("SCRB", 0, 24, 4096, 8192)),
                   EPS, None, ALU.add)

            if dbg is not None and "l3i" in dbg and s == 0 and h == 0:
                ctx.dma("gpsimd", dbg["l3i"][0, 0:24, 0:2048], HA[0:24, 0:2048])
                ctx.dma("gpsimd", dbg["l3i"][1, 0:24, 0:2048], HA[0:24, 2048:4096])
            for c in range(16):
                qsl = HA[0:24, c * 128:(c + 1) * 128]
                ksl = HA[0:24, 2048 + c * 128:2048 + (c + 1) * 128]
                ptk = P[6]
                ptkb = (ptk.h[0:128, 0:12].bitcast(BF16),
                        Span(ptk.name, 0, 128, 0, 48))
                ctx.transpose(ptkb, ksl, T["IDB"][0:24, 0:24])
                ctx.copy("vector", HA[0:128, 7424:7448], ptkb)
                if c == 0:
                    ctx.memset("vector", f32vp(HA, 0, 24, 3840, 3905), 0.0)
                else:
                    ctx.copy("scalar", (HA.h[0:24, 7448:7513],
                                        Span("HA", 0, 24, 14896, 15026)),
                             f32vp(HA, 0, 24, 3840, 3905))
                ctx.mmgroup(P[5][0:24, 0:65],
                            [(HA[0:128, 7424:7448],
                              A2.strided(0, 128, c * 1040 + h * 65, 1, 65, 65))])
                ctx.tt("vector", ALU.add, f32vp(HA, 0, 24, 3840, 3905),
                       f32vp(HA, 0, 24, 3840, 3905), P[5][0:24, 0:65])
                ps1 = P[0]
                if c > 0:
                    ctx.mmgroup(ps1[0:128, 0:256],
                                [(qsl, HA[0:24, 2048 + (c - 1) * 128:
                                          2048 + (c + 1) * 128])])
                else:
                    ctx.mmgroup(ps1[0:128, 128:256],
                                [(qsl, HA[0:24, 2048:2176])])
                ps1t = P[1]
                ctx.mmgroup(ps1t[0:128, 0:128], [(ksl, qsl)])
                ctx.tt("vector", ALU.mult, HA[0:128, 6144:6272],
                       ps1t[0:128, 0:128], MLOC[0:128, 1024:1152])
                psl = P[2]
                if c > 0:
                    ctx.mmgroup(psl[0:128, 0:256],
                                [(A1[po:po + 64, ch * S + c * 128:
                                     ch * S + (c + 1) * 128],
                                  A1[po:po + 64, 16384 + ch * S + (c - 1) * 128:
                                     16384 + ch * S + (c + 1) * 128])])
                else:
                    ctx.mmgroup(psl[0:128, 128:256],
                                [(A1[po:po + 64, ch * S:ch * S + 128],
                                  A1[po:po + 64, 16384 + ch * S:
                                     16384 + ch * S + 128])])
                gq = f32v(HA, 3200, 3456)
                if c > 0:
                    ctx.tt("vector", ALU.mult, gq, ps1[0:128, 0:256],
                           MLOC[0:128, 512:768])
                else:
                    ctx.memset("vector", f32v(HA, 3200, 3328), 0.0)
                    ctx.tt("vector", ALU.mult, f32v(HA, 3328, 3456),
                           ps1[0:128, 128:256], MLOC[0:128, 896:1024])
                ctx.reduce("vector", ALU.add, SCRF[0:128, 3104 + c:3105 + c], gq)
                SSL = f32v(HA, 3456, 3712)
                if c > 0:
                    ctx.stt("vector", SSL, psl[0:128, 0:256], SC,
                            MLOC[0:128, 0:256], ALU.mult, ALU.add)
                else:
                    ctx.memset("vector", f32v(HA, 3456, 3584), -1e30)
                    ctx.stt("vector", f32v(HA, 3584, 3712), psl[0:128, 128:256],
                            SC, MLOC[0:128, 384:512], ALU.mult, ALU.add)
                ctx.reduce("vector", ALU.max, SCRF[0:128, 3120 + c:3121 + c],
                           SSL, negate=True)
                ctx.act("scalar", SCRB[0:128, c * 256:(c + 1) * 256],
                        SSL, AF.Exp, bias=SCRF[0:128, 3120 + c:3121 + c],
                        accum_out=SCRF[0:128, 3136 + c:3137 + c])
                pgo = P[3]
                if c > 0:
                    ctx.mmgroup(pgo[0:65, 0:128],
                                [((HA.h[0:24, 7448:7513],
                                   Span("HA", 0, 24, 14896, 15026)), qsl)],
                                stop_group=False)
                    ctx.mmgroup(pgo[0:65, 0:128],
                                [(A2.strided(0, 128, c * 1040 + h * 65, 1, 65, 65),
                                  HA[0:128, 6144:6272])], accum_into=True)
                else:
                    ctx.mmgroup(pgo[0:65, 0:128],
                                [(A2.strided(0, 128, c * 1040 + h * 65, 1, 65, 65),
                                  HA[0:128, 6144:6272])])
                ctx.copy("scalar", (HA.h[0:65, 8192 + c * 128:8192 + (c + 1) * 128],
                                    Span("HA", 0, 65, 16384 + c * 256,
                                         16384 + (c + 1) * 256)),
                         pgo[0:65, 0:128])

            for c in range(16):
                pt = P[0]
                ptb1 = (pt.h[0:128, 0:1].bitcast(BF16)[0:128, 0:1],
                        Span(pt.name, 0, 128, 0, 4))
                ctx.transpose(ptb1,
                              (HA.h[64:65, 8192 + c * 128:8192 + (c + 1) * 128],
                               Span("HA", 64, 65, 16384 + c * 256,
                                    16384 + (c + 1) * 256)),
                              T["IDB"][64:65, 64:65])
                ctx.copy("vector", SCRF[0:128, 3152 + c:3153 + c], ptb1)
            ST = lambda a, b: SCRF[0:128, a:b]
            QSN = ST(3072, 3088)
            GQS = ST(3104, 3120)
            NLM = ST(3120, 3136)
            SM = ST(3136, 3152)
            GOC = ST(3152, 3168)
            GPS = ST(3168, 3184)
            W2_ = ST(3184, 3200)
            W3 = ST(3200, 3216)
            pbneg = P[1]
            ctx.mmgroup(pbneg[0:128, 0:1], [(onesrf_t[0:1, 0:128], STT[0:1, 512:513])])
            ctx.copy("vector", ST(3216, 3217), pbneg[0:128, 0:1])
            ctx.ts("vector", GPS, QSN, ST(3216, 3217), None, ALU.add)
            ctx.ts("vector", GPS, GPS, -1.0, None, ALU.mult)
            ctx.act("scalar", W2_, SM, AF.Ln)
            ctx.tt("vector", ALU.subtract, W2_, W2_, NLM)
            ctx.tt("vector", ALU.subtract, W3, GOC, GQS)
            ctx.ts("vector", W3, W3, 1e-24, None, ALU.max)
            ctx.act("scalar", W3, W3, AF.Ln)
            ctx.tt("vector", ALU.add, W3, W3, GPS)
            MX = ST(3217, 3233)
            ctx.tt("vector", ALU.max, MX, W2_, W3)
            ctx.tt("vector", ALU.subtract, W2_, W2_, MX)
            ctx.tt("vector", ALU.subtract, W3, W3, MX)
            ctx.act("scalar", W2_, W2_, AF.Exp)
            ctx.act("scalar", W3, W3, AF.Exp)
            ctx.tt("vector", ALU.add, W3, W2_, W3)
            ctx.act("scalar", W3, W3, AF.Ln)
            ctx.tt("vector", ALU.add, W3, W3, MX)
            ctx.tt("vector", ALU.subtract, GPS, GPS, W3)
            ctx.act("scalar", GPS, GPS, AF.Exp)
            NG = ST(3233, 3249)
            ctx.ts("vector", NG, GPS, -1.0, None, ALU.mult)
            RSM = ST(3249, 3265)
            ctx.recip(RSM, SM)
            if dbg is not None and "l3i" in dbg and s == 0 and h == 0:
                ctx.dma("gpsimd", dbg["l3i"][2, 0:128, 0:1024],
                        (SCRF.h[0:128, 3072:3584].bitcast(BF16)[0:128, 0:1024],
                         Span("SCRF", 0, 128, 3072 * 4, 3584 * 4)))

            for c in range(16):
                Pn = SCRB[0:128, c * 256:(c + 1) * 256]
                ctx.ts("vector", Pn, Pn, SCRF[0:128, 3249 + c:3250 + c],
                       None, ALU.mult)
                ps1 = P[0]
                if c > 0:
                    ctx.mmgroup(ps1[0:128, 0:256],
                                [(HA[0:24, c * 128:(c + 1) * 128],
                                  HA[0:24, 2048 + (c - 1) * 128:
                                     2048 + (c + 1) * 128])])
                    ctx.tt("vector", ALU.mult, f32v(HA, 3200, 3456),
                           ps1[0:128, 0:256], MLOC[0:128, 512:768])
                else:
                    ctx.memset("vector", f32v(HA, 3200, 3328), 0.0)
                    ctx.mmgroup(ps1[0:128, 128:256],
                                [(HA[0:24, 0:128], HA[0:24, 2048:2176])])
                    ctx.tt("vector", ALU.mult, f32v(HA, 3328, 3456),
                           ps1[0:128, 128:256], MLOC[0:128, 896:1024])
                ctx.stt("vector", Pn, f32v(HA, 3200, 3456),
                        SCRF[0:128, 3233 + c:3234 + c], Pn, ALU.mult, ALU.add)
                pav = P[2]
                first = True
                for kb in range(2):
                    if c == 0 and kb == 0:
                        continue
                    ptp = P[6]
                    ptb = (ptp.h[0:128, 0:64].bitcast(BF16),
                           Span(ptp.name, 0, 128, 0, 256))
                    ctx.transpose(ptb,
                                  SCRB[0:128, c * 256 + kb * 128:
                                       c * 256 + (kb + 1) * 128],
                                  T["IDB"].full())
                    ctx.copy("vector", HA[0:128, 7520:7648], ptb)
                    kc = c - 1 + kb
                    ctx.mmgroup(pav[0:64, 0:128],
                                [(A2.strided(0, 128, kc * 1040 + h * 65, 1, 65, 64),
                                  HA[0:128, 7520:7648])],
                                accum_into=(not first), stop_group=(kb == 1))
                    first = False
                ptg2 = P[1]
                ctx.transpose(ptg2[0:1, 0:128],
                              (SCRF.h[0:128, 3168 + c:3169 + c],
                               Span("SCRF", 0, 128, (3168 + c) * 4, (3169 + c) * 4)),
                              T["IDF"].full())
                ctx.copy("vector", STT[0:1, 768:896], ptg2[0:1, 0:128])
                prr = P[4]
                ctx.mmgroup(prr[0:64, 0:128],
                            [(onesrf_t[0:1, 0:64], STT[0:1, 768:896])])
                ctx.tt("vector", ALU.mult,
                       (SCRF.h[0:64, 1024:1152], Span("SCRF", 0, 64, 4096, 4608)),
                       (HA.h[0:64, 8192 + c * 128:8192 + (c + 1) * 128],
                        Span("HA", 0, 64, 16384 + c * 256, 16384 + (c + 1) * 256)),
                       prr[0:64, 0:128])
                ctx.tt("vector", ALU.add,
                       (HA.h[po:po + 64, 7938:8066],
                        Span("HA", po, po + 64, 15876, 16132)),
                       pav[0:64, 0:128],
                       (SCRF.h[0:64, 1024:1152], Span("SCRF", 0, 64, 4096, 4608)))
                ctx.dma("sync", ctx_d[s, ch, po:po + 64, c * 128:(c + 1) * 128],
                        (HA.h[po:po + 64, 7938:8066],
                         Span("HA", po, po + 64, 15876, 16132)))

        for c in range(DC):
            ctx.dma("gpsimd", A2[0:128, 16640 + c * 1024:16640 + (c + 1) * 1024],
                    H["l3_outwT"][c * 128:(c + 1) * 128, :])
        ctx.dma("sync", T["B1S"][0:128, 24:32], H["l3_outb"][:, :])
        for nb in range(S // 512):
            t0 = nb * 512
            ctx.dma("sync", XNT.strided(0, 128, 0, DC, 512, 512),
                    d4T(ctx_d, s, t0, t0 + 512))
            for m in range(DC):
                pb = P[m % 2]
                ctx.mmgroup(pb[0:128, 0:512],
                            [(A2[0:128, 16640 + c * 1024 + m * 128:
                                 16640 + c * 1024 + (m + 1) * 128],
                              XNT[0:128, c * 512:(c + 1) * 512])
                             for c in range(DC)])
                ctx.ts("vector", SCRF[0:128, (m % 4) * 512:(m % 4 + 1) * 512],
                       pb[0:128, 0:512], T["B1S"][0:128, 24 + m:25 + m],
                       None, ALU.add)
                ctx.dma("gpsimd", a_d[s, m, :, t0:t0 + 512],
                        SCRF[0:128, (m % 4) * 512:(m % 4 + 1) * 512])
    if dbg is not None and "ctx" in dbg:
        for s in range(BS):
            for c in range(DC):
                ctx.dma("gpsimd", dbg["ctx"][s, c, :, :], ctx_d[s, c, :, :])
    if dbg is not None and "a" in dbg:
        for s in range(BS):
            for c in range(DC):
                ctx.dma("gpsimd", dbg["a"][s, c, :, :], a_d[s, c, :, :])

    for c in range(DC):
        ctx.dma("gpsimd", A2[0:128, c * 16:(c + 1) * 16],
                H["l3_rwT"][c * 128:(c + 1) * 128, :])
    for s in range(BS):
        for tt in range(16):
            t0 = tt * 128
            ctx.dma("sync", XNT.strided(0, 128, 0, DC, 128, 128),
                    d4T(xn_d, s, t0, t0 + 128))
            pb = P[tt % 2]
            ctx.mmgroup(pb[0:128, 0:16],
                        [(XNT[0:128, c * 128:(c + 1) * 128],
                          A2[0:128, c * 16:(c + 1) * 16]) for c in range(DC)])
            lg = SCRF[0:128, 0:16]
            mx = SCRF[0:128, 16:17]
            gt = SCRF[0:128, 32:48]
            ctx.copy("vector", lg, pb[0:128, 0:16])
            ctx.reduce("vector", ALU.max, mx, lg)
            ctx.act("scalar", gt, lg, AF.Sigmoid)
            ctx.ts("vector", lg, lg, mx, None, ALU.is_ge)
            ctx.tt("vector", ALU.mult, gt, gt, lg)
            ctx.act("scalar", SCRB[0:128, 0:16], gt, AF.Copy)
            ptg = P[2]
            pgb = (ptg.h[0:16, 0:64].bitcast(BF16), Span(ptg.name, 0, 16, 0, 128))
            ctx.transpose(pgb, SCRB[0:128, 0:16], T["IDB"].full())
            ctx.copy("vector", SCRF[0:16, 1024:1152], pgb)
            ctx.dma("gpsimd", gates_d[s, :, t0:t0 + 128], SCRF[0:16, 1024:1152])

    for e in range(16):
        for c in range(DC):
            ctx.dma("gpsimd", A1[0:128, c * 1024:(c + 1) * 1024],
                    H["l3_gug"][e, c * 128:(c + 1) * 128, :])
            ctx.dma("gpsimd", A1[0:128, 8192 + c * 1024:8192 + (c + 1) * 1024],
                    H["l3_guu"][e, c * 128:(c + 1) * 128, :])
            ctx.dma("gpsimd", A1[0:128, 16384 + c * 1024:16384 + (c + 1) * 1024],
                    H["l3_down"][e, c * 128:(c + 1) * 128, :])
        for s in range(BS):
            for nb in range(S // 512):
                t0 = nb * 512
                load_xn_tile(ctx, T, xn_d, s, t0, 512)
                ctx.dma("sync", STT[0:1, 0:512],
                        (gates_d.h[s, e, t0:t0 + 512].unsqueeze(0),
                         Span(gates_d.name, 0, 1, 0, 1 << 62)))
                psig = P[6]
                ctx.mmgroup(psig[0:128, 0:512],
                            [(onesrf_t[0:1, 0:128], STT[0:1, 0:512])])
                ctx.copy("vector", SCRF[0:128, 3072:3584], psig[0:128, 0:512])
                for m in range(DC):
                    pg = P[0]
                    pu = P[1]
                    ctx.mmgroup(pg[0:128, 0:512],
                                [(A1[0:128, c * 1024 + m * 128:c * 1024 + (m + 1) * 128],
                                  XNT[0:128, c * 512:(c + 1) * 512])
                                 for c in range(DC)])
                    ctx.mmgroup(pu[0:128, 0:512],
                                [(A1[0:128, 8192 + c * 1024 + m * 128:
                                     8192 + c * 1024 + (m + 1) * 128],
                                  XNT[0:128, c * 512:(c + 1) * 512])
                                 for c in range(DC)])
                    ctx.tt("vector", ALU.mult, SCRF[0:128, 0:512],
                           pg[0:128, 0:512], SCRF[0:128, 3072:3584])
                    ctx.act("scalar", SCRF[0:128, 512:1024], SCRF[0:128, 0:512],
                            AF.Silu)
                    ctx.tt("vector", ALU.mult, SCRF[0:128, 0:512],
                           pu[0:128, 0:512], SCRF[0:128, 3072:3584])
                    ctx.tt("vector", ALU.mult, HA[0:128, m * 512:(m + 1) * 512],
                           SCRF[0:128, 0:512], SCRF[0:128, 512:1024])
                for st in range(4):
                    tt = nb * 4 + st
                    for half in range(2):
                        pb = P[2 + (st * 2 + half) % 2]
                        ctx.mmgroup(pb[0:128, 0:512],
                                    [(HA[0:128, k2 * 512 + st * 128:
                                         k2 * 512 + (st + 1) * 128],
                                      A1[0:128, 16384 + k2 * 1024 + half * 512:
                                         16384 + k2 * 1024 + (half + 1) * 512])
                                     for k2 in range(DC)])
                        tb = SCRF[0:128, 1024 + (st % 2) * 1024 + half * 512:
                                  1024 + (st % 2) * 1024 + half * 512 + 512]
                        ctx.copy("vector", tb, pb[0:128, 0:512])
                        ctx.dma("gpsimd",
                                y_d[s, tt, :, half * 512:(half + 1) * 512],
                                tb, accum=(e > 0))

    for fh in range(2):
        for c in range(DC):
            ctx.dma("gpsimd", A1[0:128, c * 2048:(c + 1) * 2048],
                    (H["l3_sgwT"].h[c * 128:(c + 1) * 128,
                                    fh * 2048:(fh + 1) * 2048],
                     Span(H["l3_sgwT"].name, 0, 1, 0, 1 << 62)))
            ctx.dma("gpsimd", A1[0:128, 16384 + c * 2048:16384 + (c + 1) * 2048],
                    (H["l3_suwT"].h[c * 128:(c + 1) * 128,
                                    fh * 2048:(fh + 1) * 2048],
                     Span(H["l3_suwT"].name, 0, 1, 0, 1 << 62)))
        for c2 in range(16):
            ctx.dma("gpsimd", A2[0:128, c2 * 1024:(c2 + 1) * 1024],
                    (H["l3_sdwT"].h[fh * 2048 + c2 * 128:fh * 2048 + (c2 + 1) * 128, :],
                     Span(H["l3_sdwT"].name, 0, 1, 0, 1 << 62)))
        for s in range(BS):
            for nb in range(S // 512):
                t0 = nb * 512
                load_xn_tile(ctx, T, xn_d, s, t0, 512)
                for m in range(16):
                    pg = P[0]
                    pu = P[1]
                    ctx.mmgroup(pg[0:128, 0:512],
                                [(A1[0:128, c * 2048 + m * 128:c * 2048 + (m + 1) * 128],
                                  XNT[0:128, c * 512:(c + 1) * 512])
                                 for c in range(DC)])
                    ctx.mmgroup(pu[0:128, 0:512],
                                [(A1[0:128, 16384 + c * 2048 + m * 128:
                                     16384 + c * 2048 + (m + 1) * 128],
                                  XNT[0:128, c * 512:(c + 1) * 512])
                                 for c in range(DC)])
                    ctx.act("scalar", SCRF[0:128, 0:512], pg[0:128, 0:512], AF.Silu)
                    ctx.tt("vector", ALU.mult, HA[0:128, m * 512:(m + 1) * 512],
                           SCRF[0:128, 0:512], pu[0:128, 0:512])
                for st in range(4):
                    tt = nb * 4 + st
                    for half in range(2):
                        pb = P[2 + (st * 2 + half) % 2]
                        ctx.mmgroup(pb[0:128, 0:512],
                                    [(HA[0:128, k2 * 512 + st * 128:
                                         k2 * 512 + (st + 1) * 128],
                                      A2[0:128, k2 * 1024 + half * 512:
                                         k2 * 1024 + (half + 1) * 512])
                                     for k2 in range(16)])
                        tb = SCRF[0:128, 1024 + (st % 2) * 1024 + half * 512:
                                  1024 + (st % 2) * 1024 + half * 512 + 512]
                        ctx.copy("vector", tb, pb[0:128, 0:512])
                        ctx.dma("gpsimd",
                                y_d[s, tt, :, half * 512:(half + 1) * 512],
                                tb, accum=True)
    if dbg is not None and "y" in dbg:
        for s in range(BS):
            for tt in range(16):
                ctx.dma("gpsimd", dbg["y"][s, tt, :, :], y_d[s, tt, :, :])

    emit_combine(ctx, T, x_in, x_out, a_d, y_d, bo=None)


def ctx_onesb_slice(T, po):
    t = T["_onesb_t"]
    return t[po:po + 64, 0:1]


def all_consts():
    return {**host_constants(), **l1_rotary_consts(), **l3_mask_consts()}


PREPS = {0: prep_l0, 1: prep_l1, 2: prep_l2, 3: prep_l3}


def kernel(**inputs):
    x = np.asarray(inputs["L0_attn_x"], dtype=np.float32)
    consts = all_consts()
    shards = [to_layoutB(x[2 * i:2 * i + 2]) for i in range(NCORES)]
    for layer in range(4):
        weights = PREPS[layer](inputs)
        nc = build_layer_program(layer, weights, consts)
        in_maps = [{"x_in": shards[i], **weights, **consts}
                   for i in range(NCORES)]
        res = run_bass_kernel_spmd(nc, in_maps, core_ids=list(range(NCORES)))
        shards = [np.asarray(res.results[i]["x_out"]) for i in range(NCORES)]
    return np.concatenate([from_layoutB(s) for s in shards], axis=0)


# version 03
# speedup vs baseline: 1.0121x; 1.0121x over previous
import math
from contextlib import ExitStack

import numpy as np

import concourse.bass as bass
import concourse.mybir as mybir
from concourse.bass_utils import run_bass_kernel_spmd

F32 = mybir.dt.float32
BF16 = mybir.dt.bfloat16
AF = mybir.ActivationFunctionType
ALU = mybir.AluOpType
AX = mybir.AxisListType

ENGINES = ("sync", "tensor", "vector", "scalar", "gpsimd")

NCORES = 8
S = 2048
BS = 2
D = 1024
DC = D // 128



class Span:
    __slots__ = ("name", "p0", "p1", "b0", "b1")

    def __init__(self, name, p0, p1, b0, b1):
        self.name, self.p0, self.p1, self.b0, self.b1 = name, p0, p1, b0, b1

    def overlaps(self, o):
        return (self.name == o.name and self.p0 < o.p1 and o.p0 < self.p1
                and self.b0 < o.b1 and o.b0 < self.b1)

    def contains(self, o):
        return (self.name == o.name and self.p0 <= o.p0 and o.p1 <= self.p1
                and self.b0 <= o.b0 and o.b1 <= self.b1)

    def __repr__(self):
        return f"Span({self.name},p{self.p0}:{self.p1},b{self.b0}:{self.b1})"


class Tile:

    def __init__(self, handle, name, shape, dtype):
        self.h = handle
        self.name = name
        self.shape = tuple(shape)
        self.dtype = dtype
        self.esz = 2 if dtype == BF16 else 4

    def __getitem__(self, key):
        ps, fs = key
        if isinstance(ps, int):
            ps = slice(ps, ps + 1)
        if isinstance(fs, int):
            fs = slice(fs, fs + 1)
        p0 = ps.start or 0
        p1 = ps.stop if ps.stop is not None else self.shape[0]
        f0 = fs.start or 0
        f1 = fs.stop if fs.stop is not None else self.shape[1]
        ap = self.h[p0:p1, f0:f1]
        return ap, Span(self.name, p0, p1, f0 * self.esz, f1 * self.esz)

    def full(self):
        return self[0:self.shape[0], 0:self.shape[1]]

    def strided(self, p0, p1, f0, nseg, segstride, seglen):
        ap = bass.AP(self.h, p0 * self.shape[1] + f0,
                     [[self.shape[1], p1 - p0], [segstride, nseg], [1, seglen]])
        span = Span(self.name, p0, p1, f0 * self.esz,
                    (f0 + (nseg - 1) * segstride + seglen) * self.esz)
        return ap, span


class DTen:

    def __init__(self, handle, name):
        self.h = handle
        self.name = "D:" + name

    def __getitem__(self, key):
        return self.h[key], Span(self.name, 0, 1, 0, 1 << 62)


class Ctx:
    def __init__(self, nc):
        self.nc = nc
        self.rec = []
        self.writers = {}
        self.readers = {}
        self.n_compute = {e: 0 for e in ENGINES}
        self.n_dma = {e: 0 for e in ENGINES}
        self.DMA_POOL = 8
        self.dma_uses = {e: [0] * self.DMA_POOL for e in ENGINES}
        self.sems = {}
        self.dma_sems = {}
        self.es = None
        self.out_dma_events = []
        self.sbuf_bytes = 0

    def sbuf(self, name, shape, dtype):
        h = self.es.enter_context(self.nc.sbuf_tensor(name, list(shape), dtype))
        self.sbuf_bytes += shape[1] * (2 if dtype == BF16 else 4)
        return Tile(h, name, shape, dtype)

    def psum(self, name, shape, dtype):
        h = self.es.enter_context(self.nc.psum_tensor(name, list(shape), dtype))
        return Tile(h, name, shape, dtype)

    def dram(self, name, shape, dtype, kind="Internal"):
        h = self.nc.dram_tensor(name, list(shape), dtype, kind=kind)
        return DTen(h, name)

    def op(self, engine, emit, reads=(), writes=(), kind="compute",
           is_output=False):
        waits = self._waits(engine, reads, writes)
        if kind == "dma":
            slot = self.n_dma[engine] % self.DMA_POOL
            self.dma_uses[engine][slot] += 1
            uses = self.dma_uses[engine][slot]
            if uses > 1:
                key = ("dma", engine, slot)
                if waits.get(key, 0) < 16 * (uses - 1):
                    waits[key] = 16 * (uses - 1)
            self.n_dma[engine] += 1
            ev = (engine, "dma", slot, uses)
        else:
            self.n_compute[engine] += 1
            ev = (engine, "cmp", self.n_compute[engine])
        self.rec.append((engine, emit, waits, ev, kind))
        if is_output:
            self.out_dma_events.append(ev)
        for s in writes:
            lst = self.writers.setdefault(s.name, [])
            lst[:] = [(os, oev) for (os, oev) in lst if not s.contains(os)]
            lst.append((s, ev))
            rl = self.readers.get(s.name)
            if rl:
                rl[:] = [(os, oev) for (os, oev) in rl if not s.contains(os)]
        for s in reads:
            self.readers.setdefault(s.name, []).append((s, ev))
        return ev

    def _waits(self, engine, reads, writes):
        need = {}

        def add(ev):
            if ev[1] == "dma":
                e, _k, slot, uses = ev
                key = ("dma", e, slot)
                val = uses * 16
            else:
                e, k, n = ev
                if e == engine and engine == "tensor":
                    return
                key = ("cmp", e)
                val = n
            if need.get(key, 0) < val:
                need[key] = val

        for s in reads:
            for (ws, ev) in self.writers.get(s.name, ()):
                if ws.overlaps(s):
                    add(ev)
        for s in writes:
            for (ws, ev) in self.writers.get(s.name, ()):
                if ws.overlaps(s):
                    add(ev)
            for (rs, ev) in self.readers.get(s.name, ()):
                if rs.overlaps(s):
                    add(ev)
        return need

    def emit(self):
        for e in ENGINES:
            self.sems[e] = self.es.enter_context(self.nc.semaphore(f"s_{e}"))
        for e in ENGINES:
            if self.n_dma[e] > 0:
                self.dma_sems[e] = [
                    self.es.enter_context(self.nc.semaphore(f"d_{e}{i}"))
                    for i in range(self.DMA_POOL)]
        per = {e: [] for e in ENGINES}
        for r in self.rec:
            per[r[0]].append(r)
        block = self.es.enter_context(self.nc.Block())
        sems, dsems = self.sems, self.dma_sems
        final_waits = {}
        for e in ENGINES:
            for slot in range(self.DMA_POOL):
                if self.dma_uses[e][slot] > 0:
                    final_waits[("dma", e, slot)] = 16 * self.dma_uses[e][slot]

        def make_body(ename):
            ops = per[ename]

            def body(eng):
                wm = {}
                for (_e, emit, waits, ev, kind) in ops:
                    for key in sorted(waits, key=repr):
                        val = waits[key]
                        if wm.get(key, 0) >= val:
                            continue
                        if key[0] == "cmp":
                            eng.wait_ge(sems[key[1]], val)
                        else:
                            eng.wait_ge(dsems[key[1]][key[2]], val)
                        wm[key] = val
                    inst = emit(eng)
                    if ev[1] == "dma":
                        inst.then_inc(dsems[ev[0]][ev[2]], 16)
                    else:
                        inst.then_inc(sems[ev[0]], 1)
                if ename == "gpsimd":
                    for key, val in sorted(final_waits.items(), key=repr):
                        eng.wait_ge(dsems[key[1]][key[2]], val)
            return body

        for e in ENGINES:
            if per[e] or e == "gpsimd":
                getattr(block, e)(make_body(e))

    def dma(self, q, out, in_, is_output=False, accum=False):
        oap, osp = out
        iap, isp = in_

        def em(eng, oap=oap, iap=iap, accum=accum):
            if accum:
                return eng.dma_start(out=oap, in_=iap, accum_op=ALU.add)
            return eng.dma_start(out=oap, in_=iap)
        return self.op(q, em, reads=[isp], writes=[osp], kind="dma",
                       is_output=is_output)

    def mmgroup(self, psum_out, pairs, accum_into=False, stop_group=True):
        oap, osp = psum_out
        reads = [sp for (l, r) in pairs for sp in (l[1], r[1])]
        aps = [(l[0], r[0]) for (l, r) in pairs]

        def em(eng, aps=aps, oap=oap, accum_into=accum_into,
               stop_group=stop_group):
            n = len(aps)
            inst = None
            for i, (la, ra) in enumerate(aps):
                inst = eng.matmul(oap, la, ra,
                                  start=(i == 0 and not accum_into),
                                  stop=(i == n - 1 and stop_group))
            return inst
        return self.op("tensor", em, reads=reads, writes=[osp])

    def transpose(self, psum_out, in_, ident):
        oap, osp = psum_out
        iap, isp = in_
        idap, idsp = ident

        def em(eng, oap=oap, iap=iap, idap=idap):
            return eng.transpose(oap, iap, idap)
        return self.op("tensor", em, reads=[isp, idsp], writes=[osp])

    def act(self, eng, out, in_, func, bias=0.0, scale=1.0, accum_out=None):
        oap, osp = out
        iap, isp = in_
        reads = [isp]
        writes = [osp]
        if isinstance(bias, tuple):
            reads.append(bias[1])
            bias = bias[0]
        if isinstance(scale, tuple):
            reads.append(scale[1])
            scale = scale[0]
        acc = None
        if accum_out is not None:
            acc = accum_out[0]
            writes.append(accum_out[1])

        def em(e, oap=oap, iap=iap, func=func, bias=bias, scale=scale, acc=acc):
            if acc is not None:
                return e.activation(oap, iap, func, bias=bias, scale=scale,
                                    accum_out=acc)
            return e.activation(oap, iap, func, bias=bias, scale=scale)
        return self.op(eng, em, reads=reads, writes=writes)

    def tt(self, eng, op, out, in0, in1):
        oap, osp = out
        a0, s0 = in0
        a1, s1 = in1

        def em(e, oap=oap, a0=a0, a1=a1, op=op):
            return e.tensor_tensor(oap, a0, a1, op)
        return self.op(eng, em, reads=[s0, s1], writes=[osp])

    def ts(self, eng, out, in0, s1, s2, op0, op1=ALU.bypass, accum_out=None):
        oap, osp = out
        a0, sp0 = in0
        reads = [sp0]
        if isinstance(s1, tuple):
            reads.append(s1[1])
            s1 = s1[0]
        if isinstance(s2, tuple):
            reads.append(s2[1])
            s2 = s2[0]
        writes = [osp]
        acc = None
        if accum_out is not None:
            acc = accum_out[0]
            writes.append(accum_out[1])

        def em(e, oap=oap, a0=a0, s1=s1, s2=s2, op0=op0, op1=op1, acc=acc):
            return e.tensor_scalar(oap, a0, s1, s2, op0, op1, accum_out=acc)
        return self.op(eng, em, reads=reads, writes=writes)

    def stt(self, eng, out, in0, scalar, in1, op0, op1):
        oap, osp = out
        a0, sp0 = in0
        a1, sp1 = in1
        reads = [sp0, sp1]
        if isinstance(scalar, tuple):
            reads.append(scalar[1])
            scalar = scalar[0]

        def em(e, oap=oap, a0=a0, a1=a1, scalar=scalar, op0=op0, op1=op1):
            return e.scalar_tensor_tensor(oap, a0, scalar, a1, op0, op1)
        return self.op(eng, em, reads=reads, writes=[osp])

    def copy(self, eng, out, in_):
        oap, osp = out
        iap, isp = in_

        def em(e, oap=oap, iap=iap, eng=eng):
            return e.copy(oap, iap) if eng == "scalar" else e.tensor_copy(oap, iap)
        return self.op(eng, em, reads=[isp], writes=[osp])

    def reduce(self, eng, op, out, in_, negate=False):
        oap, osp = out
        iap, isp = in_

        def em(e, oap=oap, iap=iap, op=op, negate=negate):
            return e.tensor_reduce(oap, iap, AX.X, op, negate=negate)
        return self.op(eng, em, reads=[isp], writes=[osp])

    def scan(self, out, d0, d1, initial, op0=ALU.mult, op1=ALU.add):
        oap, osp = out
        a0, s0 = d0
        a1, s1 = d1
        reads = [s0, s1]
        if isinstance(initial, tuple):
            reads.append(initial[1])
            initial = initial[0]

        def em(e, oap=oap, a0=a0, a1=a1, initial=initial, op0=op0, op1=op1):
            return e.tensor_tensor_scan(oap, a0, a1, initial, op0, op1)
        return self.op("vector", em, reads=reads, writes=[osp])

    def recip(self, out, in_):
        oap, osp = out
        iap, isp = in_

        def em(e, oap=oap, iap=iap):
            return e.reciprocal(oap, iap)
        return self.op("vector", em, reads=[isp], writes=[osp])

    def memset(self, eng, out, val):
        oap, osp = out

        def em(e, oap=oap, val=val):
            return e.memset(oap, val)
        return self.op(eng, em, reads=[], writes=[osp])



def to_layoutB(x):
    b, t, d = x.shape
    return np.ascontiguousarray(x.transpose(0, 2, 1).reshape(b, d // 128, 128, t))


def from_layoutB(xb):
    b, dc, p, t = xb.shape
    return np.ascontiguousarray(xb.reshape(b, dc * p, t).transpose(0, 2, 1))


def bias_chunks(v):
    n = v.shape[0] // 128
    return np.ascontiguousarray(v.reshape(n, 128).T)


def host_constants():
    c = {}
    c["c_identf"] = np.eye(128, dtype=np.float32)
    masks = np.zeros((128, 4 * 512), np.float32)
    for o in range(4):
        k_idx = np.arange(128)[:, None] + o * 128
        q_idx = np.arange(512)[None, :]
        masks[:, o * 512:(o + 1) * 512] = np.where(k_idx > q_idx, -1.0e6, 0.0)
    c["c_causal"] = masks
    return c



def d4T(dten, s, t0, t1):
    ap = dten.h[s, :, :, t0:t1].transpose([1, 0, 2])
    return ap, Span(dten.name, 0, 1, 0, 1 << 62)


def emit_layernorm_to_dram(ctx, T, x_in, xn_d):
    XF, SQ, STT = T["SCRF"], T["SCRB"], T["STT"]
    onesb = T["ONESBC"]
    onesrf = T["ONESRF"]
    P = T["PS"]
    for s in range(BS):
        for nb in range(S // 512):
            t0 = nb * 512
            for c in range(DC):
                ctx.dma("sync", XF[0:128, c * 512:(c + 1) * 512],
                        x_in[s, c, :, t0:t0 + 512])
            for c in range(DC):
                ctx.act("scalar", SQ[0:128, c * 512:(c + 1) * 512],
                        XF[0:128, c * 512:(c + 1) * 512], AF.Copy)
            ctx.mmgroup(P[0][0:1, 0:512],
                        [(onesb, SQ[0:128, c * 512:(c + 1) * 512])
                         for c in range(DC)])
            for c in range(DC):
                ctx.tt("vector", ALU.mult, SQ[0:128, c * 512:(c + 1) * 512],
                       SQ[0:128, c * 512:(c + 1) * 512],
                       SQ[0:128, c * 512:(c + 1) * 512])
            ctx.mmgroup(P[1][0:1, 0:512],
                        [(onesb, SQ[0:128, c * 512:(c + 1) * 512])
                         for c in range(DC)])
            f = lambda i: STT[0:1, i * 512:(i + 1) * 512]
            ctx.act("scalar", f(0), P[0][0:1, 0:512], AF.Copy, scale=1.0 / D)
            ctx.act("scalar", f(1), P[1][0:1, 0:512], AF.Copy, scale=1.0 / D)
            ctx.act("scalar", f(2), f(0), AF.Square)
            ctx.tt("vector", ALU.subtract, f(1), f(1), f(2))
            ctx.ts("vector", f(1), f(1), 1e-5, None, ALU.add)
            ctx.recip(f(2), f(1))
            ctx.act("scalar", f(1), f(2), AF.Sqrt)
            ctx.tt("vector", ALU.mult, f(3), f(0), f(1))
            ctx.mmgroup(P[2][0:128, 0:512], [(onesrf, f(1))])
            ctx.mmgroup(P[3][0:128, 0:512], [(onesrf, f(3))])
            for c in range(DC):
                ctx.tt("vector", ALU.mult, XF[0:128, c * 512:(c + 1) * 512],
                       XF[0:128, c * 512:(c + 1) * 512], P[2][0:128, 0:512])
            for c in range(DC):
                ctx.tt("vector", ALU.subtract, SQ[0:128, c * 512:(c + 1) * 512],
                       XF[0:128, c * 512:(c + 1) * 512], P[3][0:128, 0:512])
            for c in range(DC):
                ctx.dma("sync", xn_d[s, c, :, t0:t0 + 512],
                        SQ[0:128, c * 512:(c + 1) * 512])


def load_xn_tile(ctx, T, xn_d, s, t0, tlen, q="sync"):
    XNT = T["XNT"]
    ctx.dma(q, XNT.strided(0, 128, 0, DC, tlen, tlen), d4T(xn_d, s, t0, t0 + tlen))
    return XNT


def alloc_common(ctx):
    T = {}
    T["A1"] = ctx.sbuf("A1", [128, 32768], BF16)
    T["A2"] = ctx.sbuf("A2", [128, 32768], BF16)
    T["HA"] = ctx.sbuf("HA", [128, 16384], BF16)
    T["XNT"] = ctx.sbuf("XNT", [128, 4096], BF16)
    T["SCRF"] = ctx.sbuf("SCRF", [128, 4096], F32)
    T["SCRB"] = ctx.sbuf("SCRB", [128, 4096], BF16)
    T["MASK"] = ctx.sbuf("MASKT", [128, 2048], BF16)
    T["IDF"] = ctx.sbuf("IDF", [128, 128], F32)
    T["IDB"] = ctx.sbuf("IDB", [128, 128], BF16)
    T["SELB"] = ctx.sbuf("SELB", [128, 2], BF16)
    T["STT"] = ctx.sbuf("STT", [1, 2048], F32)
    T["GAT"] = ctx.sbuf("GAT", [128, 256], F32)
    T["B1S"] = ctx.sbuf("B1S", [128, 32], F32)
    T["RBS"] = ctx.sbuf("RBS", [128, 8], F32)
    T["BOS"] = ctx.sbuf("BOS", [128, 8], F32)
    onesb_t = ctx.sbuf("ONESBC", [128, 1], BF16)
    onesrf_t = ctx.sbuf("ONESRF", [1, 128], F32)
    T["PS"] = [ctx.psum(f"PSB{i}", [128, 512], F32) for i in range(8)]
    ctx.memset("vector", onesb_t.full(), 1.0)
    ctx.memset("vector", onesrf_t.full(), 1.0)
    T["ONESBC"] = onesb_t.full()
    T["ONESRF"] = onesrf_t.full()
    T["_onesrf_t"] = onesrf_t
    T["_onesb_t"] = onesb_t
    return T



def prep_l0(inputs):
    d = {}
    d["l0_wq"] = inputs["L0_attn_Wq"]
    d["l0_wk"] = inputs["L0_attn_Wk"]
    d["l0_wv"] = inputs["L0_attn_Wv"]
    wo = inputs["L0_attn_Wo"]
    d["l0_wo_rep"] = np.ascontiguousarray(
        wo.reshape(16, 64, 1024).transpose(1, 0, 2)[
            np.tile(np.arange(64), 2)])
    d["l0_bo"] = bias_chunks(inputs["L0_attn_bo"])
    d["l0_rw"] = inputs["L0_ffn_router_W"]
    d["l0_rb"] = np.ascontiguousarray(inputs["L0_ffn_router_b"].reshape(1, 8))
    d["l0_w1"] = inputs["L0_ffn_W1"]
    d["l0_b1"] = np.ascontiguousarray(
        np.stack([bias_chunks(b) for b in inputs["L0_ffn_b1"]]))
    d["l0_w2"] = inputs["L0_ffn_W2"]
    d["l0_b2"] = np.ascontiguousarray(inputs["L0_ffn_b2"]).reshape(8, 1, 1024)
    return d


def emit_l0(ctx, T, H, x_in, x_out, dbg=None):
    xn_d = ctx.dram("l0xn_d", [BS, DC, 128, S], BF16)
    a_d = ctx.dram("l0a_d", [BS, DC, 128, S], F32)
    y_d = ctx.dram("l0y_d", [BS, S // 128, 128, D], F32)
    ctx_d = ctx.dram("l0ctx_d", [BS, DC, 128, S], BF16)

    A1, A2, HA, XNT = T["A1"], T["A2"], T["HA"], T["XNT"]
    SCRF, SCRB = T["SCRF"], T["SCRB"]
    P = T["PS"]
    onesrf_t = T["_onesrf_t"]
    WOFF = 16640

    ctx.dma("sync", T["IDF"].full(), H["c_identf"][:, :])
    ctx.dma("gpsimd", T["MASK"].full(), H["c_causal"][:, :])
    ctx.dma("sync", T["BOS"].full(), H["l0_bo"][:, :])

    emit_layernorm_to_dram(ctx, T, x_in, xn_d)
    if dbg is not None and "xn" in dbg:
        for s in range(BS):
            for c in range(DC):
                ctx.dma("gpsimd", dbg["xn"][s, c, :, :], xn_d[s, c, :, :])

    NH = 16
    for s in range(BS):
        for (wkey, kind) in (("l0_wk", "K"), ("l0_wq", "Q"), ("l0_wv", "V")):
            for c in range(DC):
                ctx.dma("gpsimd", A2[0:128, WOFF + c * 1024:WOFF + (c + 1) * 1024],
                        H[wkey][c * 128:(c + 1) * 128, :])
            for nb in range(S // 512):
                t0 = nb * 512
                load_xn_tile(ctx, T, xn_d, s, t0, 512)
                if kind in ("K", "Q"):
                    base = 0 if kind == "K" else 16384
                    for m in range(DC):
                        pb = P[m % 2]
                        ctx.mmgroup(pb[0:128, 0:512],
                                    [(A2[0:128, WOFF + c * 1024 + m * 128:
                                         WOFF + c * 1024 + (m + 1) * 128],
                                      XNT[0:128, c * 512:(c + 1) * 512])
                                     for c in range(DC)])
                        ctx.act("scalar", A1[0:128, base + m * S + t0:
                                             base + m * S + t0 + 512],
                                pb[0:128, 0:512], AF.Copy)
                else:
                    for st in range(4):
                        tt = nb * 4 + st
                        for half in range(2):
                            pb = P[2 + (st * 2 + half) % 2]
                            ctx.mmgroup(pb[0:128, 0:512],
                                        [(XNT[0:128, c * 512 + st * 128:
                                              c * 512 + (st + 1) * 128],
                                          A2[0:128, WOFF + c * 1024 + half * 512:
                                             WOFF + c * 1024 + (half + 1) * 512])
                                         for c in range(DC)])
                            ctx.copy("vector",
                                     A2.strided(0, 128, tt * 1040 + half * 8 * 65,
                                                8, 65, 64),
                                     pb[0:128, 0:512])
        ctx.memset("vector", A2.strided(0, 128, 64, 16 * 16, 65, 1), 1.0)

        for qb in range(4):
            for h in range(NH):
                po = 64 * (h % 2)
                ch = h // 2
                nkt = (qb + 1) * 4
                qsl = A1[po:po + 64, 16384 + ch * S + qb * 512:
                         16384 + ch * S + (qb + 1) * 512]
                for kt in range(nkt):
                    pb = P[4 + kt % 2]
                    ctx.mmgroup(pb[0:128, 0:512],
                                [(A1[po:po + 64, ch * S + kt * 128:
                                     ch * S + (kt + 1) * 128], qsl)])
                    if kt >= qb * 4:
                        o = kt - qb * 4
                        ctx.tt("vector", ALU.add, pb[0:128, 0:512],
                               pb[0:128, 0:512],
                               T["MASK"][0:128, o * 512:(o + 1) * 512])
                    ctx.act("scalar", HA[0:128, kt * 512:(kt + 1) * 512],
                            pb[0:128, 0:512], AF.Exp, scale=0.125)
                avp = P[6]
                ctx.mmgroup(avp[0:65, 0:512],
                            [(A2.strided(0, 128, kt * 1040 + h * 65, 1, 65, 65),
                              HA[0:128, kt * 512:(kt + 1) * 512])
                             for kt in range(nkt)])
                rslot = T["STT"][0:1, 1024 + (h % 2) * 512:
                                 1024 + (h % 2) * 512 + 512]
                ctx.recip(rslot, avp[64:65, 0:512])
                ctx.copy("scalar", SCRB[po:po + 64, 0:512], avp[0:64, 0:512])
                if h % 2 == 1:
                    rb = P[7]
                    ctx.mmgroup(rb[0:64, 0:512],
                                [(onesrf_t[0:1, 0:64],
                                  T["STT"][0:1, 1024:1536])])
                    ctx.mmgroup(rb[64:128, 0:512],
                                [(onesrf_t[0:1, 0:64],
                                  T["STT"][0:1, 1536:2048])])
                    ctx.tt("vector", ALU.mult, SCRB[0:128, 512:1024],
                           SCRB[0:128, 0:512], rb[0:128, 0:512])
                    ctx.dma("sync", ctx_d[s, ch, :, qb * 512:(qb + 1) * 512],
                            SCRB[0:128, 512:1024])

        ctx.dma("gpsimd", A2[0:128, 0:16384], H["l0_wo_rep"][:, :, :])
        for ch in range(DC):
            ctx.dma("sync", XNT[0:128, 0:S], ctx_d[s, ch, :, :])
            for hh in range(2):
                h = 2 * ch + hh
                po = 64 * hh
                for m2 in range(DC):
                    pb = P[m2 % 2]
                    pairs = []
                    for j in range(16):
                        lhsT = A2[po:po + 64,
                                  j * 1024 + m2 * 128:j * 1024 + (m2 + 1) * 128]
                        rhs = XNT.strided(po, po + 64, j, 128, 16, 1)
                        pairs.append((lhsT, rhs))
                    ctx.mmgroup(pb[0:128, 0:128], pairs)
                    ctx.copy("scalar",
                             SCRF[0:128, (m2 % 8) * 128:(m2 % 8 + 1) * 128],
                             pb[0:128, 0:128])
                    ctx.dma("gpsimd", a_d[s, m2, :, 128 * h:128 * h + 128],
                            SCRF[0:128, (m2 % 8) * 128:(m2 % 8 + 1) * 128])
    if dbg is not None and "ctx" in dbg:
        for s in range(BS):
            for c in range(DC):
                ctx.dma("gpsimd", dbg["ctx"][s, c, :, :], ctx_d[s, c, :, :])
    if dbg is not None and "a" in dbg:
        for s in range(BS):
            for c in range(DC):
                ctx.dma("gpsimd", dbg["a"][s, c, :, :], a_d[s, c, :, :])

    for c in range(DC):
        ctx.dma("gpsimd", A2[0:128, c * 8:(c + 1) * 8],
                H["l0_rw"][c * 128:(c + 1) * 128, :])
    ctx.dma("sync", T["STT"][0:1, 0:8], H["l0_rb"][:, :])
    ctx.mmgroup(P[5][0:128, 0:8],
                [(onesrf_t[0:1, 0:128], T["STT"][0:1, 0:8])])
    ctx.copy("vector", T["RBS"].full(), P[5][0:128, 0:8])
    for s in range(BS):
        for tt in range(16):
            t0 = tt * 128
            ctx.dma("sync", XNT.strided(0, 128, 0, DC, 128, 128),
                    d4T(xn_d, s, t0, t0 + 128))
            pb = P[tt % 2]
            ctx.mmgroup(pb[0:128, 0:8],
                        [(XNT[0:128, c * 128:(c + 1) * 128],
                          A2[0:128, c * 8:(c + 1) * 8]) for c in range(DC)])
            lg = SCRF[0:128, 0:8]
            pr = SCRF[0:128, 8:16]
            mx = SCRF[0:128, 16:17]
            sm = SCRF[0:128, 17:18]
            ctx.tt("vector", ALU.add, lg, pb[0:128, 0:8], T["RBS"].full())
            ctx.reduce("vector", ALU.max, mx, lg, negate=True)
            ctx.act("scalar", pr, lg, AF.Exp, bias=mx, accum_out=sm)
            ctx.recip(sm, sm)
            ctx.ts("vector", pr, pr, sm, None, ALU.mult)
            ctx.reduce("vector", ALU.max, mx, pr)
            ctx.ts("vector", lg, pr, mx, None, ALU.is_ge)
            ctx.tt("vector", ALU.mult,
                   T["GAT"][0:128, s * 128 + tt * 8:s * 128 + (tt + 1) * 8],
                   pr, lg)

    if dbg is not None and "gat" in dbg:
        ctx.dma("gpsimd", dbg["gat"][:, :], T["GAT"].full())
    for e in range(8):
        for c in range(DC):
            ctx.dma("gpsimd", A1[0:128, c * 4096:(c + 1) * 4096],
                    H["l0_w1"][e, c * 128:(c + 1) * 128, :])
        for c2 in range(32):
            ctx.dma("gpsimd", A2[0:128, c2 * 1024:(c2 + 1) * 1024],
                    H["l0_w2"][e, c2 * 128:(c2 + 1) * 128, :])
        ctx.dma("sync", T["B1S"].full(), H["l0_b1"][e, :, :])
        ctx.dma("sync", T["STT"][0:1, 0:1024], H["l0_b2"][e, :, :])
        for half in range(2):
            ctx.mmgroup(P[6][0:128, 0:512],
                        [(onesrf_t[0:1, 0:128],
                          T["STT"][0:1, half * 512:(half + 1) * 512])])
            ctx.copy("vector",
                     SCRF[0:128, 3072 + half * 512:3072 + (half + 1) * 512],
                     P[6][0:128, 0:512])
        for s in range(BS):
            for nb in range(S // 512):
                t0 = nb * 512
                load_xn_tile(ctx, T, xn_d, s, t0, 512)
                for m in range(32):
                    pb = P[m % 2]
                    ctx.mmgroup(pb[0:128, 0:512],
                                [(A1[0:128, c * 4096 + m * 128:
                                     c * 4096 + (m + 1) * 128],
                                  XNT[0:128, c * 512:(c + 1) * 512])
                                 for c in range(DC)])
                    ctx.act("scalar", HA[0:128, m * 512:(m + 1) * 512],
                            pb[0:128, 0:512], AF.Relu,
                            bias=T["B1S"][0:128, m:m + 1])
                for st in range(4):
                    tt = nb * 4 + st
                    gsl = T["GAT"][0:128, s * 128 + tt * 8 + e:
                                   s * 128 + tt * 8 + e + 1]
                    for half in range(2):
                        pb = P[2 + (st * 2 + half) % 2]
                        ctx.mmgroup(pb[0:128, 0:512],
                                    [(HA[0:128, k2 * 512 + st * 128:
                                         k2 * 512 + (st + 1) * 128],
                                      A2[0:128, k2 * 1024 + half * 512:
                                         k2 * 1024 + (half + 1) * 512])
                                     for k2 in range(32)])
                        tb = SCRF[0:128, (st % 2) * 1024 + half * 512:
                                  (st % 2) * 1024 + half * 512 + 512]
                        ctx.tt("vector", ALU.add, tb, pb[0:128, 0:512],
                               SCRF[0:128, 3072 + half * 512:
                                    3072 + (half + 1) * 512])
                        ctx.act("scalar", tb, tb, AF.Copy, scale=gsl)
                        ctx.dma("gpsimd",
                                y_d[s, tt, :, half * 512:(half + 1) * 512],
                                tb, accum=(e > 0))
    if dbg is not None and "y" in dbg:
        for s in range(BS):
            for tt in range(16):
                ctx.dma("gpsimd", dbg["y"][s, tt, :, :], y_d[s, tt, :, :])

    for s in range(BS):
        for tt in range(16):
            t0 = tt * 128
            ctx.dma("sync", SCRF[0:128, 0:1024], y_d[s, tt, :, :])
            ctx.dma("sync", SCRF.strided(0, 128, 1024, DC, 128, 128),
                    d4T(x_in, s, t0, t0 + 128))
            ctx.dma("scalar", SCRF.strided(0, 128, 2048, DC, 128, 128),
                    d4T(a_d, s, t0, t0 + 128))
            for c in range(DC):
                pb = P[4 + c % 2]
                ctx.transpose(pb[0:128, 0:128],
                              SCRF[0:128, c * 128:(c + 1) * 128],
                              T["IDF"].full())
                ob = SCRF[0:128, 3072 + (c % 4) * 128:3072 + (c % 4 + 1) * 128]
                ctx.stt("vector", ob, pb[0:128, 0:128],
                        T["BOS"][0:128, c:c + 1],
                        SCRF[0:128, 1024 + c * 128:1024 + (c + 1) * 128],
                        ALU.add, ALU.add)
                ctx.tt("vector", ALU.add, ob, ob,
                       SCRF[0:128, 2048 + c * 128:2048 + (c + 1) * 128])
                ctx.dma("gpsimd", x_out[s, c, :, t0:t0 + 128], ob,
                        is_output=True)


def declare_inputs(ctx, arrays):
    H = {}
    for name, arr in arrays.items():
        H[name] = ctx.dram(name, list(arr.shape), F32, kind="ExternalInput")
    return H


def build_layer_program(layer, weights, consts, dbg_keys=()):
    nc = bass.Bass("TRN2", target_bir_lowering=False)
    ctx = Ctx(nc)
    es = ExitStack()
    ctx.es = es
    with es:
        H = declare_inputs(ctx, {**weights, **consts})
        x_in = ctx.dram("x_in", [BS, DC, 128, S], F32, kind="ExternalInput")
        x_out = ctx.dram("x_out", [BS, DC, 128, S], F32, kind="ExternalOutput")
        dbg = {}
        for k in dbg_keys:
            if k == "xn":
                dbg["xn"] = ctx.dram("dbg_xn", [BS, DC, 128, S], BF16,
                                     kind="ExternalOutput")
            elif k == "a":
                dbg["a"] = ctx.dram("dbg_a", [BS, DC, 128, S], F32,
                                    kind="ExternalOutput")
            elif k == "y":
                if layer == 1:
                    dbg["y"] = ctx.dram("dbg_y", [BS, DC, 128, S], BF16,
                                        kind="ExternalOutput")
                else:
                    dbg["y"] = ctx.dram("dbg_y", [BS, S // 128, 128, D], F32,
                                        kind="ExternalOutput")
            elif k == "ctx":
                dbg["ctx"] = ctx.dram("dbg_ctx", [BS, DC, 128, S], BF16,
                                      kind="ExternalOutput")
            elif k == "gat":
                dbg["gat"] = ctx.dram("dbg_gat", [128, 256], F32,
                                      kind="ExternalOutput")
            elif k == "l3i":
                dbg["l3i"] = ctx.dram("dbg_l3i", [3, 128, 2048], BF16,
                                      kind="ExternalOutput")
            elif k == "gat":
                dbg["gat"] = ctx.dram("dbg_gat", [128, 256], F32,
                                      kind="ExternalOutput")
            elif k == "l3i":
                dbg["l3i"] = ctx.dram("dbg_l3i", [3, 128, 2048], BF16,
                                      kind="ExternalOutput")
        T = alloc_common(ctx)
        if layer == 0:
            emit_l0(ctx, T, H, x_in, x_out, dbg=dbg if dbg else None)
        elif layer == 1:
            emit_l1(ctx, T, H, x_in, x_out, dbg=dbg if dbg else None)
        elif layer == 2:
            emit_l2(ctx, T, H, x_in, x_out, dbg=dbg if dbg else None)
        elif layer == 3:
            emit_l3(ctx, T, H, x_in, x_out, dbg=dbg if dbg else None)
        else:
            raise NotImplementedError(layer)
        ctx.emit()
    return nc






def f32v(tile, f0, f1):
    ap = tile.h[0:tile.shape[0], 2 * f0:2 * f1].bitcast(F32)
    return ap, Span(tile.name, 0, tile.shape[0], f0 * 4, f1 * 4)


def f32vp(tile, p0, p1, f0, f1):
    ap = tile.h[p0:p1, 2 * f0:2 * f1].bitcast(F32)
    return ap, Span(tile.name, p0, p1, f0 * 4, f1 * 4)


def l2_dft_const():
    M = np.zeros((512, 256), np.float64)
    M[np.arange(1, 257), np.arange(256)] = 1.0
    t = np.arange(257, 512)
    M[t, 512 - t] = -1.0
    A = np.imag(np.fft.fft(M, axis=0))
    return np.ascontiguousarray(A.T).astype(np.float32)


def prep_l2(inputs):
    d = {}
    adft = l2_dft_const()
    d["l2_adft"] = np.ascontiguousarray(
        adft.reshape(2, 128, 512).transpose(1, 0, 2).reshape(128, 1024))
    d["l2_ap"] = np.ascontiguousarray(
        inputs["L2_attn_a_params"].reshape(2, 128).T)
    d["l2_bwT"] = np.ascontiguousarray(inputs["L2_attn_B_w"].T)
    d["l2_cwT"] = np.ascontiguousarray(inputs["L2_attn_C_w"].T)
    d["l2_D"] = bias_chunks(inputs["L2_attn_D"])
    d["l2_gwT"] = np.ascontiguousarray(inputs["L2_ffn_gate_w"].T)
    d["l2_gb"] = np.ascontiguousarray(inputs["L2_ffn_gate_b"].reshape(1, 8))
    w1 = np.concatenate([inputs["L2_ffn_w1"], inputs["L2_ffn_sw1"][None]], 0)
    w3 = np.concatenate([inputs["L2_ffn_w3"], inputs["L2_ffn_sw3"][None]], 0)
    w2 = np.concatenate([inputs["L2_ffn_w2"], inputs["L2_ffn_sw2"][None]], 0)
    b1 = np.concatenate([inputs["L2_ffn_b1"], inputs["L2_ffn_sb1"][None]], 0)
    b3 = np.concatenate([inputs["L2_ffn_b3"], inputs["L2_ffn_sb3"][None]], 0)
    b2 = np.concatenate([inputs["L2_ffn_b2"], inputs["L2_ffn_sb2"][None]], 0)
    HP = 1408 // 128
    d["l2_w1"] = np.ascontiguousarray(w1)
    d["l2_w3"] = np.ascontiguousarray(w3)
    d["l2_w2"] = np.ascontiguousarray(w2)
    d["l2_b1"] = np.ascontiguousarray(
        np.stack([b.reshape(HP, 128).T for b in b1]))
    d["l2_b3"] = np.ascontiguousarray(
        np.stack([b.reshape(HP, 128).T for b in b3]))
    d["l2_b2"] = np.ascontiguousarray(b2).reshape(9, 1, 1024)
    return d


def emit_l2(ctx, T, H, x_in, x_out, dbg=None):
    xn_d = ctx.dram("l2xn_d", [BS, DC, 128, S], BF16)
    a_d = ctx.dram("l2a_d", [BS, DC, 128, S], F32)
    y_d = ctx.dram("l2y_d", [BS, S // 128, 128, D], F32)

    A1, A2, HA, XNT = T["A1"], T["A2"], T["HA"], T["XNT"]
    SCRF, SCRB, STT = T["SCRF"], T["SCRB"], T["STT"]
    P = T["PS"]
    onesrf_t = T["_onesrf_t"]

    ctx.dma("sync", T["IDF"].full(), H["c_identf"][:, :])
    emit_layernorm_to_dram(ctx, T, x_in, xn_d)
    if dbg is not None and "xn" in dbg:
        for s in range(BS):
            for c in range(DC):
                ctx.dma("gpsimd", dbg["xn"][s, c, :, :], xn_d[s, c, :, :])

    ctx.dma("sync", SCRF[0:128, 0:1024], H["l2_adft"][:, :])
    ctx.dma("sync", SCRF[0:128, 1024:1026], H["l2_ap"][:, :])
    for wc in range(4):
        ctx.mmgroup(P[0][0:128, 0:1], [
            (SCRF[0:128, k * 512 + wc * 128:k * 512 + (wc + 1) * 128],
             SCRF[0:128, 1024 + k:1024 + k + 1]) for k in range(2)])
        ctx.copy("vector", SCRF[0:128, 1032 + wc:1033 + wc], P[0][0:128, 0:1])
    OM = lambda: SCRF[0:128, 1032:1036]
    ctx.tt("vector", ALU.mult, SCRF[0:128, 1040:1044], OM(), OM())
    ctx.ts("vector", SCRF[0:128, 1044:1048], SCRF[0:128, 1040:1044], 1.0, None, ALU.add)
    ctx.recip(SCRF[0:128, 1044:1048], SCRF[0:128, 1044:1048])
    ctx.ts("vector", SCRF[0:128, 1048:1052], SCRF[0:128, 1040:1044], -1.0, 1.0,
           ALU.mult, ALU.add)
    ctx.tt("vector", ALU.mult, SCRF[0:128, 1048:1052], SCRF[0:128, 1048:1052],
           SCRF[0:128, 1044:1048])
    ctx.ts("vector", SCRF[0:128, 1052:1056], OM(), -2.0, None, ALU.mult)
    ctx.tt("vector", ALU.mult, SCRF[0:128, 1052:1056], SCRF[0:128, 1052:1056],
           SCRF[0:128, 1044:1048])
    ctx.copy("vector", SCRF[0:128, 1056:1060], SCRF[0:128, 1048:1052])
    ctx.copy("vector", SCRF[0:128, 1060:1064], SCRF[0:128, 1052:1056])
    for c in range(4):
        ctx.memset("vector", f32v(A1, c * 2048, c * 2048 + 1), 1.0)
        ctx.memset("vector", f32v(A1, 8192 + c * 2048, 8192 + c * 2048 + 1), 0.0)
    L = 1
    while L < S:
        for c in range(4):
            Cb = lambda a, b: f32v(A1, c * 2048 + a, c * 2048 + b)
            Sb = lambda a, b: f32v(A1, 8192 + c * 2048 + a, 8192 + c * 2048 + b)
            mr = SCRF[0:128, 1056 + c:1057 + c]
            mi = SCRF[0:128, 1060 + c:1061 + c]
            ctx.ts("vector", SCRF[0:128, 1536:1536 + L], Sb(0, L), mi, None, ALU.mult)
            ctx.ts("vector", SCRF[0:128, 2560:2560 + L], Cb(0, L), mi, None, ALU.mult)
            ctx.stt("vector", Cb(L, 2 * L), Cb(0, L), mr,
                    SCRF[0:128, 1536:1536 + L], ALU.mult, ALU.subtract)
            ctx.stt("vector", Sb(L, 2 * L), Sb(0, L), mr,
                    SCRF[0:128, 2560:2560 + L], ALU.mult, ALU.add)
        if 2 * L < S:
            MR = SCRF[0:128, 1056:1060]
            MI = SCRF[0:128, 1060:1064]
            ctx.tt("vector", ALU.mult, SCRF[0:128, 1064:1068], MR, MI)
            ctx.tt("vector", ALU.mult, SCRF[0:128, 1068:1072], MR, MR)
            ctx.tt("vector", ALU.mult, SCRF[0:128, 1072:1076], MI, MI)
            ctx.tt("vector", ALU.subtract, MR, SCRF[0:128, 1068:1072],
                   SCRF[0:128, 1072:1076])
            ctx.ts("vector", MI, SCRF[0:128, 1064:1068], 2.0, None, ALU.mult)
        L *= 2
    ctx.memset("vector", f32v(SCRB, 0, 2048), 1.0)

    for k in range(4):
        ctx.dma("gpsimd", A2[0:128, 24576 + k * 1024:24576 + (k + 1) * 1024],
                H["l2_cwT"][k * 128:(k + 1) * 128, :])
    for k in range(DC):
        ctx.dma("gpsimd", A2[0:128, 28672 + k * 512:28672 + (k + 1) * 512],
                H["l2_bwT"][k * 128:(k + 1) * 128, :])
    ctx.dma("sync", T["BOS"].full(), H["l2_D"][:, :])

    for s in range(BS):
        for nb in range(S // 512):
            t0 = nb * 512
            load_xn_tile(ctx, T, xn_d, s, t0, 512)
            for m in range(4):
                pb = P[m % 2]
                ctx.mmgroup(pb[0:128, 0:512],
                            [(A2[0:128, 28672 + k * 512 + m * 128:
                                 28672 + k * 512 + (m + 1) * 128],
                              XNT[0:128, k * 512:(k + 1) * 512])
                             for k in range(DC)])
                ctx.copy("scalar", f32v(A2, m * 2048 + t0, m * 2048 + t0 + 512),
                         pb[0:128, 0:512])
        for c in range(4):
            BUc = lambda: f32v(A2, c * 2048, (c + 1) * 2048)
            Cc = lambda: f32v(A1, c * 2048, (c + 1) * 2048)
            Sc = lambda: f32v(A1, 8192 + c * 2048, 8192 + (c + 1) * 2048)
            T1 = lambda: f32v(HA, 0, 2048)
            Q1 = lambda: f32v(HA, 2048, 4096)
            Q2 = lambda: f32v(HA, 4096, 6144)
            ONE = lambda: f32v(SCRB, 0, 2048)
            ctx.tt("vector", ALU.mult, T1(), Cc(), BUc())
            ctx.scan(Q1(), ONE(), T1(), 0.0)
            ctx.tt("vector", ALU.mult, T1(), Sc(), BUc())
            ctx.scan(Q2(), ONE(), T1(), 0.0)
            ctx.tt("vector", ALU.mult, Q1(), Cc(), Q1())
            ctx.tt("vector", ALU.mult, Q2(), Sc(), Q2())
            ctx.tt("vector", ALU.add, A2[0:128, 16384 + c * 2048:
                                         16384 + (c + 1) * 2048], Q1(), Q2())
        for nb in range(S // 512):
            t0 = nb * 512
            load_xn_tile(ctx, T, xn_d, s, t0, 512)
            for m in range(DC):
                pb = P[2 + m % 2]
                ctx.mmgroup(pb[0:128, 0:512],
                            [(A2[0:128, 24576 + k * 1024 + m * 128:
                                 24576 + k * 1024 + (m + 1) * 128],
                              A2[0:128, 16384 + k * 2048 + t0:
                                 16384 + k * 2048 + t0 + 512])
                             for k in range(4)])
                ctx.stt("vector", SCRF[0:128, (m % 4) * 512:(m % 4 + 1) * 512],
                        XNT[0:128, m * 512:(m + 1) * 512],
                        T["BOS"][0:128, m:m + 1], pb[0:128, 0:512],
                        ALU.mult, ALU.add)
                ctx.dma("gpsimd", a_d[s, m, :, t0:t0 + 512],
                        SCRF[0:128, (m % 4) * 512:(m % 4 + 1) * 512])
    if dbg is not None and "a" in dbg:
        for s in range(BS):
            for c in range(DC):
                ctx.dma("gpsimd", dbg["a"][s, c, :, :], a_d[s, c, :, :])

    for c in range(DC):
        ctx.dma("gpsimd", A2[0:128, c * 8:(c + 1) * 8],
                H["l2_gwT"][c * 128:(c + 1) * 128, :])
    ctx.dma("sync", STT[0:1, 0:8], H["l2_gb"][:, :])
    ctx.mmgroup(P[5][0:128, 0:8],
                [(onesrf_t[0:1, 0:128], STT[0:1, 0:8])])
    ctx.copy("vector", T["RBS"].full(), P[5][0:128, 0:8])
    for s in range(BS):
        for tt in range(16):
            t0 = tt * 128
            ctx.dma("sync", XNT.strided(0, 128, 0, DC, 128, 128),
                    d4T(xn_d, s, t0, t0 + 128))
            pb = P[tt % 2]
            ctx.mmgroup(pb[0:128, 0:8],
                        [(XNT[0:128, c * 128:(c + 1) * 128],
                          A2[0:128, c * 8:(c + 1) * 8]) for c in range(DC)])
            lg = SCRF[0:128, 0:8]
            bi = SCRF[0:128, 8:16]
            mx = SCRF[0:128, 16:17]
            sm = SCRF[0:128, 17:18]
            msk = SCRF[0:128, 24:32]
            ctx.copy("vector", lg, pb[0:128, 0:8])
            ctx.reduce("vector", ALU.max, mx, lg, negate=True)
            ctx.act("scalar", lg, lg, AF.Exp, bias=mx, accum_out=sm)
            ctx.recip(sm, sm)
            ctx.ts("vector", lg, lg, sm, None, ALU.mult)
            ctx.tt("vector", ALU.add, bi, lg, T["RBS"].full())
            ctx.reduce("vector", ALU.max, mx, bi)
            ctx.ts("vector", msk, bi, mx, None, ALU.is_ge)
            ctx.stt("vector", bi, msk, -1.0e30, bi, ALU.mult, ALU.add)
            ctx.reduce("vector", ALU.max, mx, bi)
            ctx.tt("vector", ALU.add, bi, lg, T["RBS"].full())
            ctx.ts("vector", msk, bi, mx, None, ALU.is_ge)
            ctx.tt("vector", ALU.mult,
                   T["GAT"][0:128, s * 128 + tt * 8:s * 128 + (tt + 1) * 8],
                   lg, msk)

    HP = 11
    for e in range(9):
        for c in range(DC):
            ctx.dma("gpsimd", A1[0:128, c * 1408:(c + 1) * 1408],
                    H["l2_w1"][e, c * 128:(c + 1) * 128, :])
            ctx.dma("gpsimd", A1[0:128, 16384 + c * 1408:16384 + (c + 1) * 1408],
                    H["l2_w3"][e, c * 128:(c + 1) * 128, :])
        for c2 in range(HP):
            ctx.dma("gpsimd", A2[0:128, c2 * 1024:(c2 + 1) * 1024],
                    H["l2_w2"][e, c2 * 128:(c2 + 1) * 128, :])
        ctx.dma("sync", T["B1S"][0:128, 0:11], H["l2_b1"][e, :, :])
        ctx.dma("sync", T["B1S"][0:128, 16:27], H["l2_b3"][e, :, :])
        ctx.dma("sync", STT[0:1, 0:1024], H["l2_b2"][e, :, :])
        for half in range(2):
            ctx.mmgroup(P[6][0:128, 0:512],
                        [(onesrf_t[0:1, 0:128],
                          STT[0:1, half * 512:(half + 1) * 512])])
            ctx.copy("vector",
                     SCRF[0:128, 3072 + half * 512:3072 + (half + 1) * 512],
                     P[6][0:128, 0:512])
        for s in range(BS):
            for nb in range(S // 512):
                t0 = nb * 512
                load_xn_tile(ctx, T, xn_d, s, t0, 512)
                for m in range(HP):
                    p1 = P[0]
                    p3 = P[1]
                    ctx.mmgroup(p1[0:128, 0:512],
                                [(A1[0:128, c * 1408 + m * 128:
                                     c * 1408 + (m + 1) * 128],
                                  XNT[0:128, c * 512:(c + 1) * 512])
                                 for c in range(DC)])
                    ctx.mmgroup(p3[0:128, 0:512],
                                [(A1[0:128, 16384 + c * 1408 + m * 128:
                                     16384 + c * 1408 + (m + 1) * 128],
                                  XNT[0:128, c * 512:(c + 1) * 512])
                                 for c in range(DC)])
                    tb = SCRF[0:128, 0:512]
                    ctx.ts("vector", tb, p3[0:128, 0:512],
                           T["B1S"][0:128, 16 + m:17 + m], None, ALU.add)
                    ctx.stt("vector", SCRF[0:128, 512:1024], p1[0:128, 0:512],
                            T["B1S"][0:128, m:m + 1], tb, ALU.add, ALU.mult)
                    ctx.act("scalar", HA[0:128, m * 512:(m + 1) * 512],
                            SCRF[0:128, 512:1024], AF.Silu)
                for st in range(4):
                    tt = nb * 4 + st
                    gsl = T["GAT"][0:128, s * 128 + tt * 8 + min(e, 7):
                                   s * 128 + tt * 8 + min(e, 7) + 1]
                    for half in range(2):
                        pb = P[2 + (st * 2 + half) % 2]
                        ctx.mmgroup(pb[0:128, 0:512],
                                    [(HA[0:128, k2 * 512 + st * 128:
                                         k2 * 512 + (st + 1) * 128],
                                      A2[0:128, k2 * 1024 + half * 512:
                                         k2 * 1024 + (half + 1) * 512])
                                     for k2 in range(HP)])
                        tb = SCRF[0:128, 1024 + (st % 2) * 1024 + half * 512:
                                  1024 + (st % 2) * 1024 + half * 512 + 512]
                        ctx.tt("vector", ALU.add, tb, pb[0:128, 0:512],
                               SCRF[0:128, 3072 + half * 512:
                                    3072 + (half + 1) * 512])
                        if e < 8:
                            ctx.act("scalar", tb, tb, AF.Copy, scale=gsl)
                        ctx.dma("gpsimd",
                                y_d[s, tt, :, half * 512:(half + 1) * 512],
                                tb, accum=(e > 0))
    if dbg is not None and "y" in dbg:
        for s in range(BS):
            for tt in range(16):
                ctx.dma("gpsimd", dbg["y"][s, tt, :, :], y_d[s, tt, :, :])

    emit_combine(ctx, T, x_in, x_out, a_d, y_d, bo=None)


def emit_combine(ctx, T, x_in, x_out, a_d, y_d, bo=None):
    SCRF, P = T["SCRF"], T["PS"]
    for s in range(BS):
        for tt in range(16):
            t0 = tt * 128
            ctx.dma("sync", SCRF[0:128, 0:1024], y_d[s, tt, :, :])
            ctx.dma("sync", SCRF.strided(0, 128, 1024, DC, 128, 128),
                    d4T(x_in, s, t0, t0 + 128))
            ctx.dma("scalar", SCRF.strided(0, 128, 2048, DC, 128, 128),
                    d4T(a_d, s, t0, t0 + 128))
            for c in range(DC):
                pb = P[4 + c % 2]
                ctx.transpose(pb[0:128, 0:128],
                              SCRF[0:128, c * 128:(c + 1) * 128],
                              T["IDF"].full())
                ob = SCRF[0:128, 3072 + (c % 4) * 128:3072 + (c % 4 + 1) * 128]
                if bo is not None:
                    ctx.stt("vector", ob, pb[0:128, 0:128],
                            bo[0:128, c:c + 1],
                            SCRF[0:128, 1024 + c * 128:1024 + (c + 1) * 128],
                            ALU.add, ALU.add)
                else:
                    ctx.tt("vector", ALU.add, ob, pb[0:128, 0:128],
                           SCRF[0:128, 1024 + c * 128:1024 + (c + 1) * 128])
                ctx.tt("vector", ALU.add, ob, ob,
                       SCRF[0:128, 2048 + c * 128:2048 + (c + 1) * 128])
                ctx.dma("gpsimd", x_out[s, c, :, t0:t0 + 128], ob,
                        is_output=True)



def l1_rotary_consts():
    DH, N, W = 64, 2048, 512
    half = np.arange(0, DH, 2, dtype=np.float64)
    inv_freq = 1.0 / 10000.0 ** (half / DH)
    t = np.arange(N, dtype=np.float64)
    fr = t[:, None] * inv_freq[None, :]
    fr = np.concatenate([fr, fr], -1)
    base = (half + 0.4 * DH) / (1.4 * DH)
    power = (t - N // 2) / float(W // 2)
    sc = base[None, :] ** power[:, None]
    sc = np.concatenate([sc, sc], -1)
    cos, sin = np.cos(fr), np.sin(fr)
    sgn = np.where(np.arange(DH) < 32, -1.0, 1.0)[None, :]
    CSQ = (cos * sc)
    SSQ = (sgn * sin * sc)
    CSK = (cos / sc)
    SSK = (sgn * sin / sc)
    out = {}
    for nm, tab in (("c_l1csq", CSQ), ("c_l1ssq", SSQ),
                    ("c_l1csk", CSK), ("c_l1ssk", SSK)):
        tt = np.tile(tab.T, (2, 1))
        out[nm] = np.ascontiguousarray(tt).astype(np.float32)
    m = np.zeros((128, 4096), np.float32)
    for qt in range(4):
        i_loc = qt * 128 + np.arange(128)[:, None]
        jj = np.arange(512)[None, :]
        m[:, qt * 512:(qt + 1) * 512] = np.where(jj < i_loc, -1e30, 0.0)
        m[:, 2048 + qt * 512:2048 + (qt + 1) * 512] = \
            np.where(jj > i_loc, -1e30, 0.0)
    out["c_l1mask"] = m
    return out


def prep_l1(inputs):
    d = {}
    d["l1_winT"] = np.ascontiguousarray(inputs["L1_attn_W_in"].T)
    d["l1_waT"] = np.ascontiguousarray(inputs["L1_attn_W_alpha"].T)
    d["l1_wbT"] = np.ascontiguousarray(inputs["L1_attn_W_beta"].T)
    d["l1_wvT"] = np.ascontiguousarray(inputs["L1_attn_W_v"].T)
    d["l1_woutT"] = np.ascontiguousarray(inputs["L1_attn_W_out"].T)
    d["l1_ba"] = bias_chunks(inputs["L1_attn_b_alpha"])
    d["l1_bb"] = bias_chunks(inputs["L1_attn_b_beta"])
    d["l1_bv"] = bias_chunks(inputs["L1_attn_b_v"])
    d["l1_wqkv"] = inputs["L1_ffn_wqkv"]
    d["l1_qs"] = np.ascontiguousarray(
        np.tile(inputs["L1_ffn_q_scale"], (1, 2))[:, :, None])
    d["l1_ks"] = np.ascontiguousarray(
        np.tile(inputs["L1_ffn_k_scale"], (1, 2))[:, :, None])
    d["l1_wo"] = inputs["L1_ffn_wo"]
    d["l1_lg"] = np.ascontiguousarray(
        np.stack([bias_chunks(g) for g in inputs["L1_ffn_ln_g"]]))
    d["l1_lb"] = np.ascontiguousarray(
        np.stack([bias_chunks(b) for b in inputs["L1_ffn_ln_b"]]))
    w1 = inputs["L1_ffn_w1"]
    HID = 2730
    HPAD = 2816
    w1a = np.zeros((2, 1024, HPAD), np.float32)
    w1g = np.zeros((2, 1024, HPAD), np.float32)
    w1a[:, :, :HID] = w1[:, :, :HID]
    w1g[:, :, :HID] = w1[:, :, HID:]
    b1 = inputs["L1_ffn_b1"]
    b1a = np.zeros((2, HPAD), np.float32)
    b1g = np.zeros((2, HPAD), np.float32)
    b1a[:, :HID] = b1[:, :HID]
    b1g[:, :HID] = b1[:, HID:]
    w2 = inputs["L1_ffn_w2"]
    w2p = np.zeros((2, HPAD, 1024), np.float32)
    w2p[:, :HID] = w2
    d["l1_w1a"] = np.ascontiguousarray(w1a.reshape(2, 8, 128, HPAD))
    d["l1_w1g"] = np.ascontiguousarray(w1g.reshape(2, 8, 128, HPAD))
    d["l1_w2p"] = np.ascontiguousarray(w2p)
    d["l1_b1a"] = np.ascontiguousarray(
        np.stack([b.reshape(22, 128).T for b in b1a]))
    d["l1_b1g"] = np.ascontiguousarray(
        np.stack([b.reshape(22, 128).T for b in b1g]))
    d["l1_b2"] = np.ascontiguousarray(inputs["L1_ffn_b2"]).reshape(2, 1, 1024)
    return d


def emit_l1(ctx, T, H, x_in, x_out, dbg=None):
    xn_d = ctx.dram("l1xn_d", [BS, DC, 128, S], BF16)
    a_d = ctx.dram("l1a_d", [BS, DC, 128, S], F32)
    f_d = ctx.dram("l1f_d", [BS, DC, 128, S], BF16)

    A1, A2, HA, XNT = T["A1"], T["A2"], T["HA"], T["XNT"]
    SCRF, SCRB, STT = T["SCRF"], T["SCRB"], T["STT"]
    P = T["PS"]
    onesrf_t = T["_onesrf_t"]
    SEL = T["B1S"]

    ctx.dma("gpsimd", T["IDB"].full(), H["c_identf"][:, :])
    emit_layernorm_to_dram(ctx, T, x_in, xn_d)
    if dbg is not None and "xn" in dbg:
        for s in range(BS):
            for c in range(DC):
                ctx.dma("gpsimd", dbg["xn"][s, c, :, :], xn_d[s, c, :, :])

    for s in range(BS):
        for c in range(DC):
            ctx.dma("gpsimd", A2[0:128, c * 1024:(c + 1) * 1024],
                    H["l1_winT"][c * 128:(c + 1) * 128, :])
        for nb in range(S // 512):
            t0 = nb * 512
            load_xn_tile(ctx, T, xn_d, s, t0, 512)
            for m in range(DC):
                pb = P[m % 2]
                ctx.mmgroup(pb[0:128, 0:512],
                            [(A2[0:128, c * 1024 + m * 128:c * 1024 + (m + 1) * 128],
                              XNT[0:128, c * 512:(c + 1) * 512])
                             for c in range(DC)])
                ctx.act("scalar", A1[0:128, m * S + t0:m * S + t0 + 512],
                        pb[0:128, 0:512], AF.Silu)
        for c in range(DC):
            ctx.dma("gpsimd", A2[0:128, c * 1024:(c + 1) * 1024],
                    H["l1_waT"][c * 128:(c + 1) * 128, :])
            ctx.dma("gpsimd", A2[0:128, 8192 + c * 1024:8192 + (c + 1) * 1024],
                    H["l1_wbT"][c * 128:(c + 1) * 128, :])
            ctx.dma("gpsimd", A2[0:128, 16384 + c * 1024:16384 + (c + 1) * 1024],
                    H["l1_wvT"][c * 128:(c + 1) * 128, :])
        ctx.dma("sync", SEL[0:128, 0:8], H["l1_ba"][:, :])
        ctx.dma("sync", SEL[0:128, 8:16], H["l1_bb"][:, :])
        ctx.dma("sync", SEL[0:128, 16:24], H["l1_bv"][:, :])
        for m in range(DC):
            for nb in range(S // 512):
                t0 = nb * 512
                pa, pbt, pv = P[0], P[1], P[2]
                for (pp, woff) in ((pa, 0), (pbt, 8192), (pv, 16384)):
                    ctx.mmgroup(pp[0:128, 0:512],
                                [(A2[0:128, woff + c * 1024 + m * 128:
                                     woff + c * 1024 + (m + 1) * 128],
                                  A1[0:128, c * S + t0:c * S + t0 + 512])
                                 for c in range(DC)])
                ctx.act("scalar", SCRF[0:128, t0:t0 + 512], pa[0:128, 0:512],
                        AF.Sigmoid, bias=SEL[0:128, m:m + 1])
                ctx.act("scalar", f32v(HA, 0, 512), pbt[0:128, 0:512],
                        AF.Sigmoid, bias=SEL[0:128, 8 + m:9 + m])
                ctx.act("scalar", f32v(HA, 512, 1024), pv[0:128, 0:512],
                        AF.Tanh, bias=SEL[0:128, 16 + m:17 + m])
                ctx.tt("vector", ALU.mult, SCRF[0:128, 2048 + t0:2048 + t0 + 512],
                       f32v(HA, 0, 512), f32v(HA, 512, 1024))
            ctx.scan(f32v(HA, 1024, 3072), SCRF[0:128, 0:2048],
                     SCRF[0:128, 2048:4096], 0.0)
            ctx.act("scalar", f32v(HA, 3072, 5120), f32v(HA, 1024, 3072), AF.Silu)
            ctx.tt("vector", ALU.mult, A1[0:128, 16384 + m * S:16384 + (m + 1) * S],
                   f32v(HA, 1024, 3072), f32v(HA, 3072, 5120))
        for c in range(DC):
            ctx.dma("gpsimd", A2[0:128, c * 1024:(c + 1) * 1024],
                    H["l1_woutT"][c * 128:(c + 1) * 128, :])
        for nb in range(S // 512):
            t0 = nb * 512
            for m in range(DC):
                pb = P[m % 2]
                ctx.mmgroup(pb[0:128, 0:512],
                            [(A2[0:128, c * 1024 + m * 128:c * 1024 + (m + 1) * 128],
                              A1[0:128, 16384 + c * S + t0:16384 + c * S + t0 + 512])
                             for c in range(DC)])
                ctx.copy("scalar", SCRF[0:128, (m % 4) * 512:(m % 4 + 1) * 512],
                         pb[0:128, 0:512])
                ctx.dma("gpsimd", a_d[s, m, :, t0:t0 + 512],
                        SCRF[0:128, (m % 4) * 512:(m % 4 + 1) * 512])
    if dbg is not None and "a" in dbg:
        for s in range(BS):
            for c in range(DC):
                ctx.dma("gpsimd", dbg["a"][s, c, :, :], a_d[s, c, :, :])

    SELB = T["SELB"]
    ctx.memset("vector", SELB[0:128, 0:2], 0.0)
    ctx.memset("vector", SELB[0:64, 0:1], 1.0)
    ctx.memset("vector", SELB[64:128, 1:2], 1.0)
    NW = S // 512

    for s in range(BS):
        for c in range(DC):
            ctx.dma("sync", A1[0:128, c * S:(c + 1) * S], xn_d[s, c, :, :])
        for l in range(2):
            for c in range(DC):
                ctx.dma("gpsimd", A2[0:128, c * 1536:(c + 1) * 1536],
                        H["l1_wqkv"][l, c * 128:(c + 1) * 128, :])
            for c in range(4):
                ctx.dma("gpsimd", A2[0:128, 12288 + c * 1024:12288 + (c + 1) * 1024],
                        H["l1_wo"][l, c * 128:(c + 1) * 128, :])
            ctx.dma("sync", SEL[0:128, 24:25], H["l1_qs"][l, :, :])
            ctx.dma("sync", SEL[0:128, 25:26], H["l1_ks"][l, :, :])
            for (nm, off, scol) in (("c_l1csq", 16384, 24), ("c_l1ssq", 18432, 24),
                                    ("c_l1csk", 20480, 25), ("c_l1ssk", 22528, 25)):
                ctx.dma("gpsimd", A2[0:128, off:off + 2048], H[nm][:, :])
                ctx.ts("vector", A2[0:128, off:off + 2048],
                       A2[0:128, off:off + 2048], SEL[0:128, scol:scol + 1],
                       None, ALU.mult)
            ctx.dma("gpsimd", A2[0:128, 24576:28672], H["c_l1mask"][:, :])
            for w in range(NW):
                t0 = w * 512
                qb = 6144 * (w % 2)
                if w == 0:
                    for ci in range(4):
                        c = 4 + ci
                        ctx.copy("vector",
                                 HA[0:128, 12288 + ci * 512 + 1:
                                    12288 + (ci + 1) * 512],
                                 A1[0:128, c * S:c * S + 511])
                        ctx.memset("vector",
                                   HA[0:128, 12288 + ci * 512:12288 + ci * 512 + 1],
                                   0.0)
                for m in range(12):
                    pb = P[m % 2]
                    pairs = []
                    for c in range(DC):
                        if c < 4:
                            rhs = A1[0:128, c * S + t0:c * S + t0 + 512]
                        elif w == 0:
                            rhs = HA[0:128, 12288 + (c - 4) * 512:
                                     12288 + (c - 3) * 512]
                        else:
                            rhs = A1[0:128, c * S + t0 - 1:c * S + t0 + 511]
                        pairs.append(
                            (A2[0:128, c * 1536 + m * 128:c * 1536 + (m + 1) * 128],
                             rhs))
                    ctx.mmgroup(pb[0:128, 0:512], pairs)
                    ctx.copy("scalar", HA[0:128, qb + m * 512:qb + (m + 1) * 512],
                             pb[0:128, 0:512])
                for ch in range(8):
                    src = HA[0:128, qb + ch * 512:qb + (ch + 1) * 512]
                    U = f32v(SCRB, 0, 512)
                    SW = f32v(SCRB, 512, 1024)
                    T1 = f32v(SCRB, 1024, 1536)
                    SQB = SCRB[0:128, 3072:3584]
                    ctx.tt("vector", ALU.mult, SQB, src, src)
                    ctx.mmgroup(P[4][0:1, 0:512], [(SELB[0:128, 0:1], SQB)])
                    ctx.mmgroup(P[5][0:1, 0:512], [(SELB[0:128, 1:2], SQB)])
                    for (pp, fo) in ((P[4], 0), (P[5], 512)):
                        ctx.act("scalar", STT[0:1, fo:fo + 512], pp[0:1, 0:512],
                                AF.Sqrt)
                        ctx.ts("vector", STT[0:1, fo:fo + 512],
                               STT[0:1, fo:fo + 512], 1e-12, None, ALU.max)
                        ctx.recip(STT[0:1, fo:fo + 512], STT[0:1, fo:fo + 512])
                    rb = P[6]
                    ctx.mmgroup(rb[0:64, 0:512],
                                [(onesrf_t[0:1, 0:64], STT[0:1, 0:512])])
                    ctx.mmgroup(rb[64:128, 0:512],
                                [(onesrf_t[0:1, 0:64], STT[0:1, 512:1024])])
                    ctx.tt("vector", ALU.mult, U, src, rb[0:128, 0:512])
                    for bb in range(4):
                        sb = (bb ^ 1) * 32
                        ctx.copy("vector" if bb % 2 else "scalar",
                                 f32vp(SCRB, bb * 32, bb * 32 + 32, 512, 1024),
                                 f32vp(SCRB, sb, sb + 32, 0, 512))
                    toff = 16384 if ch < 4 else 20480
                    ctx.tt("vector", ALU.mult, T1, SW,
                           A2[0:128, toff + 2048 + t0:toff + 2048 + t0 + 512])
                    ctx.tt("vector", ALU.mult, U, U,
                           A2[0:128, toff + t0:toff + t0 + 512])
                    ctx.tt("vector", ALU.add, src, U, T1)
                for h in range(8):
                    po = 64 * (h % 2)
                    chq = h // 2
                    for kb in range(8):
                        if w == 0 and kb < 4:
                            continue
                        buf = qb if kb >= 4 else 6144 * ((w - 1) % 2)
                        kk = kb % 4
                        src_v = HA[po:po + 64,
                                   buf + (8 + chq) * 512 + kk * 128:
                                   buf + (8 + chq) * 512 + (kk + 1) * 128]
                        ptp = (P[7].h[0:128, 0:32].bitcast(BF16),
                               Span(P[7].name, 0, 128, 0, 128))
                        ctx.transpose(ptp, src_v, T["IDB"][po:po + 64, po:po + 64])
                        ctx.copy("vector", SCRB[0:128, kb * 64:(kb + 1) * 64], ptp)
                    for qt in range(4):
                        qsl = HA[po:po + 64, qb + chq * 512 + qt * 128:
                                 qb + chq * 512 + (qt + 1) * 128]
                        sp0 = P[0]
                        sp1 = P[1]
                        if w > 0:
                            ctx.mmgroup(sp0[0:128, 0:512],
                                        [(qsl, HA[po:po + 64,
                                                  6144 * ((w - 1) % 2) + (4 + chq) * 512:
                                                  6144 * ((w - 1) % 2) + (4 + chq) * 512 + 512])])
                        ctx.mmgroup(sp1[0:128, 0:512],
                                    [(qsl, HA[po:po + 64, qb + (4 + chq) * 512:
                                              qb + (4 + chq) * 512 + 512])])
                        SSa = f32v(HA, 7168, 7680)
                        SSb = f32v(HA, 7680, 8192)
                        SSfull = f32v(HA, 7168, 8192)
                        if w > 0:
                            ctx.stt("vector", SSa, sp0[0:128, 0:512], 8.0,
                                    A2[0:128, 24576 + qt * 512:24576 + (qt + 1) * 512],
                                    ALU.mult, ALU.add)
                        else:
                            ctx.memset("vector", SSa, -1e30)
                        ctx.stt("vector", SSb, sp1[0:128, 0:512], 8.0,
                                A2[0:128, 26624 + qt * 512:26624 + (qt + 1) * 512],
                                ALU.mult, ALU.add)
                        mx = SCRF[0:128, 3072:3073]
                        sm = SCRF[0:128, 3073:3074]
                        rc = SCRF[0:128, 3074:3075]
                        ctx.reduce("vector", ALU.max, mx, SSfull, negate=True)
                        ctx.act("scalar", f32v(HA, 6144, 7168), SSfull, AF.Exp,
                                bias=mx, accum_out=sm)
                        ctx.recip(rc, sm)
                        ctx.ts("vector", SCRB[0:128, 1024:2048],
                               f32v(HA, 6144, 7168), rc, None, ALU.mult)
                        av = P[2 if qt % 2 == 0 else 3]
                        first = True
                        for kb in range(8):
                            if w == 0 and kb < 4:
                                continue
                            ptp = (P[7].h[0:128, 0:64].bitcast(BF16),
                                   Span(P[7].name, 0, 128, 0, 256))
                            ctx.transpose(ptp,
                                          SCRB[0:128, 1024 + kb * 128:
                                               1024 + (kb + 1) * 128],
                                          T["IDB"].full())
                            ctx.copy("vector", SCRB[0:128, 512:640], ptp)
                            ctx.mmgroup(av[0:64, 0:128],
                                        [(SCRB[0:128, kb * 64:(kb + 1) * 64],
                                          SCRB[0:128, 512:640])],
                                        accum_into=(not first),
                                        stop_group=(kb == 7))
                            first = False
                        ctx.copy("scalar",
                                 A1[po:po + 64, 16384 + chq * S + t0 + qt * 128:
                                    16384 + chq * S + t0 + (qt + 1) * 128],
                                 av[0:64, 0:128])
            for nb in range(NW):
                t0 = nb * 512
                for m in range(DC):
                    pb = P[m % 2]
                    ctx.mmgroup(pb[0:128, 0:512],
                                [(A2[0:128, 12288 + c * 1024 + m * 128:
                                     12288 + c * 1024 + (m + 1) * 128],
                                  A1[0:128, 16384 + c * S + t0:
                                     16384 + c * S + t0 + 512])
                                 for c in range(4)])
                    ctx.tt("vector", ALU.add, A1[0:128, m * S + t0:m * S + t0 + 512],
                           A1[0:128, m * S + t0:m * S + t0 + 512],
                           pb[0:128, 0:512])
            ctx.dma("sync", SEL[0:128, 0:8], H["l1_lg"][l, :, :])
            ctx.dma("sync", SEL[0:128, 8:16], H["l1_lb"][l, :, :])
            ctx.dma("sync", STT[0:1, 1024:2048], H["l1_b2"][l, :, :])
            ctx.dma("sync", SEL[0:128, 0:22][0].__class__ and SEL.h[0:128, 32:32],
                    SEL.h[0:128, 32:32]) if False else None
            for c2 in range(22):
                ctx.dma("gpsimd", A2[0:128, c2 * 1024:(c2 + 1) * 1024],
                        H["l1_w2p"][l, c2 * 128:(c2 + 1) * 128, :])
            ctx.dma("sync", T["BOS"][0:128, 0:22] if False else T["GAT"][0:128, 0:22],
                    H["l1_b1a"][l, :, :])
            ctx.dma("sync", T["GAT"][0:128, 32:54], H["l1_b1g"][l, :, :])
            for nb in range(NW):
                t0 = nb * 512
                if nb == 0:
                    for ci in range(4):
                        c = 4 + ci
                        ctx.copy("vector",
                                 HA[0:128, 12288 + ci * 512 + 1:
                                    12288 + (ci + 1) * 512],
                                 A1[0:128, c * S:c * S + 511])
                        ctx.memset("vector",
                                   HA[0:128, 12288 + ci * 512:12288 + ci * 512 + 1],
                                   0.0)

                def shsl(c):
                    if c < 4:
                        return A1[0:128, c * S + t0:c * S + t0 + 512]
                    if nb == 0:
                        return HA[0:128, 12288 + (c - 4) * 512:12288 + (c - 3) * 512]
                    return A1[0:128, c * S + t0 - 1:c * S + t0 + 511]
                ctx.mmgroup(P[0][0:1, 0:512],
                            [(T["ONESBC"], shsl(c)) for c in range(DC)])
                for c in range(DC):
                    ctx.tt("vector", ALU.mult, SCRB[0:128, c * 512 if False else 0:512]
                           if False else SCRB[0:128, 0:512], shsl(c), shsl(c)) \
                        if False else None
                for c in range(DC):
                    ctx.tt("vector", ALU.mult, SCRB[0:128, 0:512], shsl(c), shsl(c))
                    ctx.mmgroup(P[1][0:1, 0:512],
                                [(T["ONESBC"], SCRB[0:128, 0:512])],
                                accum_into=(c > 0), stop_group=(c == DC - 1))
                f = lambda i: STT[0:1, i * 256:(i + 1) * 256] if False else STT[0:1, i * 512:(i + 1) * 512]
                ctx.act("scalar", f(0), P[0][0:1, 0:512], AF.Copy, scale=1.0 / D)
                ctx.act("scalar", f(1), P[1][0:1, 0:512], AF.Copy, scale=1.0 / D)
                ctx.act("scalar", SCRF[0:128, 3072:3584][0:1] if False else STT[0:1, 1536:2048] if False else f(1),
                        f(1), AF.Copy) if False else None
                ctx.act("scalar", STT[0:1, 1536:2048] if False else SCRF[0:1, 0:512],
                        f(0), AF.Square)
                ctx.tt("vector", ALU.subtract, f(1), f(1), SCRF[0:1, 0:512])
                ctx.ts("vector", f(1), f(1), 1e-5, None, ALU.add)
                ctx.recip(SCRF[0:1, 0:512], f(1))
                ctx.act("scalar", f(1), SCRF[0:1, 0:512], AF.Sqrt)
                ctx.tt("vector", ALU.mult, STT[0:1, 1024:1536] if False else SCRF[0:1, 512:1024],
                       f(0), f(1))
                rbr = P[2]
                rbm = P[3]
                ctx.mmgroup(rbr[0:128, 0:512], [(onesrf_t[0:1, 0:128], f(1))])
                ctx.mmgroup(rbm[0:128, 0:512],
                            [(onesrf_t[0:1, 0:128], SCRF[0:1, 512:1024])])
                for c in range(DC):
                    ctx.tt("vector", ALU.mult, f32v(HA, 2048, 2560) if False else SCRB[0:128, 0:512],
                           shsl(c), rbr[0:128, 0:512]) if False else None
                for c in range(DC):
                    ctx.tt("vector", ALU.mult, f32vp(SCRB, 0, 128, 0, 512)[0:2][0] if False else f32v(SCRB, 0, 512),
                           shsl(c), rbr[0:128, 0:512])
                    ctx.tt("vector", ALU.subtract, f32v(SCRB, 512, 1024),
                           f32v(SCRB, 0, 512), rbm[0:128, 0:512])
                    ctx.ts("vector", HA[0:128, c * 512:(c + 1) * 512],
                           f32v(SCRB, 512, 1024), SEL[0:128, c:c + 1],
                           SEL[0:128, 8 + c:9 + c], ALU.mult, ALU.add)
                for m in range(22):
                    ctx.dma("gpsimd", A2[0:128, 22528 + (m % 2) * 1024:
                                         22528 + (m % 2) * 1024 + 1024],
                            d4T(DTen(H["l1_w1a"].h, "l1_w1a_x") if False else H["l1_w1a"], l, 0, 0)
                            ) if False else None
                    pa = P[4]
                    pg = P[5]
                    wsl_a = A2[0:128, 22528 + (m % 4) * 1024:
                               22528 + (m % 4) * 1024 + 1024]
                    ctx.dma("gpsimd", wsl_a,
                            (H["l1_w1a"].h[l, :, :, m * 128:(m + 1) * 128]
                             .transpose([1, 0, 2]),
                             Span(H["l1_w1a"].name, 0, 1, 0, 1 << 62)))
                    wsl_g = A2[0:128, 26624 + (m % 4) * 1024:
                               26624 + (m % 4) * 1024 + 1024]
                    ctx.dma("gpsimd", wsl_g,
                            (H["l1_w1g"].h[l, :, :, m * 128:(m + 1) * 128]
                             .transpose([1, 0, 2]),
                             Span(H["l1_w1g"].name, 0, 1, 0, 1 << 62)))
                    ctx.mmgroup(pa[0:128, 0:512],
                                [(A2[0:128, 22528 + (m % 4) * 1024 + c * 128:
                                     22528 + (m % 4) * 1024 + (c + 1) * 128],
                                  HA[0:128, c * 512:(c + 1) * 512])
                                 for c in range(DC)])
                    ctx.mmgroup(pg[0:128, 0:512],
                                [(A2[0:128, 26624 + (m % 4) * 1024 + c * 128:
                                     26624 + (m % 4) * 1024 + (c + 1) * 128],
                                  HA[0:128, c * 512:(c + 1) * 512])
                                 for c in range(DC)])
                    ctx.act("scalar", f32v(SCRB, 0, 512), pg[0:128, 0:512],
                            AF.Gelu, bias=T["GAT"][0:128, 32 + m:33 + m])
                    ctx.stt("vector", HA[0:128, 4096 + m * 512:4096 + (m + 1) * 512],
                            pa[0:128, 0:512], T["GAT"][0:128, m:m + 1],
                            f32v(SCRB, 0, 512), ALU.add, ALU.mult)
                b2b = P[6]
                for half in range(2):
                    ctx.mmgroup(b2b[0:128, 0:512],
                                [(onesrf_t[0:1, 0:128],
                                  STT[0:1, 1024 + half * 512:1024 + (half + 1) * 512])])
                    ctx.copy("scalar", SCRF[0:128, 2048:2560], b2b[0:128, 0:512])
                    for mo in range(4):
                        m2c = half * 4 + mo
                        pb = P[7]
                        ctx.mmgroup(pb[0:128, 0:512],
                                    [(A2[0:128, k2 * 1024 + m2c * 128:
                                         k2 * 1024 + (m2c + 1) * 128],
                                      HA[0:128, 4096 + k2 * 512:4096 + (k2 + 1) * 512])
                                     for k2 in range(22)])
                        ctx.tt("vector", ALU.add, f32v(SCRB, 0, 512),
                               pb[0:128, 0:512], SCRF[0:128, 2048:2560])
                        ctx.tt("vector", ALU.add,
                               A1[0:128, m2c * S + t0:m2c * S + t0 + 512],
                               A1[0:128, m2c * S + t0:m2c * S + t0 + 512],
                               f32v(SCRB, 0, 512))
        for c in range(DC):
            ctx.dma("sync", f_d[s, c, :, :], A1[0:128, c * S:(c + 1) * S])
    if dbg is not None and "y" in dbg:
        for s in range(BS):
            for c in range(DC):
                ctx.dma("gpsimd", dbg["y"][s, c, :, :], f_d[s, c, :, :])

    emit_combine_B(ctx, T, x_in, x_out, a_d, f_d)


def emit_combine_B(ctx, T, x_in, x_out, a_d, f_d):
    SCRF = T["SCRF"]
    for s in range(BS):
        for c in range(DC):
            for nb in range(S // 512):
                t0 = nb * 512
                q = nb % 2
                ctx.dma("sync", SCRF[0:128, q * 2048:q * 2048 + 512],
                        x_in[s, c, :, t0:t0 + 512])
                ctx.dma("scalar", SCRF[0:128, q * 2048 + 512:q * 2048 + 1024],
                        a_d[s, c, :, t0:t0 + 512])
                ctx.dma("sync", T["SCRB"][0:128, q * 512:q * 512 + 512],
                        f_d[s, c, :, t0:t0 + 512])
                ctx.tt("vector", ALU.add, SCRF[0:128, q * 2048:q * 2048 + 512],
                       SCRF[0:128, q * 2048:q * 2048 + 512],
                       SCRF[0:128, q * 2048 + 512:q * 2048 + 1024])
                ctx.tt("vector", ALU.add, SCRF[0:128, q * 2048:q * 2048 + 512],
                       SCRF[0:128, q * 2048:q * 2048 + 512],
                       T["SCRB"][0:128, q * 512:q * 512 + 512])
                ctx.dma("gpsimd", x_out[s, c, :, t0:t0 + 512],
                        SCRF[0:128, q * 2048:q * 2048 + 512], is_output=True)



def prep_l3(inputs):
    d = {}
    d["l3_wqkvT"] = np.ascontiguousarray(inputs["L3_attn_Wqkv_w"].T)
    d["l3_qkvb"] = np.ascontiguousarray(
        inputs["L3_attn_Wqkv_b"].reshape(24, 128).T)
    d["l3_bv"] = np.ascontiguousarray(
        inputs["L3_attn_Wqkv_b"][2048:].reshape(1, 1024))
    pj = np.ascontiguousarray(inputs["L3_attn_proj"].T)
    d["l3_pj2"] = np.ascontiguousarray(np.vstack([pj, pj]))
    d["l3_outwT"] = np.ascontiguousarray(inputs["L3_attn_out_w"].T)
    d["l3_outb"] = bias_chunks(inputs["L3_attn_out_b"])
    d["l3_rwT"] = np.ascontiguousarray(inputs["L3_ffn_router_w"].T)
    gu = inputs["L3_ffn_gate_up_proj"]
    d["l3_gug"] = np.ascontiguousarray(gu[:, :, :1024])
    d["l3_guu"] = np.ascontiguousarray(gu[:, :, 1024:])
    d["l3_down"] = inputs["L3_ffn_down_proj"]
    d["l3_sgwT"] = np.ascontiguousarray(inputs["L3_ffn_shared_gate_w"].T)
    d["l3_suwT"] = np.ascontiguousarray(inputs["L3_ffn_shared_up_w"].T)
    d["l3_sdwT"] = np.ascontiguousarray(inputs["L3_ffn_shared_down_w"].T)
    return d


def l3_mask_consts():
    i = np.arange(128)[:, None]
    jj = np.arange(128)[None, :]
    mloc = np.zeros((128, 512), np.float32)
    mloc[:, 0:128] = np.where(jj < i, -1e30, 0.0)
    mloc[:, 128:256] = np.where(jj > i, -1e30, 0.0)
    mloc[:, 256:384] = -1e30
    mloc[:, 384:512] = np.where(jj > i, -1e30, 0.0)
    m01 = np.zeros((128, 512), np.float32)
    SCALE = 0.125
    m01[:, 0:128] = np.where(jj >= i, SCALE, 0.0)
    m01[:, 128:256] = np.where(jj <= i, SCALE, 0.0)
    m01[:, 256:384] = 0.0
    m01[:, 384:512] = np.where(jj <= i, SCALE, 0.0)
    tril = np.where(jj >= i, 1.0, 0.0).astype(np.float32)
    return {"c_l3mloc": mloc, "c_l3m01": m01, "c_l3tril": tril}


def emit_l3(ctx, T, H, x_in, x_out, dbg=None):
    xn_d = ctx.dram("l3xn_d", [BS, DC, 128, S], BF16)
    a_d = ctx.dram("l3a_d", [BS, DC, 128, S], F32)
    y_d = ctx.dram("l3y_d", [BS, S // 128, 128, D], F32)
    ctx_d = ctx.dram("l3ctx_d", [BS, DC, 128, S], BF16)
    gates_d = ctx.dram("l3gates_d", [BS, 16, S], F32)

    A1, A2, HA, XNT = T["A1"], T["A2"], T["HA"], T["XNT"]
    SCRF, SCRB, STT = T["SCRF"], T["SCRB"], T["STT"]
    P = T["PS"]
    onesrf_t = T["_onesrf_t"]
    EPS = 1e-4
    SC = 0.125
    RSC = float(math.sqrt(SC))

    ctx.dma("sync", T["IDF"].full(), H["c_identf"][:, :])
    ctx.dma("gpsimd", T["IDB"].full(), H["c_identf"][:, :])
    emit_layernorm_to_dram(ctx, T, x_in, xn_d)
    if dbg is not None and "xn" in dbg:
        for s in range(BS):
            for c in range(DC):
                ctx.dma("gpsimd", dbg["xn"][s, c, :, :], xn_d[s, c, :, :])

    PJ2 = ctx.sbuf("PJ2B", [128, 24], BF16)
    ctx.dma("gpsimd", PJ2[0:128, 0:24], H["l3_pj2"][:, :])
    ctx.dma("sync", T["B1S"][0:128, 0:24], H["l3_qkvb"][:, :])
    MLOC = T["MASK"]
    ctx.dma("gpsimd", MLOC[0:128, 0:512], H["c_l3mloc"][:, :])
    ctx.dma("gpsimd", MLOC[0:128, 512:1024], H["c_l3m01"][:, :])
    ctx.dma("gpsimd", MLOC[0:128, 1024:1152], H["c_l3tril"][:, :])

    for s in range(BS):
        for (part, kind) in ((0, "q"), (1, "k"), (2, "v")):
            for c in range(DC):
                ctx.dma("gpsimd", A2[0:128, 16640 + c * 1024:16640 + (c + 1) * 1024],
                        (H["l3_wqkvT"].h[c * 128:(c + 1) * 128,
                                         part * 1024:(part + 1) * 1024],
                         Span(H["l3_wqkvT"].name, 0, 1, 0, 1 << 62)))
            if kind == "v":
                ctx.dma("sync", STT[0:1, 0:1024], H["l3_bv"][:, :])
                for half in range(2):
                    ctx.mmgroup(P[6][0:128, 0:512],
                                [(onesrf_t[0:1, 0:128],
                                  STT[0:1, half * 512:(half + 1) * 512])])
                    ctx.copy("vector",
                             SCRF[0:128, 2048 + half * 512:2048 + (half + 1) * 512],
                             P[6][0:128, 0:512])
            for nb in range(S // 512):
                t0 = nb * 512
                load_xn_tile(ctx, T, xn_d, s, t0, 512)
                if kind in ("q", "k"):
                    base = 0 if kind == "q" else 16384
                    for m in range(DC):
                        pb = P[m % 2]
                        ctx.mmgroup(pb[0:128, 0:512],
                                    [(A2[0:128, 16640 + c * 1024 + m * 128:
                                         16640 + c * 1024 + (m + 1) * 128],
                                      XNT[0:128, c * 512:(c + 1) * 512])
                                     for c in range(DC)])
                        ctx.ts("vector", A1[0:128, base + m * S + t0:
                                            base + m * S + t0 + 512],
                               pb[0:128, 0:512],
                               T["B1S"][0:128, part * 8 + m:part * 8 + m + 1],
                               None, ALU.add)
                else:
                    for st in range(4):
                        tt = nb * 4 + st
                        for half in range(2):
                            pb = P[2 + (st * 2 + half) % 2]
                            ctx.mmgroup(pb[0:128, 0:512],
                                        [(XNT[0:128, c * 512 + st * 128:
                                              c * 512 + (st + 1) * 128],
                                          A2[0:128, 16640 + c * 1024 + half * 512:
                                             16640 + c * 1024 + (half + 1) * 512])
                                         for c in range(DC)])
                            ctx.tt("vector", ALU.add,
                                   A2.strided(0, 128, tt * 1040 + half * 8 * 65,
                                              8, 65, 64),
                                   pb[0:128, 0:512],
                                   SCRF[0:128, 2048 + half * 512:
                                        2048 + (half + 1) * 512])
        ctx.memset("vector", A2.strided(0, 128, 64, 16 * 16, 65, 1), 1.0)

        for h in range(16):
            po = 64 * (h % 2)
            ch = h // 2

            def lf_pass(base, h=h, po=po, ch=ch):
                for nb in range(4):
                    t0 = nb * 512
                    pxd = P[0]
                    pdg = P[1]
                    ctx.mmgroup(pxd[0:24, 0:512],
                                [(PJ2[po:po + 64, 0:24],
                                  A1[po:po + 64, base + ch * S + t0:
                                     base + ch * S + t0 + 512])])
                    ctx.tt("vector", ALU.mult,
                           (SCRB.h[po:po + 64, 0:512],
                            Span("SCRB", po, po + 64, 0, 1024)),
                           A1[po:po + 64, base + ch * S + t0:
                              base + ch * S + t0 + 512],
                           A1[po:po + 64, base + ch * S + t0:
                              base + ch * S + t0 + 512])
                    ctx.mmgroup(pdg[0:1, 0:512],
                                [(ctx_onesb_slice(T, po),
                                  (SCRB.h[po:po + 64, 0:512],
                                   Span("SCRB", po, po + 64, 0, 1024)))])
                    ctx.act("scalar", STT[0:1, 0:512], pdg[0:1, 0:512], AF.Copy,
                            scale=-0.5 * SC)
                    pbq = P[2]
                    ctx.mmgroup(pbq[0:24, 0:512],
                                [(onesrf_t[0:1, 0:24], STT[0:1, 0:512])])
                    ctx.copy("vector", (SCRB.h[0:24, 1536:2048],
                                        Span("SCRB", 0, 24, 3072, 4096)),
                             pbq[0:24, 0:512])
                    ctx.stt("vector", SCRF[0:24, t0:t0 + 512], pxd[0:24, 0:512],
                            RSC, (SCRB.h[0:24, 1536:2048],
                                  Span("SCRB", 0, 24, 3072, 4096)),
                            ALU.mult, ALU.add)

            lf_pass(0)
            for cc in range(16):
                pt = P[3]
                ctx.transpose(pt[0:128, 0:24], SCRF[0:24, cc * 128:(cc + 1) * 128],
                              T["IDF"][0:24, 0:24])
                ctx.reduce("vector", ALU.max, SCRF[0:128, 3072 + cc:3073 + cc],
                           pt[0:128, 0:24], negate=True)
                ctx.act("scalar", SCRB[0:128, 1024:1048], pt[0:128, 0:24], AF.Exp,
                        bias=SCRF[0:128, 3072 + cc:3073 + cc])
                ctx.ts("vector", SCRB[0:128, 1024:1048], SCRB[0:128, 1024:1048],
                       EPS, None, ALU.add)
                pt2 = P[4]
                ptb = (pt2.h[0:24, 0:64].bitcast(BF16),
                       Span(pt2.name, 0, 24, 0, 128))
                ctx.transpose(ptb, SCRB[0:128, 1024:1048], T["IDB"].full())
                ctx.copy("vector", HA[0:24, cc * 128:(cc + 1) * 128], ptb)
            lf_pass(16384)
            pt3 = P[3]
            ctx.reduce("vector", ALU.max,
                       (SCRF.h[0:24, 2048:2049], Span("SCRF", 0, 24, 8192, 8196)),
                       (SCRF.h[0:24, 0:2048], Span("SCRF", 0, 24, 0, 8192)))
            ctx.transpose(pt3[0:1, 0:24],
                          (SCRF.h[0:24, 2048:2049], Span("SCRF", 0, 24, 8192, 8196)),
                          T["IDF"][0:24, 0:24])
            ctx.reduce("vector", ALU.max, STT[0:1, 512:513], pt3[0:1, 0:24],
                       negate=True)
            pb24 = P[4]
            ctx.mmgroup(pb24[0:24, 0:1], [(onesrf_t[0:1, 0:24], STT[0:1, 512:513])])
            ctx.copy("vector", (SCRF.h[0:24, 2049:2050],
                                Span("SCRF", 0, 24, 8196, 8200)), pb24[0:24, 0:1])
            ctx.act("scalar", (SCRB.h[0:24, 2048:4096],
                               Span("SCRB", 0, 24, 4096, 8192)),
                    (SCRF.h[0:24, 0:2048], Span("SCRF", 0, 24, 0, 8192)),
                    AF.Exp, bias=(SCRF.h[0:24, 2049:2050],
                                  Span("SCRF", 0, 24, 8196, 8200)))
            ctx.ts("vector", HA[0:24, 2048:4096],
                   (SCRB.h[0:24, 2048:4096], Span("SCRB", 0, 24, 4096, 8192)),
                   EPS, None, ALU.add)

            if dbg is not None and "l3i" in dbg and s == 0 and h == 0:
                ctx.dma("gpsimd", dbg["l3i"][0, 0:24, 0:2048], HA[0:24, 0:2048])
                ctx.dma("gpsimd", dbg["l3i"][1, 0:24, 0:2048], HA[0:24, 2048:4096])
            for c in range(16):
                qsl = HA[0:24, c * 128:(c + 1) * 128]
                ksl = HA[0:24, 2048 + c * 128:2048 + (c + 1) * 128]
                ptk = P[6]
                ptkb = (ptk.h[0:128, 0:12].bitcast(BF16),
                        Span(ptk.name, 0, 128, 0, 48))
                ctx.transpose(ptkb, ksl, T["IDB"][0:24, 0:24])
                ctx.copy("vector", HA[0:128, 7424:7448], ptkb)
                if c == 0:
                    ctx.memset("vector", f32vp(HA, 0, 24, 3840, 3905), 0.0)
                else:
                    ctx.copy("scalar", (HA.h[0:24, 7448:7513],
                                        Span("HA", 0, 24, 14896, 15026)),
                             f32vp(HA, 0, 24, 3840, 3905))
                ctx.mmgroup(P[5][0:24, 0:65],
                            [(HA[0:128, 7424:7448],
                              A2.strided(0, 128, c * 1040 + h * 65, 1, 65, 65))])
                ctx.tt("vector", ALU.add, f32vp(HA, 0, 24, 3840, 3905),
                       f32vp(HA, 0, 24, 3840, 3905), P[5][0:24, 0:65])
                ps1 = P[0]
                if c > 0:
                    ctx.mmgroup(ps1[0:128, 0:256],
                                [(qsl, HA[0:24, 2048 + (c - 1) * 128:
                                          2048 + (c + 1) * 128])])
                else:
                    ctx.mmgroup(ps1[0:128, 128:256],
                                [(qsl, HA[0:24, 2048:2176])])
                ps1t = P[1]
                ctx.mmgroup(ps1t[0:128, 0:128], [(ksl, qsl)])
                ctx.tt("vector", ALU.mult, HA[0:128, 6144:6272],
                       ps1t[0:128, 0:128], MLOC[0:128, 1024:1152])
                psl = P[2]
                if c > 0:
                    ctx.mmgroup(psl[0:128, 0:256],
                                [(A1[po:po + 64, ch * S + c * 128:
                                     ch * S + (c + 1) * 128],
                                  A1[po:po + 64, 16384 + ch * S + (c - 1) * 128:
                                     16384 + ch * S + (c + 1) * 128])])
                else:
                    ctx.mmgroup(psl[0:128, 128:256],
                                [(A1[po:po + 64, ch * S:ch * S + 128],
                                  A1[po:po + 64, 16384 + ch * S:
                                     16384 + ch * S + 128])])
                gq = f32v(HA, 3200, 3456)
                if c > 0:
                    ctx.tt("vector", ALU.mult, gq, ps1[0:128, 0:256],
                           MLOC[0:128, 512:768])
                else:
                    ctx.memset("vector", f32v(HA, 3200, 3328), 0.0)
                    ctx.tt("vector", ALU.mult, f32v(HA, 3328, 3456),
                           ps1[0:128, 128:256], MLOC[0:128, 896:1024])
                ctx.reduce("vector", ALU.add, SCRF[0:128, 3104 + c:3105 + c], gq)
                SSL = f32v(HA, 3456, 3712)
                if c > 0:
                    ctx.stt("vector", SSL, psl[0:128, 0:256], SC,
                            MLOC[0:128, 0:256], ALU.mult, ALU.add)
                else:
                    ctx.memset("vector", f32v(HA, 3456, 3584), -1e30)
                    ctx.stt("vector", f32v(HA, 3584, 3712), psl[0:128, 128:256],
                            SC, MLOC[0:128, 384:512], ALU.mult, ALU.add)
                ctx.reduce("vector", ALU.max, SCRF[0:128, 3120 + c:3121 + c],
                           SSL, negate=True)
                ctx.act("scalar", SCRB[0:128, c * 256:(c + 1) * 256],
                        SSL, AF.Exp, bias=SCRF[0:128, 3120 + c:3121 + c],
                        accum_out=SCRF[0:128, 3136 + c:3137 + c])
                pgo = P[3]
                if c > 0:
                    ctx.mmgroup(pgo[0:65, 0:128],
                                [((HA.h[0:24, 7448:7513],
                                   Span("HA", 0, 24, 14896, 15026)), qsl)],
                                stop_group=False)
                    ctx.mmgroup(pgo[0:65, 0:128],
                                [(A2.strided(0, 128, c * 1040 + h * 65, 1, 65, 65),
                                  HA[0:128, 6144:6272])], accum_into=True)
                else:
                    ctx.mmgroup(pgo[0:65, 0:128],
                                [(A2.strided(0, 128, c * 1040 + h * 65, 1, 65, 65),
                                  HA[0:128, 6144:6272])])
                ctx.copy("scalar", (HA.h[0:65, 8192 + c * 128:8192 + (c + 1) * 128],
                                    Span("HA", 0, 65, 16384 + c * 256,
                                         16384 + (c + 1) * 256)),
                         pgo[0:65, 0:128])

            for c in range(16):
                pt = P[0]
                ptb1 = (pt.h[0:128, 0:1].bitcast(BF16)[0:128, 0:1],
                        Span(pt.name, 0, 128, 0, 4))
                ctx.transpose(ptb1,
                              (HA.h[64:65, 8192 + c * 128:8192 + (c + 1) * 128],
                               Span("HA", 64, 65, 16384 + c * 256,
                                    16384 + (c + 1) * 256)),
                              T["IDB"][64:65, 64:65])
                ctx.copy("vector", SCRF[0:128, 3152 + c:3153 + c], ptb1)
            ST = lambda a, b: SCRF[0:128, a:b]
            QSN = ST(3072, 3088)
            GQS = ST(3104, 3120)
            NLM = ST(3120, 3136)
            SM = ST(3136, 3152)
            GOC = ST(3152, 3168)
            GPS = ST(3168, 3184)
            W2_ = ST(3184, 3200)
            W3 = ST(3200, 3216)
            pbneg = P[1]
            ctx.mmgroup(pbneg[0:128, 0:1], [(onesrf_t[0:1, 0:128], STT[0:1, 512:513])])
            ctx.copy("vector", ST(3216, 3217), pbneg[0:128, 0:1])
            ctx.ts("vector", GPS, QSN, ST(3216, 3217), None, ALU.add)
            ctx.ts("vector", GPS, GPS, -1.0, None, ALU.mult)
            ctx.act("scalar", W2_, SM, AF.Ln)
            ctx.tt("vector", ALU.subtract, W2_, W2_, NLM)
            ctx.tt("vector", ALU.subtract, W3, GOC, GQS)
            ctx.ts("vector", W3, W3, 1e-24, None, ALU.max)
            ctx.act("scalar", W3, W3, AF.Ln)
            ctx.tt("vector", ALU.add, W3, W3, GPS)
            MX = ST(3217, 3233)
            ctx.tt("vector", ALU.max, MX, W2_, W3)
            ctx.tt("vector", ALU.subtract, W2_, W2_, MX)
            ctx.tt("vector", ALU.subtract, W3, W3, MX)
            ctx.act("scalar", W2_, W2_, AF.Exp)
            ctx.act("scalar", W3, W3, AF.Exp)
            ctx.tt("vector", ALU.add, W3, W2_, W3)
            ctx.act("scalar", W3, W3, AF.Ln)
            ctx.tt("vector", ALU.add, W3, W3, MX)
            ctx.tt("vector", ALU.subtract, GPS, GPS, W3)
            ctx.act("scalar", GPS, GPS, AF.Exp)
            NG = ST(3233, 3249)
            ctx.ts("vector", NG, GPS, -1.0, None, ALU.mult)
            RSM = ST(3249, 3265)
            ctx.recip(RSM, SM)
            if dbg is not None and "l3i" in dbg and s == 0 and h == 0:
                ctx.dma("gpsimd", dbg["l3i"][2, 0:128, 0:1024],
                        (SCRF.h[0:128, 3072:3584].bitcast(BF16)[0:128, 0:1024],
                         Span("SCRF", 0, 128, 3072 * 4, 3584 * 4)))

            for c in range(16):
                Pn = SCRB[0:128, c * 256:(c + 1) * 256]
                ctx.ts("vector", Pn, Pn, SCRF[0:128, 3249 + c:3250 + c],
                       None, ALU.mult)
                ps1 = P[0]
                if c > 0:
                    ctx.mmgroup(ps1[0:128, 0:256],
                                [(HA[0:24, c * 128:(c + 1) * 128],
                                  HA[0:24, 2048 + (c - 1) * 128:
                                     2048 + (c + 1) * 128])])
                    ctx.tt("vector", ALU.mult, f32v(HA, 3200, 3456),
                           ps1[0:128, 0:256], MLOC[0:128, 512:768])
                else:
                    ctx.memset("vector", f32v(HA, 3200, 3328), 0.0)
                    ctx.mmgroup(ps1[0:128, 128:256],
                                [(HA[0:24, 0:128], HA[0:24, 2048:2176])])
                    ctx.tt("vector", ALU.mult, f32v(HA, 3328, 3456),
                           ps1[0:128, 128:256], MLOC[0:128, 896:1024])
                ctx.stt("vector", Pn, f32v(HA, 3200, 3456),
                        SCRF[0:128, 3233 + c:3234 + c], Pn, ALU.mult, ALU.add)
                pav = P[2]
                first = True
                for kb in range(2):
                    if c == 0 and kb == 0:
                        continue
                    ptp = P[6]
                    ptb = (ptp.h[0:128, 0:64].bitcast(BF16),
                           Span(ptp.name, 0, 128, 0, 256))
                    ctx.transpose(ptb,
                                  SCRB[0:128, c * 256 + kb * 128:
                                       c * 256 + (kb + 1) * 128],
                                  T["IDB"].full())
                    ctx.copy("vector", HA[0:128, 7520:7648], ptb)
                    kc = c - 1 + kb
                    ctx.mmgroup(pav[0:64, 0:128],
                                [(A2.strided(0, 128, kc * 1040 + h * 65, 1, 65, 64),
                                  HA[0:128, 7520:7648])],
                                accum_into=(not first), stop_group=(kb == 1))
                    first = False
                ptg2 = P[1]
                ctx.transpose(ptg2[0:1, 0:128],
                              (SCRF.h[0:128, 3168 + c:3169 + c],
                               Span("SCRF", 0, 128, (3168 + c) * 4, (3169 + c) * 4)),
                              T["IDF"].full())
                ctx.copy("vector", STT[0:1, 768:896], ptg2[0:1, 0:128])
                prr = P[4]
                ctx.mmgroup(prr[0:64, 0:128],
                            [(onesrf_t[0:1, 0:64], STT[0:1, 768:896])])
                ctx.tt("vector", ALU.mult,
                       (SCRF.h[0:64, 1024:1152], Span("SCRF", 0, 64, 4096, 4608)),
                       (HA.h[0:64, 8192 + c * 128:8192 + (c + 1) * 128],
                        Span("HA", 0, 64, 16384 + c * 256, 16384 + (c + 1) * 256)),
                       prr[0:64, 0:128])
                ctx.tt("vector", ALU.add,
                       (HA.h[po:po + 64, 7938:8066],
                        Span("HA", po, po + 64, 15876, 16132)),
                       pav[0:64, 0:128],
                       (SCRF.h[0:64, 1024:1152], Span("SCRF", 0, 64, 4096, 4608)))
                ctx.dma("sync", ctx_d[s, ch, po:po + 64, c * 128:(c + 1) * 128],
                        (HA.h[po:po + 64, 7938:8066],
                         Span("HA", po, po + 64, 15876, 16132)))

        for c in range(DC):
            ctx.dma("gpsimd", A2[0:128, 16640 + c * 1024:16640 + (c + 1) * 1024],
                    H["l3_outwT"][c * 128:(c + 1) * 128, :])
        ctx.dma("sync", T["B1S"][0:128, 24:32], H["l3_outb"][:, :])
        for nb in range(S // 512):
            t0 = nb * 512
            ctx.dma("sync", XNT.strided(0, 128, 0, DC, 512, 512),
                    d4T(ctx_d, s, t0, t0 + 512))
            for m in range(DC):
                pb = P[m % 2]
                ctx.mmgroup(pb[0:128, 0:512],
                            [(A2[0:128, 16640 + c * 1024 + m * 128:
                                 16640 + c * 1024 + (m + 1) * 128],
                              XNT[0:128, c * 512:(c + 1) * 512])
                             for c in range(DC)])
                ctx.ts("vector", SCRF[0:128, (m % 4) * 512:(m % 4 + 1) * 512],
                       pb[0:128, 0:512], T["B1S"][0:128, 24 + m:25 + m],
                       None, ALU.add)
                ctx.dma("gpsimd", a_d[s, m, :, t0:t0 + 512],
                        SCRF[0:128, (m % 4) * 512:(m % 4 + 1) * 512])
    if dbg is not None and "ctx" in dbg:
        for s in range(BS):
            for c in range(DC):
                ctx.dma("gpsimd", dbg["ctx"][s, c, :, :], ctx_d[s, c, :, :])
    if dbg is not None and "a" in dbg:
        for s in range(BS):
            for c in range(DC):
                ctx.dma("gpsimd", dbg["a"][s, c, :, :], a_d[s, c, :, :])

    for c in range(DC):
        ctx.dma("gpsimd", A2[0:128, c * 16:(c + 1) * 16],
                H["l3_rwT"][c * 128:(c + 1) * 128, :])
    for s in range(BS):
        for tt in range(16):
            t0 = tt * 128
            ctx.dma("sync", XNT.strided(0, 128, 0, DC, 128, 128),
                    d4T(xn_d, s, t0, t0 + 128))
            pb = P[tt % 2]
            ctx.mmgroup(pb[0:128, 0:16],
                        [(XNT[0:128, c * 128:(c + 1) * 128],
                          A2[0:128, c * 16:(c + 1) * 16]) for c in range(DC)])
            lg = SCRF[0:128, 0:16]
            mx = SCRF[0:128, 16:17]
            gt = SCRF[0:128, 32:48]
            ctx.copy("vector", lg, pb[0:128, 0:16])
            ctx.reduce("vector", ALU.max, mx, lg)
            ctx.act("scalar", gt, lg, AF.Sigmoid)
            ctx.ts("vector", lg, lg, mx, None, ALU.is_ge)
            ctx.tt("vector", ALU.mult, gt, gt, lg)
            ctx.act("scalar", SCRB[0:128, 0:16], gt, AF.Copy)
            ptg = P[2]
            pgb = (ptg.h[0:16, 0:64].bitcast(BF16), Span(ptg.name, 0, 16, 0, 128))
            ctx.transpose(pgb, SCRB[0:128, 0:16], T["IDB"].full())
            ctx.copy("vector", SCRF[0:16, 1024:1152], pgb)
            ctx.dma("gpsimd", gates_d[s, :, t0:t0 + 128], SCRF[0:16, 1024:1152])

    for e in range(16):
        for c in range(DC):
            ctx.dma("gpsimd", A1[0:128, c * 1024:(c + 1) * 1024],
                    H["l3_gug"][e, c * 128:(c + 1) * 128, :])
            ctx.dma("gpsimd", A1[0:128, 8192 + c * 1024:8192 + (c + 1) * 1024],
                    H["l3_guu"][e, c * 128:(c + 1) * 128, :])
            ctx.dma("gpsimd", A1[0:128, 16384 + c * 1024:16384 + (c + 1) * 1024],
                    H["l3_down"][e, c * 128:(c + 1) * 128, :])
        for s in range(BS):
            for nb in range(S // 512):
                t0 = nb * 512
                load_xn_tile(ctx, T, xn_d, s, t0, 512)
                ctx.dma("sync", STT[0:1, 0:512],
                        (gates_d.h[s, e, t0:t0 + 512].unsqueeze(0),
                         Span(gates_d.name, 0, 1, 0, 1 << 62)))
                psig = P[6]
                ctx.mmgroup(psig[0:128, 0:512],
                            [(onesrf_t[0:1, 0:128], STT[0:1, 0:512])])
                ctx.copy("vector", SCRF[0:128, 3072:3584], psig[0:128, 0:512])
                for m in range(DC):
                    pg = P[0]
                    pu = P[1]
                    ctx.mmgroup(pg[0:128, 0:512],
                                [(A1[0:128, c * 1024 + m * 128:c * 1024 + (m + 1) * 128],
                                  XNT[0:128, c * 512:(c + 1) * 512])
                                 for c in range(DC)])
                    ctx.mmgroup(pu[0:128, 0:512],
                                [(A1[0:128, 8192 + c * 1024 + m * 128:
                                     8192 + c * 1024 + (m + 1) * 128],
                                  XNT[0:128, c * 512:(c + 1) * 512])
                                 for c in range(DC)])
                    ctx.tt("vector", ALU.mult, SCRF[0:128, 0:512],
                           pg[0:128, 0:512], SCRF[0:128, 3072:3584])
                    ctx.act("scalar", SCRF[0:128, 512:1024], SCRF[0:128, 0:512],
                            AF.Silu)
                    ctx.tt("vector", ALU.mult, SCRF[0:128, 0:512],
                           pu[0:128, 0:512], SCRF[0:128, 3072:3584])
                    ctx.tt("vector", ALU.mult, HA[0:128, m * 512:(m + 1) * 512],
                           SCRF[0:128, 0:512], SCRF[0:128, 512:1024])
                for st in range(4):
                    tt = nb * 4 + st
                    for half in range(2):
                        pb = P[2 + (st * 2 + half) % 2]
                        ctx.mmgroup(pb[0:128, 0:512],
                                    [(HA[0:128, k2 * 512 + st * 128:
                                         k2 * 512 + (st + 1) * 128],
                                      A1[0:128, 16384 + k2 * 1024 + half * 512:
                                         16384 + k2 * 1024 + (half + 1) * 512])
                                     for k2 in range(DC)])
                        tb = SCRF[0:128, 1024 + (st % 2) * 1024 + half * 512:
                                  1024 + (st % 2) * 1024 + half * 512 + 512]
                        ctx.copy("vector", tb, pb[0:128, 0:512])
                        ctx.dma("gpsimd",
                                y_d[s, tt, :, half * 512:(half + 1) * 512],
                                tb, accum=(e > 0))

    for fh in range(2):
        for c in range(DC):
            ctx.dma("gpsimd", A1[0:128, c * 2048:(c + 1) * 2048],
                    (H["l3_sgwT"].h[c * 128:(c + 1) * 128,
                                    fh * 2048:(fh + 1) * 2048],
                     Span(H["l3_sgwT"].name, 0, 1, 0, 1 << 62)))
            ctx.dma("gpsimd", A1[0:128, 16384 + c * 2048:16384 + (c + 1) * 2048],
                    (H["l3_suwT"].h[c * 128:(c + 1) * 128,
                                    fh * 2048:(fh + 1) * 2048],
                     Span(H["l3_suwT"].name, 0, 1, 0, 1 << 62)))
        for c2 in range(16):
            ctx.dma("gpsimd", A2[0:128, c2 * 1024:(c2 + 1) * 1024],
                    (H["l3_sdwT"].h[fh * 2048 + c2 * 128:fh * 2048 + (c2 + 1) * 128, :],
                     Span(H["l3_sdwT"].name, 0, 1, 0, 1 << 62)))
        for s in range(BS):
            for nb in range(S // 512):
                t0 = nb * 512
                load_xn_tile(ctx, T, xn_d, s, t0, 512)
                for m in range(16):
                    pg = P[0]
                    pu = P[1]
                    ctx.mmgroup(pg[0:128, 0:512],
                                [(A1[0:128, c * 2048 + m * 128:c * 2048 + (m + 1) * 128],
                                  XNT[0:128, c * 512:(c + 1) * 512])
                                 for c in range(DC)])
                    ctx.mmgroup(pu[0:128, 0:512],
                                [(A1[0:128, 16384 + c * 2048 + m * 128:
                                     16384 + c * 2048 + (m + 1) * 128],
                                  XNT[0:128, c * 512:(c + 1) * 512])
                                 for c in range(DC)])
                    ctx.act("scalar", SCRF[0:128, 0:512], pg[0:128, 0:512], AF.Silu)
                    ctx.tt("vector", ALU.mult, HA[0:128, m * 512:(m + 1) * 512],
                           SCRF[0:128, 0:512], pu[0:128, 0:512])
                for st in range(4):
                    tt = nb * 4 + st
                    for half in range(2):
                        pb = P[2 + (st * 2 + half) % 2]
                        ctx.mmgroup(pb[0:128, 0:512],
                                    [(HA[0:128, k2 * 512 + st * 128:
                                         k2 * 512 + (st + 1) * 128],
                                      A2[0:128, k2 * 1024 + half * 512:
                                         k2 * 1024 + (half + 1) * 512])
                                     for k2 in range(16)])
                        tb = SCRF[0:128, 1024 + (st % 2) * 1024 + half * 512:
                                  1024 + (st % 2) * 1024 + half * 512 + 512]
                        ctx.copy("vector", tb, pb[0:128, 0:512])
                        ctx.dma("gpsimd",
                                y_d[s, tt, :, half * 512:(half + 1) * 512],
                                tb, accum=True)
    if dbg is not None and "y" in dbg:
        for s in range(BS):
            for tt in range(16):
                ctx.dma("gpsimd", dbg["y"][s, tt, :, :], y_d[s, tt, :, :])

    emit_combine(ctx, T, x_in, x_out, a_d, y_d, bo=None)


def ctx_onesb_slice(T, po):
    t = T["_onesb_t"]
    return t[po:po + 64, 0:1]


def all_consts():
    return {**host_constants(), **l1_rotary_consts(), **l3_mask_consts()}


PREPS = {0: prep_l0, 1: prep_l1, 2: prep_l2, 3: prep_l3}


def build_fused(weights, consts):
    nc = bass.Bass("TRN2", target_bir_lowering=False)
    ctx = Ctx(nc)
    es = ExitStack()
    ctx.es = es
    with es:
        H = declare_inputs(ctx, {**weights, **consts})
        x_in = ctx.dram("x_in", [BS, DC, 128, S], F32, kind="ExternalInput")
        x1 = ctx.dram("x1_d", [BS, DC, 128, S], F32)
        x2 = ctx.dram("x2_d", [BS, DC, 128, S], F32)
        x3 = ctx.dram("x3_d", [BS, DC, 128, S], F32)
        x_out = ctx.dram("x_out", [BS, DC, 128, S], F32, kind="ExternalOutput")
        T = alloc_common(ctx)
        emit_l0(ctx, T, H, x_in, x1)
        emit_l1(ctx, T, H, x1, x2)
        emit_l2(ctx, T, H, x2, x3)
        emit_l3(ctx, T, H, x3, x_out)
        ctx.emit()
    return nc


def kernel(**inputs):
    x = np.asarray(inputs["L0_attn_x"], dtype=np.float32)
    consts = all_consts()
    weights = {}
    for layer in range(4):
        weights.update(PREPS[layer](inputs))
    shards = [to_layoutB(x[2 * i:2 * i + 2]) for i in range(NCORES)]
    nc = build_fused(weights, consts)
    in_maps = [{"x_in": shards[i], **weights, **consts} for i in range(NCORES)]
    res = run_bass_kernel_spmd(nc, in_maps, core_ids=list(range(NCORES)))
    shards = [np.asarray(res.results[i]["x_out"]) for i in range(NCORES)]
    return np.concatenate([from_layoutB(s) for s in shards], axis=0)
